# version 05
# baseline (speedup 1.0000x reference)
import contextlib
import math
import numpy as np
import concourse.bass as bass
import concourse.mybir as mybir
from concourse.bass_utils import run_bass_kernel_spmd

F32 = mybir.dt.float32
BF16 = mybir.dt.bfloat16
AF = mybir.ActivationFunctionType
ALU = mybir.AluOpType
AX = mybir.AxisListType

T = 2048
C = 2048
NKV = 4
NH = 16
SCALE = 1.0 / math.sqrt(128.0)
RN = 1.0 / float(C)
EPS = 1e-5

ENGS = ("sync", "gpsimd", "tensor", "vector", "scalar")
_ENG_SEM = {"tensor": "t", "vector": "v", "scalar": "s", "gpsimd": "g"}


class Prog:

    def __init__(self, nc, ctx):
        self.nc = nc
        self.ctx = ctx
        self.stream = {e: [] for e in ENGS}
        self.sems = {}
        self.cnt = {}

    def sem(self, name):
        if name not in self.sems:
            self.sems[name] = self.ctx.enter_context(self.nc.semaphore(name))
            self.cnt[name] = 0
        return self.sems[name]

    def wait(self, eng, name, val):
        if val is None or val <= 0:
            return
        s = self.sem(name)
        self.stream[eng].append(lambda e, s=s, v=val: e.wait_ge(s, v))

    def op(self, eng, fn, sem=None, n=1):
        if sem is None:
            self.stream[eng].append(fn)
            return None
        s = self.sem(sem)
        self.cnt[sem] += n
        v = self.cnt[sem]
        self.stream[eng].append(lambda e, fn=fn, s=s, n=n: fn(e).then_inc(s, n))
        return v

    def dma(self, eng, dst, src, sem=None):
        if sem is None:
            sem = "di" if eng == "sync" else "do"
        return self.op(eng, lambda e, d=dst, s=src: e.dma_start(out=d, in_=s), sem, 16)

    def val(self, name):
        return self.cnt.get(name, 0)

    def fence(self, engines=("tensor", "vector", "scalar", "sync", "gpsimd")):
        snap = {k: self.val(k) for k in ("t", "v", "s", "g")}
        for e in engines:
            for k, v in snap.items():
                if _ENG_SEM.get(e) == k:
                    continue
                self.wait(e, k, v)

    def mm3(self, bank, lh, ll, rh, rl, start, stop, sem=None):
        self.op("tensor", lambda e, o=bank, l=lh, r=rh, st=start: e.matmul(o, l, r, start=st, stop=False))
        self.op("tensor", lambda e, o=bank, l=lh, r=rl: e.matmul(o, l, r, start=False, stop=False))
        return self.op("tensor", lambda e, o=bank, l=ll, r=rh, sp=stop: e.matmul(o, l, r, start=False, stop=sp), sem)


def emit_ln(P, sX, sXN, ln, dma_sem, dma_val):
    (sNegSum, sSumSq, sNegM, sMsq, sVar, sStd, sRstd, sEps) = ln
    P.wait("vector", dma_sem, dma_val)
    rs = P.op("vector", lambda e, o=sNegSum, i=sX: e.reduce_sum(o[:, :], i[:], axis=AX.X, negate=True), "v")
    P.wait("scalar", dma_sem, dma_val)
    a = P.op("scalar", lambda e, o=sXN, i=sX, acc=sSumSq: e.activation(o[:], i[:], AF.Square, accum_out=acc[:, :]), "s")
    P.wait("vector", "s", a)
    P.wait("vector", "v", rs)
    b = P.op("vector", lambda e, o=sNegM, i=sNegSum: e.tensor_scalar_mul(o[:, :], i[:, :], RN), "v")
    P.wait("vector", "v", b)
    c = P.op("vector", lambda e, o=sMsq, i=sNegM: e.tensor_mul(o[:, :], i[:, :], i[:, :]), "v")
    P.wait("vector", "v", c)
    d = P.op("vector", lambda e, o=sVar, i=sSumSq, j=sMsq: e.scalar_tensor_tensor(
        o[:, :], i[:, :], RN, j[:, :], ALU.mult, ALU.subtract), "v")
    P.wait("scalar", "v", d)
    ee = P.op("scalar", lambda e, o=sStd, i=sVar, ep=sEps: e.activation(o[:, :], i[:, :], AF.Sqrt, bias=ep[:, 0:1]), "s")
    P.wait("vector", "s", ee)
    f = P.op("vector", lambda e, o=sRstd, i=sStd: e.reciprocal(o[:, :], i[:, :]), "v")
    P.wait("vector", "v", f)
    g = P.op("vector", lambda e, o=sXN, i=sX, m=sNegM, r=sRstd: e.tensor_scalar(
        o[:], i[:], m[:, 0:1], r[:, 0:1], ALU.add, ALU.mult), "v")
    return {"xn_v": g, "sq_s": a}


def build_nc():
    nc = bass.Bass("TRN2", target_bir_lowering=False, debug=False)

    x = nc.dram_tensor("x", [T, C], F32, kind="ExternalInput").ap()
    xq = nc.dram_tensor("xq", [1024, C], F32, kind="ExternalInput").ap()
    wq = nc.dram_tensor("wq", [NH, 128, C], F32, kind="ExternalInput").ap()
    wkv = nc.dram_tensor("wkv", [16, 128, 1024], F32, kind="ExternalInput").ap()
    wo = nc.dram_tensor("wo", [16, 128, C], F32, kind="ExternalInput").ap()
    dwt = nc.dram_tensor("dwt", [16, 128, 128], F32, kind="ExternalInput").ap()
    u128 = nc.dram_tensor("u128", [128, C], F32, kind="ExternalInput").ap()
    rwt = nc.dram_tensor("rwt", [16, 128, 8], F32, kind="ExternalInput").ap()
    cost = nc.dram_tensor("cost", [128, T], F32, kind="ExternalInput").ap()
    sint = nc.dram_tensor("sint", [128, T], F32, kind="ExternalInput").ap()
    cosq = nc.dram_tensor("cosq", [128, 1024], F32, kind="ExternalInput").ap()
    sinq = nc.dram_tensor("sinq", [128, 1024], F32, kind="ExternalInput").ap()
    rt = nc.dram_tensor("rt", [128, 128], F32, kind="ExternalInput").ap()
    rep = nc.dram_tensor("rep", [8, 128], F32, kind="ExternalInput").ap()
    ident = nc.dram_tensor("ident", [128, 128], F32, kind="ExternalInput").ap()
    onesc = nc.dram_tensor("onesc", [128, 1], F32, kind="ExternalInput").ap()
    onesr = nc.dram_tensor("onesr", [1, 128], F32, kind="ExternalInput").ap()
    maskt = nc.dram_tensor("maskt", [128, 8], F32, kind="ExternalInput").ap()
    out = nc.dram_tensor("out", [1024, C], F32, kind="ExternalOutput").ap()
    wqh = nc.dram_tensor("wqh", [NH, 128, C], BF16).ap()
    wql = nc.dram_tensor("wql", [NH, 128, C], BF16).ap()
    woh = nc.dram_tensor("woh", [16, 128, C], BF16).ap()
    wol = nc.dram_tensor("wol", [16, 128, C], BF16).ap()
    wkvh = nc.dram_tensor("wkvh", [16, 128, 1024], BF16).ap()
    wkvl = nc.dram_tensor("wkvl", [16, 128, 1024], BF16).ap()

    with contextlib.ExitStack() as ctx:
        P = Prog(nc, ctx)
        sbf = lambda name, shape, dt=F32: ctx.enter_context(nc.sbuf_tensor(name, shape, dt))

        sKh = sbf("sKh", [128, NKV * T], BF16)
        sKl = sbf("sKl", [128, NKV * T], BF16)
        sVh = sbf("sVh", [128, 16 * 512], BF16)
        sVl = sbf("sVl", [128, 16 * 512], BF16)
        sCosQ = sbf("sCosQ", [128, 1024])
        sSinQ = sbf("sSinQ", [128, 1024])
        sDWh = sbf("sDWh", [128, 16 * 128], BF16)
        sDWl = sbf("sDWl", [128, 16 * 128], BF16)
        sUh = sbf("sUh", [128, C], BF16)
        sUl = sbf("sUl", [128, C], BF16)
        sRW = sbf("sRW", [128, 16 * 8])
        sRT = sbf("sRT", [128, 128])
        sID = sbf("sID", [128, 128])
        sREP = sbf("sREP", [8, 128])
        sOC = sbf("sOC", [128, 1])
        sOR = sbf("sOR", [1, 128])
        sMask = sbf("sMask", [128, 8])
        sX = sbf("sX", [128, C])
        sX2 = sbf("sX2", [128, C])
        sXN = sbf("sXN", [128, C])
        sT1 = sbf("sT1", [128, 256])
        sT2 = sbf("sT2", [128, 256])
        sXf = sbf("sXf", [128, 4 * 128])
        sEps = sbf("sEps", [128, 1])
        ln = tuple(sbf(n, [128, 1]) for n in
                   ("sNegSum", "sSumSq", "sNegM", "sMsq", "sVar", "sStd", "sRstd")) + (sEps,)

        P.op("vector", lambda e, o=sEps: e.memset(o[:, :], EPS), "v")
        for d, s in ((sRT[:], rt), (sID[:], ident), (sREP[:], rep),
                     (sOC[:], onesc), (sOR[:], onesr), (sMask[:], maskt),
                     (sCosQ[:], cosq), (sSinQ[:], sinq)):
            P.dma("sync", d, s)
        for cch in range(16):
            P.dma("sync", sRW[:, cch * 8:(cch + 1) * 8], rwt[cch])

        def split(dst_h, dst_l, src):
            hv = P.op("vector", lambda e, o=dst_h, i=src: e.tensor_copy(o, i), "v")
            P.wait("gpsimd", "v", hv)
            gv = P.op("gpsimd", lambda e, o=dst_l, i=src, hh=dst_h: e.tensor_sub(o, i, hh), "g")
            return hv, gv

        def split_v(dst_h, dst_l, src):
            hv = P.op("vector", lambda e, o=dst_h, i=src: e.tensor_copy(o, i), "v")
            P.wait("vector", "v", hv)
            lv = P.op("vector", lambda e, o=dst_l, i=src, hh=dst_h: e.tensor_sub(o, i, hh), "v")
            return hv, lv

        for cch in range(16):
            P.dma("sync", sXN[:, cch * 128:(cch + 1) * 128], dwt[cch])
        dwd = P.val("di")
        P.wait("vector", "di", dwd)
        dw_h, dw_g = split(sDWh[:], sDWl[:], sXN[:])
        ud = P.dma("sync", sX[:], u128)
        P.wait("vector", "di", ud)
        u_h, u_g = split(sUh[:], sUl[:], sX[:])

        with contextlib.ExitStack() as p1:
            sb1 = lambda name, shape, dt=F32: p1.enter_context(nc.sbuf_tensor(name, shape, dt))
            sXnTh = sb1("sXnTh", [128, 16 * 512], BF16)
            sXnTl = sb1("sXnTl", [128, 16 * 512], BF16)
            sWkvh = sb1("sWkvh", [128, 2 * 1024], BF16)
            sWkvl = sb1("sWkvl", [128, 2 * 1024], BF16)
            sKst = sb1("sKst", [128, 4 * 512])
            sCos = sb1("sCos", [128, T])
            sSin = sb1("sSin", [128, T])
            sT3 = sb1("sT3", [128, 256])
            sXN2 = sb1("sXN2", [128, C])
            ln2 = tuple(sb1(n, [128, 1]) for n in
                        ("sNegSum2", "sSumSq2", "sNegM2", "sMsq2", "sVar2", "sStd2", "sRstd2")) + (ln[-1],)
            sWS = sb1("sWS", [128, 1024])
            sWH = sb1("sWH", [128, 2 * 1024], BF16)
            sWL = sb1("sWL", [128, 2 * 1024], BF16)
            pKb = p1.enter_context(nc.psum_tensor("pKb", [128, 2048], F32))
            pVb = p1.enter_context(nc.psum_tensor("pVb", [128, 2048], F32))

            P.dma("sync", sCos[:], cost)
            P.dma("sync", sSin[:], sint)
            cst_di = P.val("di")
            P.wait("tensor", "di", cst_di)
            P.wait("vector", "di", cst_di)
            P.wait("scalar", "v", max(dw_h, u_h))
            P.wait("scalar", "g", max(dw_g, u_g))
            P.wait("sync", "g", max(dw_g, u_g))

            wsplit_state = {"j": 0, "vg": {}, "dw": {}}
            wsplit_jobs = []
            for cch in range(16):
                wsplit_jobs.append((wkv[cch], 1024, wkvh[cch], wkvl[cch]))
            for hh in range(NH):
                for hf in range(2):
                    wsplit_jobs.append((wq[hh][:, hf * 1024:(hf + 1) * 1024], 1024,
                                        wqh[hh][:, hf * 1024:(hf + 1) * 1024],
                                        wql[hh][:, hf * 1024:(hf + 1) * 1024]))
            for cch in range(16):
                for hf in range(2):
                    wsplit_jobs.append((wo[cch][:, hf * 1024:(hf + 1) * 1024], 1024,
                                        woh[cch][:, hf * 1024:(hf + 1) * 1024],
                                        wol[cch][:, hf * 1024:(hf + 1) * 1024]))

            def emit_wsplit(njobs):
                st = wsplit_state
                for _ in range(njobs):
                    if not wsplit_jobs:
                        return
                    src, width, dsth, dstl = wsplit_jobs.pop(0)
                    j = st["j"]
                    st["j"] += 1
                    par = j % 2
                    P.wait("sync", "v", st["vg"].get(("v", j - 1), 0))
                    P.wait("sync", "g", st["vg"].get(("g", j - 1), 0))
                    dv = P.dma("sync", sWS[:, 0:width], src)
                    P.wait("vector", "di", dv)
                    P.wait("vector", "dw", st["dw"].get(("h", j - 2), 0))
                    hv = P.op("vector", lambda e, o=sWH[:, par * 1024:par * 1024 + width],
                              i=sWS[:, 0:width]: e.tensor_copy(o, i), "v")
                    P.wait("gpsimd", "v", hv)
                    P.wait("gpsimd", "dw", st["dw"].get(("l", j - 2), 0))
                    gv = P.op("gpsimd", lambda e, o=sWL[:, par * 1024:par * 1024 + width],
                              i=sWS[:, 0:width], hh2=sWH[:, par * 1024:par * 1024 + width]:
                              e.tensor_sub(o, i, hh2), "g")
                    st["vg"][("v", j)] = hv
                    st["vg"][("g", j)] = gv
                    P.wait("scalar", "v", hv)
                    st["dw"][("h", j)] = P.dma("scalar", dsth, sWH[:, par * 1024:par * 1024 + width], "dw")
                    P.wait("scalar", "g", gv)
                    st["dw"][("l", j)] = P.dma("scalar", dstl, sWL[:, par * 1024:par * 1024 + width], "dw")
                    st["dw"][j] = st["dw"][("l", j)]

            emit_wsplit(16)
            wkv_dw = {c: wsplit_state["dw"][c] for c in range(16)}

            def tr_slot(j):
                j = j % 8
                buf = pKb if j < 4 else pVb
                base = (j % 4) * 512 + 256
                return buf[:, base:base + 128]

            tr_evac_s = [0] * 8
            prev_evac_s, prev_rope_v, prev_rope_g = 0, 0, 0
            x_free = {0: (0, 0), 1: (0, 0)}
            xn_done = {}
            wkv_t = {}
            kc = 0
            split_done = []

            def emit_p1_ln(tb):
                par = tb % 2
                bx = sX if par == 0 else sX2
                bn = sXN if par == 0 else sXN2
                bl = ln if par == 0 else ln2
                P.wait("sync", "v", x_free[par][0])
                P.wait("sync", "s", x_free[par][1])
                xd = P.dma("sync", bx[:], x[tb * 128:(tb + 1) * 128, :])
                lv = emit_ln(P, bx, bn, bl, "di", xd)
                x_free[par] = (lv["xn_v"], lv["sq_s"])
                xn_done[tb] = lv

            emit_p1_ln(0)
            emit_p1_ln(1)
            for g in range(4):
                for ti in range(4):
                    tb = g * 4 + ti
                    bn = sXN if tb % 2 == 0 else sXN2
                    lv = xn_done[tb]
                    P.wait("tensor", "v", lv["xn_v"])
                    if g > 0 and ti == 0:
                        P.wait("tensor", "s", prev_evac_s)
                        P.wait("tensor", "v", prev_rope_v)
                        P.wait("tensor", "g", prev_rope_g)
                    ttr = 0
                    for cch in range(16):
                        j = (ti * 16 + cch) % 8
                        P.wait("tensor", "s", tr_evac_s[j])
                        slot = tr_slot(j)
                        ttr = P.op("tensor", lambda e, o=slot, i=bn[:, cch * 128:(cch + 1) * 128],
                                   idm=sID: e.transpose(o, i, idm[:, :]), "t")
                        P.wait("scalar", "t", ttr)
                        xf = sXf[:, (j % 4) * 128:(j % 4 + 1) * 128]
                        P.wait("scalar", "v", split_done[-4][0] if len(split_done) >= 4 else 0)
                        P.wait("scalar", "g", split_done[-4][1] if len(split_done) >= 4 else 0)
                        tr_evac_s[j] = P.op("scalar", lambda e, o=xf, i=slot: e.copy(o, i), "s")
                        P.wait("vector", "s", tr_evac_s[j])
                        dh = sXnTh[:, cch * 512 + ti * 128: cch * 512 + ti * 128 + 128]
                        dl = sXnTl[:, cch * 512 + ti * 128: cch * 512 + ti * 128 + 128]
                        split_done.append(split(dh, dl, xf))
                    P.wait("scalar", "t", ttr)
                    if tb + 2 < 16:
                        emit_p1_ln(tb + 2)
                xn_split_v = split_done[-1][0]
                xn_split_g = split_done[-1][1]

                for cch in range(16):
                    P.wait("sync", "t", wkv_t.get(kc - 2, 0))
                    P.wait("sync", "dw", wkv_dw[cch])
                    P.dma("sync", sWkvh[:, (kc % 2) * 1024:(kc % 2 + 1) * 1024], wkvh[cch])
                    wv = P.dma("sync", sWkvl[:, (kc % 2) * 1024:(kc % 2 + 1) * 1024], wkvl[cch])
                    P.wait("tensor", "di", wv)
                    if cch == 0:
                        P.wait("tensor", "v", xn_split_v)
                        P.wait("tensor", "g", xn_split_g)
                    st0, sp0 = (cch == 0), (cch == 15)
                    base = (kc % 2) * 1024
                    lastmm = 0
                    for kvh in range(NKV):
                        lastmm = P.mm3(pKb[:, kvh * 512:(kvh + 1) * 512],
                                       sWkvh[:, base + kvh * 128:base + (kvh + 1) * 128],
                                       sWkvl[:, base + kvh * 128:base + (kvh + 1) * 128],
                                       sXnTh[:, cch * 512:(cch + 1) * 512],
                                       sXnTl[:, cch * 512:(cch + 1) * 512], st0, sp0, "t")
                    for ti in range(4):
                        lastmm = P.mm3(pVb[:, ti * 512:(ti + 1) * 512],
                                       sXnTh[:, cch * 512 + ti * 128:cch * 512 + (ti + 1) * 128],
                                       sXnTl[:, cch * 512 + ti * 128:cch * 512 + (ti + 1) * 128],
                                       sWkvh[:, base + 512:base + 1024],
                                       sWkvl[:, base + 512:base + 1024], st0, sp0, "t")
                    wkv_t[kc] = lastmm
                    kc += 1
                kv_done = wkv_t[kc - 1]

                P.wait("scalar", "t", kv_done)
                P.wait("vector", "t", kv_done)
                ev_s, ev_v = 0, 0
                for ti in range(4):
                    dsth = sVh[:, (g * 4 + ti) * 512:(g * 4 + ti + 1) * 512]
                    dstl = sVl[:, (g * 4 + ti) * 512:(g * 4 + ti + 1) * 512]
                    ev_s = P.op("scalar", lambda e, o=dsth, i=pVb[:, ti * 512:(ti + 1) * 512]: e.copy(o, i), "s")
                    P.wait("vector", "s", ev_s)
                    ev_v = P.op("vector", lambda e, o=dstl, i=pVb[:, ti * 512:(ti + 1) * 512], hh=dsth:
                                e.tensor_sub(o, i, hh), "v")
                evK = [0] * NKV
                for kvh in range(NKV):
                    evK[kvh] = P.op("scalar", lambda e, o=sKst[:, kvh * 512:(kvh + 1) * 512],
                                    i=pKb[:, kvh * 512:(kvh + 1) * 512]: e.copy(o, i), "s")
                prev_evac_s = max(ev_s, max(evK))

                rv, rg = prev_rope_v, prev_rope_g
                for kvh in range(NKV):
                    P.wait("tensor", "s", evK[kvh])
                    if kvh == 0:
                        P.wait("tensor", "v", ev_v)
                    rot_t = P.op("tensor", lambda e, o=pKb[:, kvh * 512:(kvh + 1) * 512], l=sRT,
                                 r=sKst[:, kvh * 512:(kvh + 1) * 512]:
                                 e.matmul(o, l[:, :], r, start=True, stop=True), "t")
                    P.wait("vector", "t", rot_t)
                    for u in range(2):
                        csl = slice(g * 512 + u * 256, g * 512 + (u + 1) * 256)
                        ksl = slice(kvh * 512 + u * 256, kvh * 512 + (u + 1) * 256)
                        P.op("vector", lambda e, o=sT1, i=sKst[:, ksl], c2=sCos[:, csl]: e.tensor_mul(o[:], i, c2))
                        P.op("vector", lambda e, o=sT2, i=pKb[:, ksl], c2=sSin[:, csl]: e.tensor_mul(o[:], i, c2))
                        P.wait("vector", "g", rg)
                        t3v = P.op("vector", lambda e, o=sT3, a=sT1, b=sT2: e.tensor_add(o[:], a[:], b[:]), "v")
                        P.wait("vector", "v", t3v)
                        kdst = slice(kvh * T + g * 512 + u * 256, kvh * T + g * 512 + (u + 1) * 256)
                        rv, rg = split(sKh[:, kdst], sKl[:, kdst], sT3[:])
                prev_rope_v, prev_rope_g = rv, rg

                emit_wsplit(8)

            emit_wsplit(100)
            wsp_done = P.val("dw")

            ph = P.sem("ph")
            for eng in ENGS:
                P.op(eng, lambda e, s=ph: e.nop().then_inc(s, 1))
            for eng in ENGS:
                P.wait(eng, "ph", 5)

        with contextlib.ExitStack() as p2:
            sb2 = lambda name, shape, dt=F32: p2.enter_context(nc.sbuf_tensor(name, shape, dt))
            sXnTqh = sb2("sXnTqh", [128, 16 * 256], BF16)
            sXnTql = sb2("sXnTql", [128, 16 * 256], BF16)
            sYallh = sb2("sYallh", [128, 16 * 256], BF16)
            sYalll = sb2("sYalll", [128, 16 * 256], BF16)
            sPallh = sb2("sPallh", [128, 16 * 256], BF16)
            sPalll = sb2("sPalll", [128, 16 * 256], BF16)
            sWqh = sb2("sWqh", [128, 2 * 1024], BF16)
            sWql = sb2("sWql", [128, 2 * 1024], BF16)
            sWoh = sb2("sWoh", [128, 2 * 1024], BF16)
            sWol = sb2("sWol", [128, 2 * 1024], BF16)
            sQh = sb2("sQh", [128, 2 * 256])
            sQhh = sb2("sQhh", [128, 2 * 256], BF16)
            sQhl = sb2("sQhl", [128, 2 * 256], BF16)
            sPf = sb2("sPf", [128, 4 * 256])
            sYf = sb2("sYf", [128, 256])
            sAcc = sb2("sAcc", [128, 256])
            sB = sb2("sB", [128, 256])
            sOT = sb2("sOT", [128, 2 * 256])
            sDr = sb2("sDr", [1, 256])
            sW8 = sb2("sW8", [128, 16])
            sW2 = sb2("sW2", [128, 16])
            sGmask = sb2("sGmask", [128, 16])
            sGate = sb2("sGate", [128, 16])
            sG8 = sb2("sG8", [8, 256])
            sA = sb2("sA", [128, 256])
            sAGf = sb2("sAGf", [128, 256])
            sAGh = sb2("sAGh", [128, 256], BF16)
            sAGl = sb2("sAGl", [128, 256], BF16)
            sNM8 = sb2("sNM8", [128, 2])
            sDen8 = sb2("sDen8", [128, 2])
            sRec8 = sb2("sRec8", [128, 2])
            sM1 = sb2("sM1", [128, 2])
            sM2 = sb2("sM2", [128, 2])
            pb = [p2.enter_context(nc.psum_tensor("pb%d" % i, [128, 512], F32)) for i in range(8)]
            pQ, pRot, pS0, pS1, pY, pD, pB, pF = pb

            P.wait("sync", "dw", wsp_done)

            wq_t = {}
            wo_t = {}
            x_free_v, x_free_s = 0, 0

            for qt in range(4):
                KB = 4 * qt + 4
                P.fence()

                tr_evac = [0] * 4
                split_hist = []
                rt_mm = {}
                for t2 in range(2):
                    sidx = 2 * qt + t2
                    if x_free_v is not None:
                        P.wait("sync", "v", x_free_v)
                    P.wait("sync", "s", x_free_s)
                    P.wait("sync", "do2", P.val("do2"))
                    xd = P.dma("sync", sX[:], xq[sidx * 128:(sidx + 1) * 128, :])
                    lv = emit_ln(P, sX, sXN, ln, "di", xd)
                    x_free_v, x_free_s = lv["xn_v"], lv["sq_s"]
                    P.wait("tensor", "v", lv["xn_v"])
                    ttr = 0
                    for cch in range(16):
                        j = (t2 * 16 + cch) % 4
                        slot = pb[j][:, 256:384]
                        P.wait("tensor", "s", tr_evac[j])
                        ttr = P.op("tensor", lambda e, o=slot, i=sXN[:, cch * 128:(cch + 1) * 128],
                                   idm=sID: e.transpose(o, i, idm[:, :]), "t")
                        P.wait("scalar", "t", ttr)
                        xf = sXf[:, (cch % 4) * 128:(cch % 4 + 1) * 128]
                        P.wait("scalar", "v", split_hist[-4][0] if len(split_hist) >= 4 else 0)
                        P.wait("scalar", "g", split_hist[-4][1] if len(split_hist) >= 4 else 0)
                        P.wait("scalar", "t", rt_mm.get((t2, cch - 4), 0))
                        tr_evac[j] = P.op("scalar", lambda e, o=xf, i=slot: e.copy(o, i), "s")
                        P.wait("vector", "s", tr_evac[j])
                        dh = sXnTqh[:, cch * 256 + t2 * 128: cch * 256 + t2 * 128 + 128]
                        dl = sXnTql[:, cch * 256 + t2 * 128: cch * 256 + t2 * 128 + 128]
                        split_hist.append(split(dh, dl, xf))
                        pRt = pD[:, 256:264] if t2 == 0 else pD[:, 384:392]
                        P.wait("tensor", "s", tr_evac[j])
                        rt_mm[(t2, cch)] = P.op("tensor", lambda e, o=pRt, l=xf, r=sRW[:, cch * 8:(cch + 1) * 8],
                                                st=(cch == 0), sp=(cch == 15): e.matmul(o, l, r, start=st, stop=sp), "t")
                    P.wait("scalar", "t", ttr)
                xnq_v = split_hist[-1][0]
                xnq_g = split_hist[-1][1]

                P.wait("tensor", "v", xnq_v)
                P.wait("tensor", "g", xnq_g)
                hmm = 0
                for cch in range(16):
                    hmm = P.mm3(pF[:, 0:256], sDWh[:, cch * 128:(cch + 1) * 128], sDWl[:, cch * 128:(cch + 1) * 128],
                                sXnTqh[:, cch * 256:(cch + 1) * 256], sXnTql[:, cch * 256:(cch + 1) * 256],
                                (cch == 0), (cch == 15), "t")
                P.wait("scalar", "t", hmm)
                silu_s = P.op("scalar", lambda e, o=sA, i=pF[:, 0:256]: e.activation(o[:], i, AF.Silu), "s")
                ag_v = 0
                for t2 in range(2):
                    pRt = pD[:, 256:264] if t2 == 0 else pD[:, 384:392]
                    rmm = max(rt_mm[(0, 15)], rt_mm[(1, 15)])
                    P.wait("vector", "t", rmm)
                    nm = P.op("vector", lambda e, o=sNM8[:, t2:t2 + 1], i=pRt:
                              e.reduce_max(o, i, axis=AX.X, negate=True), "v")
                    P.wait("scalar", "v", nm)
                    ex = P.op("scalar", lambda e, o=sW8[:, t2 * 8:(t2 + 1) * 8], i=pRt, b=sNM8[:, t2:t2 + 1],
                              acc=sDen8[:, t2:t2 + 1]: e.activation(o, i, AF.Exp, bias=b, accum_out=acc), "s")
                    P.wait("vector", "s", ex)
                    rc = P.op("vector", lambda e, o=sRec8[:, t2:t2 + 1], i=sDen8[:, t2:t2 + 1]:
                              e.reciprocal(o, i), "v")
                    P.wait("vector", "v", rc)
                    sm = P.op("vector", lambda e, o=sW8[:, t2 * 8:(t2 + 1) * 8], r=sRec8[:, t2:t2 + 1]:
                              e.tensor_scalar_mul(o, o, r), "v")
                    P.wait("vector", "v", sm)
                    m1 = P.op("vector", lambda e, o=sM1[:, t2:t2 + 1], i=sW8[:, t2 * 8:(t2 + 1) * 8]:
                              e.reduce_max(o, i, axis=AX.X), "v")
                    P.wait("vector", "v", m1)
                    eq = P.op("vector", lambda e, o=sGmask[:, t2 * 8:(t2 + 1) * 8], i=sW8[:, t2 * 8:(t2 + 1) * 8],
                              m=sM1[:, t2:t2 + 1]: e.tensor_single_scalar(o, i, m, ALU.is_ge), "v")
                    P.wait("vector", "v", eq)
                    w2 = P.op("vector", lambda e, o=sW2[:, t2 * 8:(t2 + 1) * 8], i=sGmask[:, t2 * 8:(t2 + 1) * 8],
                              w=sW8[:, t2 * 8:(t2 + 1) * 8]: e.scalar_tensor_tensor(o, i, -1e30, w, ALU.mult, ALU.add), "v")
                    P.wait("vector", "v", w2)
                    m2 = P.op("vector", lambda e, o=sM2[:, t2:t2 + 1], i=sW2[:, t2 * 8:(t2 + 1) * 8]:
                              e.reduce_max(o, i, axis=AX.X), "v")
                    P.wait("vector", "v", m2)
                    gm = P.op("vector", lambda e, o=sGmask[:, t2 * 8:(t2 + 1) * 8], i=sW8[:, t2 * 8:(t2 + 1) * 8],
                              m=sM2[:, t2:t2 + 1]: e.tensor_single_scalar(o, i, m, ALU.is_ge), "v")
                    P.wait("vector", "v", gm)
                    gv = P.op("vector", lambda e, o=sGate[:, t2 * 8:(t2 + 1) * 8], i=sW8[:, t2 * 8:(t2 + 1) * 8],
                              m=sGmask[:, t2 * 8:(t2 + 1) * 8]: e.tensor_mul(o, i, m), "v")
                    P.wait("tensor", "v", gv)
                    gt_t = P.op("tensor", lambda e, o=pB[0:8, 384:512], i=sGate[:, t2 * 8:(t2 + 1) * 8], idm=sID:
                                e.transpose(o, i, idm[:, :]), "t")
                    P.wait("scalar", "t", gt_t)
                    g8 = P.op("scalar", lambda e, o=sG8[0:8, t2 * 128:(t2 + 1) * 128], i=pB[0:8, 384:512]:
                              e.copy(o, i), "s")
                    P.wait("tensor", "s", g8)
                    if t2 == 1:
                        P.wait("tensor", "v", ag_v)
                    grep_t = P.op("tensor", lambda e, o=pB[:, 256:384], l=sREP, r=sG8[0:8, t2 * 128:(t2 + 1) * 128]:
                                  e.matmul(o, l[:, :], r, start=True, stop=True), "t")
                    P.wait("vector", "t", grep_t)
                    P.wait("vector", "s", silu_s)
                    ag_v = P.op("vector", lambda e, o=sAGf[:, t2 * 128:(t2 + 1) * 128], i=sA[:, t2 * 128:(t2 + 1) * 128],
                                g2=pB[:, 256:384]: e.tensor_mul(o, i, g2), "v")
                P.wait("vector", "v", ag_v)
                agh, agg = split_v(sAGh[:], sAGl[:], sAGf[:])

                S = {}

                def stage_q(h, qt=qt):
                    par = h % 2
                    st = {"par": par}
                    wdi = {}
                    for half in range(2):
                        jj = 2 * h + half
                        P.wait("sync", "t", wq_t.get(jj - 2, 0))
                        P.dma("sync", sWqh[:, (jj % 2) * 1024:(jj % 2 + 1) * 1024],
                              wqh[h][:, half * 1024:(half + 1) * 1024])
                        wdi[half] = P.dma("sync", sWql[:, (jj % 2) * 1024:(jj % 2 + 1) * 1024],
                                          wql[h][:, half * 1024:(half + 1) * 1024])
                    if h == 0:
                        P.wait("tensor", "v", xnq_v)
                        P.wait("tensor", "g", xnq_g)
                    if h >= 1:
                        P.wait("tensor", "s", S[h - 1]["qev"])
                    qmm = 0
                    for cch in range(16):
                        jj = 2 * h + cch // 8
                        if cch == 0:
                            P.wait("tensor", "di", wdi[0])
                        if cch == 8:
                            P.wait("tensor", "di", wdi[1])
                        base2 = (jj % 2) * 1024 + (cch % 8) * 128
                        qmm = P.mm3(pQ[:, 0:256], sWqh[:, base2: base2 + 128],
                                    sWql[:, base2: base2 + 128],
                                    sXnTqh[:, cch * 256:(cch + 1) * 256], sXnTql[:, cch * 256:(cch + 1) * 256],
                                    (cch == 0), (cch == 15), "t" if cch == 15 else None)
                    wq_t[2 * h] = qmm
                    wq_t[2 * h + 1] = qmm
                    st["qmm"] = qmm
                    P.wait("scalar", "t", qmm)
                    st["qev"] = P.op("scalar", lambda e, o=sQh[:, par * 256:(par + 1) * 256], i=pQ[:, 0:256]:
                                     e.copy(o, i), "s")
                    S[h] = st

                def stage_rot(h, qt=qt):
                    st = S[h]
                    par = st["par"]
                    P.wait("tensor", "s", st["qev"])
                    rot_t = P.op("tensor", lambda e, o=pRot[:, 0:256], l=sRT, r=sQh[:, par * 256:(par + 1) * 256]:
                                 e.matmul(o, l[:, :], r, start=True, stop=True), "t")
                    P.wait("vector", "t", rot_t)
                    for t2 in range(2):
                        qsl = slice((2 * qt + t2) * 128, (2 * qt + t2 + 1) * 128)
                        dsl = slice(t2 * 128, (t2 + 1) * 128)
                        P.op("vector", lambda e, o=sT1[:, dsl],
                             i=sQh[:, par * 256 + t2 * 128: par * 256 + (t2 + 1) * 128],
                             c2=sCosQ[:, qsl]: e.tensor_mul(o, i, c2))
                        P.op("vector", lambda e, o=sT2[:, dsl], i=pRot[:, dsl], c2=sSinQ[:, qsl]:
                             e.tensor_mul(o, i, c2))
                    rope_v = P.op("vector", lambda e, o=sQh[:, par * 256:(par + 1) * 256], a=sT1, b=sT2:
                                  e.tensor_add(o, a[:], b[:]), "v")
                    P.wait("vector", "v", rope_v)
                    st["rope_v"] = rope_v
                    qs_v, qs_lo = split_v(sQhh[:, par * 256:(par + 1) * 256], sQhl[:, par * 256:(par + 1) * 256],
                                          sQh[:, par * 256:(par + 1) * 256])
                    st["qs_v"] = qs_lo

                def stage_s(h, qt=qt, KB=KB):
                    st = S[h]
                    par = st["par"]
                    kvh = h // 4
                    full_end = 4 * qt + 2
                    exp_s = {}
                    hi_s = {}
                    lo_v = {}
                    acc_v = 0
                    mask_v = 0
                    av_t = 0
                    pairs = []
                    kb = 0
                    while kb < KB:
                        paired = (kb % 2 == 0) and (kb + 1 < full_end)
                        pairs.append((kb, 2 if paired else 1))
                        kb += 2 if paired else 1

                    def emit_av(pi):
                        nonlocal av_t
                        kb0, n2 = pairs[pi]
                        p0 = kb0 // 2
                        P.wait("tensor", "s", hi_s[p0])
                        P.wait("tensor", "v", lo_v[p0])
                        if pi == 0:
                            P.wait("tensor", "v", S.get(h - 1, {}).get("ym", 0))
                        for kbx in range(kb0, kb0 + n2):
                            cols = slice(0, 256) if kbx < full_end else slice(128, 256)
                            vsl = slice(kbx * 512 + kvh * 128, kbx * 512 + (kvh + 1) * 128)
                            av_t = P.mm3(pY[:, cols], sVh[:, vsl], sVl[:, vsl],
                                         sPallh[:, kbx * 256 + cols.start: kbx * 256 + cols.stop],
                                         sPalll[:, kbx * 256 + cols.start: kbx * 256 + cols.stop],
                                         (kbx == 0), (kbx == KB - 1),
                                         "t" if kbx == KB - 1 else None)

                    for pi, (kb, n2) in enumerate(pairs):
                        p = kb // 2
                        bank = pS0 if (p % 2) == 0 else pS1
                        if kb == 0:
                            P.wait("tensor", "v", st["qs_v"])
                            P.wait("scalar", "t", S.get(h - 1, {}).get("av_t", 0))
                        if p >= 2:
                            P.wait("tensor", "s", exp_s.get(p - 2, 0))
                        smm = 0
                        for k2 in range(n2):
                            kbx = kb + k2
                            cols = slice(0, 256) if kbx < full_end else slice(128, 256)
                            boff = (kbx % 2) * 256
                            ksl = slice(kvh * T + kbx * 128, kvh * T + (kbx + 1) * 128)
                            qsl2 = slice(par * 256 + cols.start, par * 256 + cols.stop)
                            smm = P.mm3(bank[:, boff + cols.start: boff + cols.stop], sKh[:, ksl], sKl[:, ksl],
                                        sQhh[:, qsl2], sQhl[:, qsl2], True, True, "t")
                        if pi >= 1:
                            emit_av(pi - 1)
                        P.wait("scalar", "t", smm)
                        P.wait("scalar", "v", lo_v.get(p - 2, 0))
                        sl = (p % 2) * 512
                        if n2 == 2:
                            psrc = bank[:, 0:512]
                            pfdst = sPf[:, sl:sl + 512]
                        else:
                            cols = slice(0, 256) if kb < full_end else slice(128, 256)
                            boff = (kb % 2) * 256
                            psrc = bank[:, boff + cols.start: boff + cols.stop]
                            pfdst = sPf[:, sl + boff + cols.start: sl + boff + cols.stop]
                        exp_s[p] = P.op("scalar", lambda e, o=pfdst, i=psrc:
                                        e.activation(o, i, AF.Exp, scale=SCALE), "s")
                        P.wait("vector", "s", exp_s[p])
                        for kbx in range(kb, kb + n2):
                            if kbx == 4 * qt + 1:
                                mask_v = P.op("vector", lambda e, o=sPf[:, sl + (kbx % 2) * 256: sl + (kbx % 2) * 256 + 128],
                                              m=sMask[:, 2 * qt:2 * qt + 1]: e.tensor_scalar_mul(o, o, m), "v")
                            if kbx == 4 * qt + 3:
                                mask_v = P.op("vector", lambda e, o=sPf[:, sl + (kbx % 2) * 256 + 128: sl + (kbx % 2) * 256 + 256],
                                              m=sMask[:, 2 * qt + 1:2 * qt + 2]: e.tensor_scalar_mul(o, o, m), "v")
                        for kbx in range(kb, kb + n2):
                            cols = slice(0, 256) if kbx < full_end else slice(128, 256)
                            if kbx == 0:
                                P.wait("vector", "t", S.get(h - 1, {}).get("d_t", 0))
                                acc_v = P.op("vector", lambda e, o=sAcc, i=sPf[:, sl:sl + 256]: e.tensor_copy(o[:], i), "v")
                            else:
                                acc_v = P.op("vector", lambda e, o=sAcc[:, cols],
                                             i=sPf[:, sl + (kbx % 2) * 256 + cols.start: sl + (kbx % 2) * 256 + cols.stop]:
                                             e.tensor_add(o, o, i), "v")
                        if mask_v:
                            P.wait("scalar", "v", mask_v)
                        P.wait("scalar", "s", exp_s[p])
                        if n2 == 2:
                            hdst = sPallh[:, kb * 256:(kb + 2) * 256]
                            ldst = sPalll[:, kb * 256:(kb + 2) * 256]
                            pfs = sPf[:, sl:sl + 512]
                        else:
                            cols = slice(0, 256) if kb < full_end else slice(128, 256)
                            boff = (kb % 2) * 256
                            hdst = sPallh[:, kb * 256 + cols.start: kb * 256 + cols.stop]
                            ldst = sPalll[:, kb * 256 + cols.start: kb * 256 + cols.stop]
                            pfs = sPf[:, sl + boff + cols.start: sl + boff + cols.stop]
                        hi_s[p] = P.op("scalar", lambda e, o=hdst, i=pfs: e.copy(o, i), "s")
                        P.wait("vector", "s", hi_s[p])
                        lo_v[p] = P.op("vector", lambda e, o=ldst, i=pfs, hh=hdst: e.tensor_sub(o, i, hh), "v")
                    emit_av(len(pairs) - 1)
                    st["av_t"] = av_t
                    st["acc_v"] = acc_v

                def stage_av(h, qt=qt, KB=KB):
                    st = S[h]
                    av_t = st["av_t"]
                    P.wait("tensor", "s", S.get(h - 1, {}).get("bev", 0))
                    P.wait("tensor", "v", st["acc_v"])
                    d_t = P.op("tensor", lambda e, o=pD[0:1, 0:256], l=sOC, r=sAcc:
                               e.matmul(o, l[:, :], r[:], start=True, stop=True), "t")
                    st["d_t"] = d_t
                    P.wait("vector", "t", d_t)
                    rec_v = P.op("vector", lambda e, o=sDr, i=pD[0:1, 0:256]: e.reciprocal(o[0:1, :], i), "v")
                    P.wait("tensor", "v", rec_v)
                    b_t = P.op("tensor", lambda e, o=pB[:, 0:256], l=sOR, r=sDr:
                               e.matmul(o, l[:, :], r[0:1, :], start=True, stop=True), "t")
                    P.wait("scalar", "t", b_t)
                    bev = P.op("scalar", lambda e, o=sB, i=pB[:, 0:256]: e.copy(o[:], i), "s")
                    st["bev"] = bev
                    P.wait("vector", "s", bev)
                    P.wait("vector", "t", av_t)
                    P.wait("vector", "v", S.get(h - 1, {}).get("yh_v", 0))
                    ym = P.op("vector", lambda e, o=sYf, i=pY[:, 0:256], b=sB: e.tensor_mul(o[:], i, b[:]), "v")
                    st["ym"] = ym
                    P.wait("vector", "v", ym)
                    yh_v, yh_lo = split_v(sYallh[:, h * 256:(h + 1) * 256], sYalll[:, h * 256:(h + 1) * 256], sYf[:])
                    st["yh_v"] = yh_lo

                for h in range(NH + 1):
                    if h < NH:
                        stage_q(h)
                    if h >= 1:
                        stage_s(h - 1)
                    if h < NH:
                        stage_rot(h)
                    if h >= 1:
                        stage_av(h - 1)

                P.wait("sync", "v", x_free_v)
                P.wait("sync", "s", x_free_s)
                P.wait("sync", "do2", P.val("do2"))
                xda = P.dma("sync", sX[:], xq[(2 * qt) * 128:(2 * qt + 1) * 128, :])
                xdb = P.dma("sync", sX2[:], xq[(2 * qt + 1) * 128:(2 * qt + 2) * 128, :])
                P.fence(engines=("tensor",))
                oadd_v = {0: 0, 1: 0}
                oev = {0: 0, 1: 0}
                tr_free = {0: 0, 1: 0}
                tr_free_t = {0: 0, 1: 0}
                for cch in range(16):
                    par = cch % 2
                    wdi = {}
                    for half in range(2):
                        jj = 2 * cch + half
                        P.wait("sync", "t", wo_t.get(jj - 2, 0))
                        P.dma("sync", sWoh[:, (jj % 2) * 1024:(jj % 2 + 1) * 1024],
                              woh[cch][:, half * 1024:(half + 1) * 1024])
                        wdi[half] = P.dma("sync", sWol[:, (jj % 2) * 1024:(jj % 2 + 1) * 1024],
                                          wol[cch][:, half * 1024:(half + 1) * 1024])
                    pO = pQ if par == 0 else pF
                    P.wait("tensor", "s", oev[par])
                    for h in range(NH):
                        jj = 2 * cch + h // 8
                        if h == 0:
                            P.wait("tensor", "di", wdi[0])
                        if h == 8:
                            P.wait("tensor", "di", wdi[1])
                        base2 = (jj % 2) * 1024 + (h % 8) * 128
                        P.mm3(pO[:, 0:256], sWoh[:, base2: base2 + 128],
                              sWol[:, base2: base2 + 128],
                              sYallh[:, h * 256:(h + 1) * 256], sYalll[:, h * 256:(h + 1) * 256],
                              (h == 0), False)
                    if cch == 0:
                        P.wait("tensor", "v", agg)
                    omm = P.mm3(pO[:, 0:256], sUh[:, cch * 128:(cch + 1) * 128], sUl[:, cch * 128:(cch + 1) * 128],
                                sAGh[:], sAGl[:], False, True, "t")
                    wo_t[2 * cch] = omm
                    wo_t[2 * cch + 1] = omm
                    P.wait("scalar", "t", omm)
                    P.wait("scalar", "t", max(tr_free_t.values()))
                    oev[par] = P.op("scalar", lambda e, o=sOT[:, par * 256:(par + 1) * 256], i=pO[:, 0:256]:
                                    e.copy(o, i), "s")
                    for t2 in range(2):
                        ptr = (pRot if t2 == 0 else pS0)[:, 0:128]
                        P.wait("tensor", "s", oev[par])
                        P.wait("tensor", "v", tr_free[t2])
                        tr_t = P.op("tensor", lambda e, o=ptr,
                                    i=sOT[:, par * 256 + t2 * 128: par * 256 + (t2 + 1) * 128], idm=sID:
                                    e.transpose(o, i, idm[:, :]), "t")
                        tr_free_t[t2] = tr_t
                        P.wait("vector", "t", tr_t)
                        if cch == 0 and t2 == 0:
                            P.wait("vector", "di", xdb)
                        sxt = sX if t2 == 0 else sX2
                        tr_free[t2] = P.op("vector", lambda e, o=sxt[:, cch * 128:(cch + 1) * 128], i=ptr,
                                           x2=sxt[:, cch * 128:(cch + 1) * 128]: e.tensor_add(o, i, x2), "v")
                        oadd_v[t2] = tr_free[t2]
                P.wait("sync", "v", oadd_v[0])
                P.dma("sync", out[(2 * qt) * 128:(2 * qt + 1) * 128, :], sX[:], "do2")
                P.wait("sync", "v", oadd_v[1])
                P.dma("sync", out[(2 * qt + 1) * 128:(2 * qt + 2) * 128, :], sX2[:], "do2")
                x_free_v = None
                x_free_s = 0

            P.wait("gpsimd", "do2", P.val("do2"))

        blk = ctx.enter_context(nc.Block())

        @blk.sync
        def _(e):
            for f in P.stream["sync"]:
                f(e)

        @blk.gpsimd
        def _(e):
            for f in P.stream["gpsimd"]:
                f(e)

        @blk.tensor
        def _(e):
            for f in P.stream["tensor"]:
                f(e)

        @blk.vector
        def _(e):
            for f in P.stream["vector"]:
                f(e)

        @blk.scalar
        def _(e):
            for f in P.stream["scalar"]:
                f(e)

    return nc


def _host_prep(L0_attn_x, Wq, Wk, Wv, Wo, router_w, down_w, up_w):
    f32 = np.float32
    x = np.ascontiguousarray(np.asarray(L0_attn_x, f32))
    Wq = np.asarray(Wq, f32)
    Wk = np.asarray(Wk, f32)
    Wv = np.asarray(Wv, f32)
    Wo = np.asarray(Wo, f32)
    router_w = np.asarray(router_w, f32)
    down_w = np.asarray(down_w, f32)
    up_w = np.asarray(up_w, f32)

    wq_t = np.ascontiguousarray(Wq.reshape(16, 128, 16, 128).transpose(2, 1, 0, 3).reshape(16, 128, C))
    wkv = np.ascontiguousarray(np.concatenate(
        [Wk.reshape(16, 128, 512), Wv.reshape(16, 128, 512)], axis=2))
    wo_t = np.ascontiguousarray(Wo.reshape(16, 128, 16, 128).transpose(2, 1, 0, 3).reshape(16, 128, C))
    dwt = np.ascontiguousarray(down_w.reshape(128, C).T.reshape(16, 128, 128))
    u128 = np.ascontiguousarray(up_w.transpose(0, 2, 1).reshape(128, C))
    rwt = np.ascontiguousarray(router_w.T.reshape(16, 128, 8))

    j = np.arange(64, dtype=np.float64)
    freqs = 1.0 / (10000.0 ** (2.0 * j / 128.0))
    tpos = np.arange(T, dtype=np.float64)
    emb = np.outer(tpos, freqs)
    cosT = np.cos(emb).T
    sinT = np.sin(emb).T
    cost = np.ascontiguousarray(np.concatenate([cosT, cosT], axis=0).astype(f32))
    sint = np.ascontiguousarray(np.concatenate([sinT, sinT], axis=0).astype(f32))

    R = np.zeros((128, 128), f32)
    for dd in range(64):
        R[dd, dd + 64] = -1.0
        R[dd + 64, dd] = 1.0
    rt = np.ascontiguousarray(R.T)

    rep = np.zeros((8, 128), f32)
    for e8 in range(8):
        rep[e8, e8 * 16:(e8 + 1) * 16] = 1.0
    ident = np.eye(128, dtype=f32)
    onesc = np.ones((128, 1), f32)
    onesr = np.ones((1, 128), f32)

    shared = dict(wq=wq_t, wkv=wkv, wo=wo_t, dwt=dwt, u128=u128, rwt=rwt,
                  cost=cost, sint=sint, rt=rt, rep=rep, ident=ident,
                  onesc=onesc, onesr=onesr)

    in_maps = []
    for core in range(8):
        seq, par = core // 2, core % 2
        own_qb = [2 * s + par for s in range(8)]
        xs = np.ascontiguousarray(x[seq])
        xqrows = np.ascontiguousarray(np.concatenate([xs[qb * 128:(qb + 1) * 128] for qb in own_qb], axis=0))
        own_cols = np.concatenate([np.arange(qb * 128, (qb + 1) * 128) for qb in own_qb])
        cosq = np.ascontiguousarray(cost[:, own_cols])
        sinq = np.ascontiguousarray(sint[:, own_cols])
        maskt = np.full((128, 8), 1.0 if par == 1 else 0.0, f32)
        m = dict(shared)
        m.update(x=xs, xq=xqrows, cosq=cosq, sinq=sinq, maskt=maskt)
        in_maps.append(m)
    return x, in_maps


def kernel(L0_attn_x, L0_attn_Wq, L0_attn_Wk, L0_attn_Wv, L0_attn_Wo,
           L0_ffn_router_w, L0_ffn_down_w, L0_ffn_up_w):
    x, in_maps = _host_prep(L0_attn_x, L0_attn_Wq, L0_attn_Wk, L0_attn_Wv, L0_attn_Wo,
                            L0_ffn_router_w, L0_ffn_down_w, L0_ffn_up_w)
    nc = build_nc()
    res = run_bass_kernel_spmd(nc, in_maps, core_ids=list(range(8)))
    B = x.shape[0]
    outp = np.empty((B, T, C), np.float32)
    for core in range(8):
        seq, par = core // 2, core % 2
        o = res.results[core]["out"]
        for s in range(8):
            qb = 2 * s + par
            outp[seq, qb * 128:(qb + 1) * 128, :] = o[s * 128:(s + 1) * 128, :]
    return outp


# version 06
# speedup vs baseline: 1.0618x; 1.0618x over previous
import contextlib
import math
import numpy as np
import concourse.bass as bass
import concourse.mybir as mybir
from concourse.bass_utils import run_bass_kernel_spmd

F32 = mybir.dt.float32
BF16 = mybir.dt.bfloat16
AF = mybir.ActivationFunctionType
ALU = mybir.AluOpType
AX = mybir.AxisListType

T = 2048
C = 2048
NKV = 4
NH = 16
SCALE = 1.0 / math.sqrt(128.0)
RN = 1.0 / float(C)
EPS = 1e-5

ENGS = ("sync", "gpsimd", "tensor", "vector", "scalar")
_ENG_SEM = {"tensor": "t", "vector": "v", "scalar": "s", "gpsimd": "g"}


class Prog:

    def __init__(self, nc, ctx):
        self.nc = nc
        self.ctx = ctx
        self.stream = {e: [] for e in ENGS}
        self.sems = {}
        self.cnt = {}

    def sem(self, name):
        if name not in self.sems:
            self.sems[name] = self.ctx.enter_context(self.nc.semaphore(name))
            self.cnt[name] = 0
        return self.sems[name]

    def wait(self, eng, name, val):
        if val is None or val <= 0:
            return
        s = self.sem(name)
        self.stream[eng].append(lambda e, s=s, v=val: e.wait_ge(s, v))

    def op(self, eng, fn, sem=None, n=1):
        if sem is None:
            self.stream[eng].append(fn)
            return None
        s = self.sem(sem)
        self.cnt[sem] += n
        v = self.cnt[sem]
        self.stream[eng].append(lambda e, fn=fn, s=s, n=n: fn(e).then_inc(s, n))
        return v

    def dma(self, eng, dst, src, sem=None):
        if sem is None:
            sem = "di" if eng == "sync" else "do"
        return self.op(eng, lambda e, d=dst, s=src: e.dma_start(out=d, in_=s), sem, 16)

    def val(self, name):
        return self.cnt.get(name, 0)

    def fence(self, engines=("tensor", "vector", "scalar", "sync", "gpsimd")):
        snap = {k: self.val(k) for k in ("t", "v", "s", "g")}
        for e in engines:
            for k, v in snap.items():
                if _ENG_SEM.get(e) == k:
                    continue
                self.wait(e, k, v)

    def mm3(self, bank, lh, ll, rh, rl, start, stop, sem=None):
        self.op("tensor", lambda e, o=bank, l=lh, r=rh, st=start: e.matmul(o, l, r, start=st, stop=False))
        self.op("tensor", lambda e, o=bank, l=lh, r=rl: e.matmul(o, l, r, start=False, stop=False))
        return self.op("tensor", lambda e, o=bank, l=ll, r=rh, sp=stop: e.matmul(o, l, r, start=False, stop=sp), sem)


def emit_ln(P, sX, sXN, ln, dma_sem, dma_val):
    (sNegSum, sSumSq, sNegM, sMsq, sVar, sStd, sRstd, sEps) = ln
    P.wait("vector", dma_sem, dma_val)
    rs = P.op("vector", lambda e, o=sNegSum, i=sX: e.reduce_sum(o[:, :], i[:], axis=AX.X, negate=True), "v")
    P.wait("scalar", dma_sem, dma_val)
    a = P.op("scalar", lambda e, o=sXN, i=sX, acc=sSumSq: e.activation(o[:], i[:], AF.Square, accum_out=acc[:, :]), "s")
    P.wait("vector", "s", a)
    P.wait("vector", "v", rs)
    b = P.op("vector", lambda e, o=sNegM, i=sNegSum: e.tensor_scalar_mul(o[:, :], i[:, :], RN), "v")
    P.wait("vector", "v", b)
    c = P.op("vector", lambda e, o=sMsq, i=sNegM: e.tensor_mul(o[:, :], i[:, :], i[:, :]), "v")
    P.wait("vector", "v", c)
    d = P.op("vector", lambda e, o=sVar, i=sSumSq, j=sMsq: e.scalar_tensor_tensor(
        o[:, :], i[:, :], RN, j[:, :], ALU.mult, ALU.subtract), "v")
    P.wait("scalar", "v", d)
    ee = P.op("scalar", lambda e, o=sStd, i=sVar, ep=sEps: e.activation(o[:, :], i[:, :], AF.Sqrt, bias=ep[:, 0:1]), "s")
    P.wait("vector", "s", ee)
    f = P.op("vector", lambda e, o=sRstd, i=sStd: e.reciprocal(o[:, :], i[:, :]), "v")
    P.wait("vector", "v", f)
    g = P.op("vector", lambda e, o=sXN, i=sX, m=sNegM, r=sRstd: e.tensor_scalar(
        o[:], i[:], m[:, 0:1], r[:, 0:1], ALU.add, ALU.mult), "v")
    return {"xn_v": g, "sq_s": a}


def build_nc():
    nc = bass.Bass("TRN2", target_bir_lowering=False, debug=False)

    x = nc.dram_tensor("x", [T, C], F32, kind="ExternalInput").ap()
    xq = nc.dram_tensor("xq", [1024, C], F32, kind="ExternalInput").ap()
    wq = nc.dram_tensor("wq", [NH, 128, C], F32, kind="ExternalInput").ap()
    wkv = nc.dram_tensor("wkv", [16, 128, 1024], F32, kind="ExternalInput").ap()
    wo = nc.dram_tensor("wo", [16, 128, C], F32, kind="ExternalInput").ap()
    dwt = nc.dram_tensor("dwt", [16, 128, 128], F32, kind="ExternalInput").ap()
    u128 = nc.dram_tensor("u128", [128, C], F32, kind="ExternalInput").ap()
    rwt = nc.dram_tensor("rwt", [16, 128, 8], F32, kind="ExternalInput").ap()
    cost = nc.dram_tensor("cost", [128, T], F32, kind="ExternalInput").ap()
    sint = nc.dram_tensor("sint", [128, T], F32, kind="ExternalInput").ap()
    cosq = nc.dram_tensor("cosq", [128, 1024], F32, kind="ExternalInput").ap()
    sinq = nc.dram_tensor("sinq", [128, 1024], F32, kind="ExternalInput").ap()
    rt = nc.dram_tensor("rt", [128, 128], F32, kind="ExternalInput").ap()
    rep = nc.dram_tensor("rep", [8, 128], F32, kind="ExternalInput").ap()
    ident = nc.dram_tensor("ident", [128, 128], F32, kind="ExternalInput").ap()
    onesc = nc.dram_tensor("onesc", [128, 1], F32, kind="ExternalInput").ap()
    onesr = nc.dram_tensor("onesr", [1, 128], F32, kind="ExternalInput").ap()
    maskt = nc.dram_tensor("maskt", [128, 8], F32, kind="ExternalInput").ap()
    out = nc.dram_tensor("out", [1024, C], F32, kind="ExternalOutput").ap()
    wqh = nc.dram_tensor("wqh", [NH, 128, C], BF16).ap()
    wql = nc.dram_tensor("wql", [NH, 128, C], BF16).ap()
    woh = nc.dram_tensor("woh", [16, 128, C], BF16).ap()
    wol = nc.dram_tensor("wol", [16, 128, C], BF16).ap()
    wkvh = nc.dram_tensor("wkvh", [16, 128, 1024], BF16).ap()
    wkvl = nc.dram_tensor("wkvl", [16, 128, 1024], BF16).ap()

    with contextlib.ExitStack() as ctx:
        P = Prog(nc, ctx)
        sbf = lambda name, shape, dt=F32: ctx.enter_context(nc.sbuf_tensor(name, shape, dt))

        sKh = sbf("sKh", [128, NKV * T], BF16)
        sKl = sbf("sKl", [128, NKV * T], BF16)
        sVh = sbf("sVh", [128, 16 * 512], BF16)
        sVl = sbf("sVl", [128, 16 * 512], BF16)
        sCosQ = sbf("sCosQ", [128, 1024])
        sSinQ = sbf("sSinQ", [128, 1024])
        sDWh = sbf("sDWh", [128, 16 * 128], BF16)
        sDWl = sbf("sDWl", [128, 16 * 128], BF16)
        sUh = sbf("sUh", [128, C], BF16)
        sUl = sbf("sUl", [128, C], BF16)
        sRW = sbf("sRW", [128, 16 * 8])
        sRT = sbf("sRT", [128, 128])
        sID = sbf("sID", [128, 128])
        sREP = sbf("sREP", [8, 128])
        sOC = sbf("sOC", [128, 1])
        sOR = sbf("sOR", [1, 128])
        sMask = sbf("sMask", [128, 8])
        sX = sbf("sX", [128, C])
        sX2 = sbf("sX2", [128, C])
        sXN = sbf("sXN", [128, C])
        sT1 = sbf("sT1", [128, 256])
        sT2 = sbf("sT2", [128, 256])
        sXf = sbf("sXf", [128, 4 * 128])
        sEps = sbf("sEps", [128, 1])
        ln = tuple(sbf(n, [128, 1]) for n in
                   ("sNegSum", "sSumSq", "sNegM", "sMsq", "sVar", "sStd", "sRstd")) + (sEps,)

        P.op("vector", lambda e, o=sEps: e.memset(o[:, :], EPS), "v")
        for d, s in ((sRT[:], rt), (sID[:], ident), (sREP[:], rep),
                     (sOC[:], onesc), (sOR[:], onesr), (sMask[:], maskt),
                     (sCosQ[:], cosq), (sSinQ[:], sinq)):
            P.dma("sync", d, s)
        for cch in range(16):
            P.dma("sync", sRW[:, cch * 8:(cch + 1) * 8], rwt[cch])

        def split(dst_h, dst_l, src):
            hv = P.op("vector", lambda e, o=dst_h, i=src: e.tensor_copy(o, i), "v")
            P.wait("gpsimd", "v", hv)
            gv = P.op("gpsimd", lambda e, o=dst_l, i=src, hh=dst_h: e.tensor_sub(o, i, hh), "g")
            return hv, gv

        def split_v(dst_h, dst_l, src):
            hv = P.op("vector", lambda e, o=dst_h, i=src: e.tensor_copy(o, i), "v")
            P.wait("vector", "v", hv)
            lv = P.op("vector", lambda e, o=dst_l, i=src, hh=dst_h: e.tensor_sub(o, i, hh), "v")
            return hv, lv

        for cch in range(16):
            P.dma("sync", sXN[:, cch * 128:(cch + 1) * 128], dwt[cch])
        dwd = P.val("di")
        P.wait("vector", "di", dwd)
        dw_h, dw_g = split(sDWh[:], sDWl[:], sXN[:])
        ud = P.dma("sync", sX[:], u128)
        P.wait("vector", "di", ud)
        u_h, u_g = split(sUh[:], sUl[:], sX[:])

        with contextlib.ExitStack() as p1:
            sb1 = lambda name, shape, dt=F32: p1.enter_context(nc.sbuf_tensor(name, shape, dt))
            sXnTh = sb1("sXnTh", [128, 16 * 512], BF16)
            sXnTl = sb1("sXnTl", [128, 16 * 512], BF16)
            sWkvh = sb1("sWkvh", [128, 2 * 1024], BF16)
            sWkvl = sb1("sWkvl", [128, 2 * 1024], BF16)
            sKst = sb1("sKst", [128, 4 * 512])
            sCos = sb1("sCos", [128, T])
            sSin = sb1("sSin", [128, T])
            sT3 = sb1("sT3", [128, 256])
            sXN2 = sb1("sXN2", [128, C])
            ln2 = tuple(sb1(n, [128, 1]) for n in
                        ("sNegSum2", "sSumSq2", "sNegM2", "sMsq2", "sVar2", "sStd2", "sRstd2")) + (ln[-1],)
            sWS = sb1("sWS", [128, 1024])
            sWH = sb1("sWH", [128, 2 * 1024], BF16)
            sWL = sb1("sWL", [128, 2 * 1024], BF16)
            pKb = p1.enter_context(nc.psum_tensor("pKb", [128, 2048], F32))
            pVb = p1.enter_context(nc.psum_tensor("pVb", [128, 2048], F32))

            P.dma("sync", sCos[:], cost)
            P.dma("sync", sSin[:], sint)
            cst_di = P.val("di")
            P.wait("tensor", "di", cst_di)
            P.wait("vector", "di", cst_di)
            P.wait("scalar", "v", max(dw_h, u_h))
            P.wait("scalar", "g", max(dw_g, u_g))
            P.wait("sync", "g", max(dw_g, u_g))

            wsplit_state = {"j": 0, "vg": {}, "dw": {}}
            wsplit_jobs = []
            for cch in range(16):
                wsplit_jobs.append((wkv[cch], 1024, wkvh[cch], wkvl[cch]))
            for hh in range(NH):
                for hf in range(2):
                    wsplit_jobs.append((wq[hh][:, hf * 1024:(hf + 1) * 1024], 1024,
                                        wqh[hh][:, hf * 1024:(hf + 1) * 1024],
                                        wql[hh][:, hf * 1024:(hf + 1) * 1024]))
            for cch in range(16):
                for hf in range(2):
                    wsplit_jobs.append((wo[cch][:, hf * 1024:(hf + 1) * 1024], 1024,
                                        woh[cch][:, hf * 1024:(hf + 1) * 1024],
                                        wol[cch][:, hf * 1024:(hf + 1) * 1024]))

            def emit_wsplit(njobs):
                st = wsplit_state
                for _ in range(njobs):
                    if not wsplit_jobs:
                        return
                    src, width, dsth, dstl = wsplit_jobs.pop(0)
                    j = st["j"]
                    st["j"] += 1
                    par = j % 2
                    P.wait("sync", "v", st["vg"].get(("v", j - 1), 0))
                    P.wait("sync", "g", st["vg"].get(("g", j - 1), 0))
                    dv = P.dma("sync", sWS[:, 0:width], src)
                    P.wait("vector", "di", dv)
                    P.wait("vector", "dw", st["dw"].get(("h", j - 2), 0))
                    hv = P.op("vector", lambda e, o=sWH[:, par * 1024:par * 1024 + width],
                              i=sWS[:, 0:width]: e.tensor_copy(o, i), "v")
                    P.wait("gpsimd", "v", hv)
                    P.wait("gpsimd", "dw", st["dw"].get(("l", j - 2), 0))
                    gv = P.op("gpsimd", lambda e, o=sWL[:, par * 1024:par * 1024 + width],
                              i=sWS[:, 0:width], hh2=sWH[:, par * 1024:par * 1024 + width]:
                              e.tensor_sub(o, i, hh2), "g")
                    st["vg"][("v", j)] = hv
                    st["vg"][("g", j)] = gv
                    P.wait("scalar", "v", hv)
                    st["dw"][("h", j)] = P.dma("scalar", dsth, sWH[:, par * 1024:par * 1024 + width], "dw")
                    P.wait("scalar", "g", gv)
                    st["dw"][("l", j)] = P.dma("scalar", dstl, sWL[:, par * 1024:par * 1024 + width], "dw")
                    st["dw"][j] = st["dw"][("l", j)]

            emit_wsplit(16)
            wkv_dw = {c: wsplit_state["dw"][c] for c in range(16)}

            def wq_dw(chunk):
                return wsplit_state["dw"].get(16 + chunk, None)

            def wo_dw(chunk):
                return wsplit_state["dw"].get(48 + chunk, None)

            def tr_slot(j):
                base = (j % 4) * 512 + 256
                return pKb[:, base:base + 128]

            tr_evac_s = [0] * 8
            prev_evac_s, prev_rope_v, prev_rope_g = 0, 0, 0
            x_free = {0: (0, 0), 1: (0, 0)}
            xn_done = {}
            wkv_t = {}
            kc = 0
            split_done = []

            def emit_p1_ln(tb):
                par = tb % 2
                bx = sX if par == 0 else sX2
                bn = sXN if par == 0 else sXN2
                bl = ln if par == 0 else ln2
                P.wait("sync", "v", x_free[par][0])
                P.wait("sync", "s", x_free[par][1])
                xd = P.dma("sync", bx[:], x[tb * 128:(tb + 1) * 128, :])
                lv = emit_ln(P, bx, bn, bl, "di", xd)
                x_free[par] = (lv["xn_v"], lv["sq_s"])
                xn_done[tb] = lv

            emit_p1_ln(0)
            emit_p1_ln(1)
            for g in range(4):
                for ti in range(4):
                    tb = g * 4 + ti
                    bn = sXN if tb % 2 == 0 else sXN2
                    lv = xn_done[tb]
                    P.wait("tensor", "v", lv["xn_v"])
                    if g > 0 and ti == 0:
                        P.wait("tensor", "s", prev_evac_s)
                    ttr = 0
                    for cch in range(16):
                        j = (ti * 16 + cch) % 4
                        P.wait("tensor", "s", tr_evac_s[j])
                        slot = tr_slot(j)
                        ttr = P.op("tensor", lambda e, o=slot, i=bn[:, cch * 128:(cch + 1) * 128],
                                   idm=sID: e.transpose(o, i, idm[:, :]), "t")
                        P.wait("scalar", "t", ttr)
                        xf = sXf[:, (j % 4) * 128:(j % 4 + 1) * 128]
                        P.wait("scalar", "v", split_done[-4][0] if len(split_done) >= 4 else 0)
                        P.wait("scalar", "g", split_done[-4][1] if len(split_done) >= 4 else 0)
                        tr_evac_s[j] = P.op("scalar", lambda e, o=xf, i=slot: e.copy(o, i), "s")
                        P.wait("vector", "s", tr_evac_s[j])
                        dh = sXnTh[:, cch * 512 + ti * 128: cch * 512 + ti * 128 + 128]
                        dl = sXnTl[:, cch * 512 + ti * 128: cch * 512 + ti * 128 + 128]
                        split_done.append(split(dh, dl, xf))
                    P.wait("scalar", "t", ttr)
                    if tb + 2 < 16:
                        emit_p1_ln(tb + 2)
                xn_split_v = split_done[-1][0]
                xn_split_g = split_done[-1][1]

                for cch in range(16):
                    P.wait("sync", "t", wkv_t.get(kc - 2, 0))
                    P.wait("sync", "dw", wkv_dw[cch])
                    P.dma("sync", sWkvh[:, (kc % 2) * 1024:(kc % 2 + 1) * 1024], wkvh[cch])
                    wv = P.dma("sync", sWkvl[:, (kc % 2) * 1024:(kc % 2 + 1) * 1024], wkvl[cch])
                    P.wait("tensor", "di", wv)
                    if cch == 0:
                        P.wait("tensor", "v", xn_split_v)
                        P.wait("tensor", "g", xn_split_g)
                        P.wait("tensor", "v", prev_rope_v)
                        P.wait("tensor", "g", prev_rope_g)
                    st0, sp0 = (cch == 0), (cch == 15)
                    base = (kc % 2) * 1024
                    lastmm = 0
                    for kvh in range(NKV):
                        lastmm = P.mm3(pKb[:, kvh * 512:(kvh + 1) * 512],
                                       sWkvh[:, base + kvh * 128:base + (kvh + 1) * 128],
                                       sWkvl[:, base + kvh * 128:base + (kvh + 1) * 128],
                                       sXnTh[:, cch * 512:(cch + 1) * 512],
                                       sXnTl[:, cch * 512:(cch + 1) * 512], st0, sp0, "t")
                    for ti in range(4):
                        lastmm = P.mm3(pVb[:, ti * 512:(ti + 1) * 512],
                                       sXnTh[:, cch * 512 + ti * 128:cch * 512 + (ti + 1) * 128],
                                       sXnTl[:, cch * 512 + ti * 128:cch * 512 + (ti + 1) * 128],
                                       sWkvh[:, base + 512:base + 1024],
                                       sWkvl[:, base + 512:base + 1024], st0, sp0, "t")
                    wkv_t[kc] = lastmm
                    kc += 1
                kv_done = wkv_t[kc - 1]

                P.wait("scalar", "t", kv_done)
                P.wait("vector", "t", kv_done)
                ev_s, ev_v = 0, 0
                for ti in range(4):
                    dsth = sVh[:, (g * 4 + ti) * 512:(g * 4 + ti + 1) * 512]
                    dstl = sVl[:, (g * 4 + ti) * 512:(g * 4 + ti + 1) * 512]
                    ev_s = P.op("scalar", lambda e, o=dsth, i=pVb[:, ti * 512:(ti + 1) * 512]: e.copy(o, i), "s")
                    P.wait("vector", "s", ev_s)
                    ev_v = P.op("vector", lambda e, o=dstl, i=pVb[:, ti * 512:(ti + 1) * 512], hh=dsth:
                                e.tensor_sub(o, i, hh), "v")
                evK = [0] * NKV
                for kvh in range(NKV):
                    evK[kvh] = P.op("scalar", lambda e, o=sKst[:, kvh * 512:(kvh + 1) * 512],
                                    i=pKb[:, kvh * 512:(kvh + 1) * 512]: e.copy(o, i), "s")
                prev_evac_s = max(ev_s, max(evK))

                rv, rg = prev_rope_v, prev_rope_g
                for kvh in range(NKV):
                    P.wait("tensor", "s", evK[kvh])
                    if kvh == 0:
                        P.wait("tensor", "v", ev_v)
                    rot_t = P.op("tensor", lambda e, o=pVb[:, kvh * 512:(kvh + 1) * 512], l=sRT,
                                 r=sKst[:, kvh * 512:(kvh + 1) * 512]:
                                 e.matmul(o, l[:, :], r, start=True, stop=True), "t")
                    P.wait("vector", "t", rot_t)
                    for u in range(2):
                        csl = slice(g * 512 + u * 256, g * 512 + (u + 1) * 256)
                        ksl = slice(kvh * 512 + u * 256, kvh * 512 + (u + 1) * 256)
                        P.op("vector", lambda e, o=sT1, i=sKst[:, ksl], c2=sCos[:, csl]: e.tensor_mul(o[:], i, c2))
                        P.op("vector", lambda e, o=sT2, i=pVb[:, ksl], c2=sSin[:, csl]: e.tensor_mul(o[:], i, c2))
                        P.wait("vector", "g", rg)
                        t3v = P.op("vector", lambda e, o=sT3, a=sT1, b=sT2: e.tensor_add(o[:], a[:], b[:]), "v")
                        P.wait("vector", "v", t3v)
                        kdst = slice(kvh * T + g * 512 + u * 256, kvh * T + g * 512 + (u + 1) * 256)
                        rv, rg = split(sKh[:, kdst], sKl[:, kdst], sT3[:])
                prev_rope_v, prev_rope_g = rv, rg

                emit_wsplit(16)

            emit_wsplit(100)
            wsp_done = P.val("dw")

            ph = P.sem("ph")
            for eng in ENGS:
                P.op(eng, lambda e, s=ph: e.nop().then_inc(s, 1))
            for eng in ENGS:
                P.wait(eng, "ph", 5)

        with contextlib.ExitStack() as p2:
            sb2 = lambda name, shape, dt=F32: p2.enter_context(nc.sbuf_tensor(name, shape, dt))
            sXnTqh = sb2("sXnTqh", [128, 16 * 256], BF16)
            sXnTql = sb2("sXnTql", [128, 16 * 256], BF16)
            sYallh = sb2("sYallh", [128, 16 * 256], BF16)
            sYalll = sb2("sYalll", [128, 16 * 256], BF16)
            sPallh = sb2("sPallh", [128, 16 * 256], BF16)
            sPalll = sb2("sPalll", [128, 16 * 256], BF16)
            sWqh = sb2("sWqh", [128, 2 * 1024], BF16)
            sWql = sb2("sWql", [128, 2 * 1024], BF16)
            sWoh = sb2("sWoh", [128, 2 * 1024], BF16)
            sWol = sb2("sWol", [128, 2 * 1024], BF16)
            sQh = sb2("sQh", [128, 2 * 256])
            sQhh = sb2("sQhh", [128, 2 * 256], BF16)
            sQhl = sb2("sQhl", [128, 2 * 256], BF16)
            sPf = sb2("sPf", [128, 4 * 256])
            sYf = sb2("sYf", [128, 256])
            sAcc = sb2("sAcc", [128, 256])
            sB = sb2("sB", [128, 256])
            sOT = sb2("sOT", [128, 2 * 256])
            sDr = sb2("sDr", [1, 256])
            sW8 = sb2("sW8", [128, 16])
            sW2 = sb2("sW2", [128, 16])
            sGmask = sb2("sGmask", [128, 16])
            sGate = sb2("sGate", [128, 16])
            sG8 = sb2("sG8", [8, 256])
            sA = sb2("sA", [128, 256])
            sAGf = sb2("sAGf", [128, 256])
            sAGh = sb2("sAGh", [128, 256], BF16)
            sAGl = sb2("sAGl", [128, 256], BF16)
            sNM8 = sb2("sNM8", [128, 2])
            sDen8 = sb2("sDen8", [128, 2])
            sRec8 = sb2("sRec8", [128, 2])
            sM1 = sb2("sM1", [128, 2])
            sM2 = sb2("sM2", [128, 2])
            pb = [p2.enter_context(nc.psum_tensor("pb%d" % i, [128, 512], F32)) for i in range(8)]
            pQ, pRot, pS0, pS1, pY, pD, pB, pF = pb


            wq_t = {}
            wo_t = {}
            x_free_v, x_free_s = 0, 0

            for qt in range(4):
                KB = 4 * qt + 4
                P.fence()

                tr_evac = [0] * 4
                split_hist = []
                rt_mm = {}
                for t2 in range(2):
                    sidx = 2 * qt + t2
                    if x_free_v is not None:
                        P.wait("sync", "v", x_free_v)
                    P.wait("sync", "s", x_free_s)
                    P.wait("sync", "do2", P.val("do2"))
                    xd = P.dma("sync", sX[:], xq[sidx * 128:(sidx + 1) * 128, :])
                    lv = emit_ln(P, sX, sXN, ln, "di", xd)
                    x_free_v, x_free_s = lv["xn_v"], lv["sq_s"]
                    P.wait("tensor", "v", lv["xn_v"])
                    ttr = 0
                    for cch in range(16):
                        j = (t2 * 16 + cch) % 4
                        slot = pb[j][:, 256:384]
                        P.wait("tensor", "s", tr_evac[j])
                        ttr = P.op("tensor", lambda e, o=slot, i=sXN[:, cch * 128:(cch + 1) * 128],
                                   idm=sID: e.transpose(o, i, idm[:, :]), "t")
                        P.wait("scalar", "t", ttr)
                        xf = sXf[:, (cch % 4) * 128:(cch % 4 + 1) * 128]
                        P.wait("scalar", "v", split_hist[-4][0] if len(split_hist) >= 4 else 0)
                        P.wait("scalar", "g", split_hist[-4][1] if len(split_hist) >= 4 else 0)
                        P.wait("scalar", "t", rt_mm.get((t2, cch - 4), 0))
                        tr_evac[j] = P.op("scalar", lambda e, o=xf, i=slot: e.copy(o, i), "s")
                        P.wait("vector", "s", tr_evac[j])
                        dh = sXnTqh[:, cch * 256 + t2 * 128: cch * 256 + t2 * 128 + 128]
                        dl = sXnTql[:, cch * 256 + t2 * 128: cch * 256 + t2 * 128 + 128]
                        split_hist.append(split(dh, dl, xf))
                        pRt = pD[:, 256:264] if t2 == 0 else pD[:, 384:392]
                        P.wait("tensor", "s", tr_evac[j])
                        rt_mm[(t2, cch)] = P.op("tensor", lambda e, o=pRt, l=xf, r=sRW[:, cch * 8:(cch + 1) * 8],
                                                st=(cch == 0), sp=(cch == 15): e.matmul(o, l, r, start=st, stop=sp), "t")
                    P.wait("scalar", "t", ttr)
                xnq_v = split_hist[-1][0]
                xnq_g = split_hist[-1][1]

                P.wait("tensor", "v", xnq_v)
                P.wait("tensor", "g", xnq_g)
                hmm = 0
                for cch in range(16):
                    hmm = P.mm3(pF[:, 0:256], sDWh[:, cch * 128:(cch + 1) * 128], sDWl[:, cch * 128:(cch + 1) * 128],
                                sXnTqh[:, cch * 256:(cch + 1) * 256], sXnTql[:, cch * 256:(cch + 1) * 256],
                                (cch == 0), (cch == 15), "t")
                P.wait("scalar", "t", hmm)
                silu_s = P.op("scalar", lambda e, o=sA, i=pF[:, 0:256]: e.activation(o[:], i, AF.Silu), "s")
                ag_v = 0
                for t2 in range(2):
                    pRt = pD[:, 256:264] if t2 == 0 else pD[:, 384:392]
                    rmm = max(rt_mm[(0, 15)], rt_mm[(1, 15)])
                    P.wait("vector", "t", rmm)
                    nm = P.op("vector", lambda e, o=sNM8[:, t2:t2 + 1], i=pRt:
                              e.reduce_max(o, i, axis=AX.X, negate=True), "v")
                    P.wait("scalar", "v", nm)
                    ex = P.op("scalar", lambda e, o=sW8[:, t2 * 8:(t2 + 1) * 8], i=pRt, b=sNM8[:, t2:t2 + 1],
                              acc=sDen8[:, t2:t2 + 1]: e.activation(o, i, AF.Exp, bias=b, accum_out=acc), "s")
                    P.wait("vector", "s", ex)
                    rc = P.op("vector", lambda e, o=sRec8[:, t2:t2 + 1], i=sDen8[:, t2:t2 + 1]:
                              e.reciprocal(o, i), "v")
                    P.wait("vector", "v", rc)
                    sm = P.op("vector", lambda e, o=sW8[:, t2 * 8:(t2 + 1) * 8], r=sRec8[:, t2:t2 + 1]:
                              e.tensor_scalar_mul(o, o, r), "v")
                    P.wait("vector", "v", sm)
                    m1 = P.op("vector", lambda e, o=sM1[:, t2:t2 + 1], i=sW8[:, t2 * 8:(t2 + 1) * 8]:
                              e.reduce_max(o, i, axis=AX.X), "v")
                    P.wait("vector", "v", m1)
                    eq = P.op("vector", lambda e, o=sGmask[:, t2 * 8:(t2 + 1) * 8], i=sW8[:, t2 * 8:(t2 + 1) * 8],
                              m=sM1[:, t2:t2 + 1]: e.tensor_single_scalar(o, i, m, ALU.is_ge), "v")
                    P.wait("vector", "v", eq)
                    w2 = P.op("vector", lambda e, o=sW2[:, t2 * 8:(t2 + 1) * 8], i=sGmask[:, t2 * 8:(t2 + 1) * 8],
                              w=sW8[:, t2 * 8:(t2 + 1) * 8]: e.scalar_tensor_tensor(o, i, -1e30, w, ALU.mult, ALU.add), "v")
                    P.wait("vector", "v", w2)
                    m2 = P.op("vector", lambda e, o=sM2[:, t2:t2 + 1], i=sW2[:, t2 * 8:(t2 + 1) * 8]:
                              e.reduce_max(o, i, axis=AX.X), "v")
                    P.wait("vector", "v", m2)
                    gm = P.op("vector", lambda e, o=sGmask[:, t2 * 8:(t2 + 1) * 8], i=sW8[:, t2 * 8:(t2 + 1) * 8],
                              m=sM2[:, t2:t2 + 1]: e.tensor_single_scalar(o, i, m, ALU.is_ge), "v")
                    P.wait("vector", "v", gm)
                    gv = P.op("vector", lambda e, o=sGate[:, t2 * 8:(t2 + 1) * 8], i=sW8[:, t2 * 8:(t2 + 1) * 8],
                              m=sGmask[:, t2 * 8:(t2 + 1) * 8]: e.tensor_mul(o, i, m), "v")
                    P.wait("tensor", "v", gv)
                    gt_t = P.op("tensor", lambda e, o=pB[0:8, 384:512], i=sGate[:, t2 * 8:(t2 + 1) * 8], idm=sID:
                                e.transpose(o, i, idm[:, :]), "t")
                    P.wait("scalar", "t", gt_t)
                    g8 = P.op("scalar", lambda e, o=sG8[0:8, t2 * 128:(t2 + 1) * 128], i=pB[0:8, 384:512]:
                              e.copy(o, i), "s")
                    P.wait("tensor", "s", g8)
                    if t2 == 1:
                        P.wait("tensor", "v", ag_v)
                    grep_t = P.op("tensor", lambda e, o=pB[:, 256:384], l=sREP, r=sG8[0:8, t2 * 128:(t2 + 1) * 128]:
                                  e.matmul(o, l[:, :], r, start=True, stop=True), "t")
                    P.wait("vector", "t", grep_t)
                    P.wait("vector", "s", silu_s)
                    ag_v = P.op("vector", lambda e, o=sAGf[:, t2 * 128:(t2 + 1) * 128], i=sA[:, t2 * 128:(t2 + 1) * 128],
                                g2=pB[:, 256:384]: e.tensor_mul(o, i, g2), "v")
                P.wait("vector", "v", ag_v)
                agh, agg = split_v(sAGh[:], sAGl[:], sAGf[:])

                S = {}

                def stage_q(h, qt=qt):
                    par = h % 2
                    st = {"par": par}
                    wdi = {}
                    for half in range(2):
                        jj = 2 * h + half
                        P.wait("sync", "t", wq_t.get(jj - 2, 0))
                        P.wait("sync", "dw", wq_dw(jj))
                        P.dma("sync", sWqh[:, (jj % 2) * 1024:(jj % 2 + 1) * 1024],
                              wqh[h][:, half * 1024:(half + 1) * 1024])
                        wdi[half] = P.dma("sync", sWql[:, (jj % 2) * 1024:(jj % 2 + 1) * 1024],
                                          wql[h][:, half * 1024:(half + 1) * 1024])
                    if h == 0:
                        P.wait("tensor", "v", xnq_v)
                        P.wait("tensor", "g", xnq_g)
                    if h >= 1:
                        P.wait("tensor", "s", S[h - 1]["qev"])
                    qmm = 0
                    for cch in range(16):
                        jj = 2 * h + cch // 8
                        if cch == 0:
                            P.wait("tensor", "di", wdi[0])
                        if cch == 8:
                            P.wait("tensor", "di", wdi[1])
                        base2 = (jj % 2) * 1024 + (cch % 8) * 128
                        qmm = P.mm3(pQ[:, 0:256], sWqh[:, base2: base2 + 128],
                                    sWql[:, base2: base2 + 128],
                                    sXnTqh[:, cch * 256:(cch + 1) * 256], sXnTql[:, cch * 256:(cch + 1) * 256],
                                    (cch == 0), (cch == 15), "t" if cch == 15 else None)
                    wq_t[2 * h] = qmm
                    wq_t[2 * h + 1] = qmm
                    st["qmm"] = qmm
                    P.wait("scalar", "t", qmm)
                    st["qev"] = P.op("scalar", lambda e, o=sQh[:, par * 256:(par + 1) * 256], i=pQ[:, 0:256]:
                                     e.copy(o, i), "s")
                    S[h] = st

                def stage_rot(h, qt=qt):
                    st = S[h]
                    par = st["par"]
                    P.wait("tensor", "s", st["qev"])
                    rot_t = P.op("tensor", lambda e, o=pRot[:, 0:256], l=sRT, r=sQh[:, par * 256:(par + 1) * 256]:
                                 e.matmul(o, l[:, :], r, start=True, stop=True), "t")
                    P.wait("vector", "t", rot_t)
                    for t2 in range(2):
                        qsl = slice((2 * qt + t2) * 128, (2 * qt + t2 + 1) * 128)
                        dsl = slice(t2 * 128, (t2 + 1) * 128)
                        P.op("vector", lambda e, o=sT1[:, dsl],
                             i=sQh[:, par * 256 + t2 * 128: par * 256 + (t2 + 1) * 128],
                             c2=sCosQ[:, qsl]: e.tensor_mul(o, i, c2))
                        P.op("vector", lambda e, o=sT2[:, dsl], i=pRot[:, dsl], c2=sSinQ[:, qsl]:
                             e.tensor_mul(o, i, c2))
                    rope_v = P.op("vector", lambda e, o=sQh[:, par * 256:(par + 1) * 256], a=sT1, b=sT2:
                                  e.tensor_add(o, a[:], b[:]), "v")
                    P.wait("vector", "v", rope_v)
                    st["rope_v"] = rope_v
                    qs_v, qs_lo = split_v(sQhh[:, par * 256:(par + 1) * 256], sQhl[:, par * 256:(par + 1) * 256],
                                          sQh[:, par * 256:(par + 1) * 256])
                    st["qs_v"] = qs_lo

                def stage_s(h, qt=qt, KB=KB):
                    st = S[h]
                    par = st["par"]
                    kvh = h // 4
                    full_end = 4 * qt + 2
                    exp_s = {}
                    hi_s = {}
                    lo_v = {}
                    acc_v = 0
                    mask_v = 0
                    av_t = 0
                    pairs = []
                    kb = 0
                    while kb < KB:
                        paired = (kb % 2 == 0) and (kb + 1 < full_end)
                        pairs.append((kb, 2 if paired else 1))
                        kb += 2 if paired else 1

                    pYb = pY if h % 2 == 0 else pF

                    def emit_av(pi):
                        nonlocal av_t
                        kb0, n2 = pairs[pi]
                        p0 = kb0 // 2
                        P.wait("tensor", "s", hi_s[p0])
                        P.wait("tensor", "v", lo_v[p0])
                        if pi == 0:
                            P.wait("tensor", "v", S.get(h - 2, {}).get("ym", 0))
                            if h == 1:
                                P.wait("tensor", "s", silu_s)
                        for kbx in range(kb0, kb0 + n2):
                            cols = slice(0, 256) if kbx < full_end else slice(128, 256)
                            vsl = slice(kbx * 512 + kvh * 128, kbx * 512 + (kvh + 1) * 128)
                            av_t = P.mm3(pYb[:, cols], sVh[:, vsl], sVl[:, vsl],
                                         sPallh[:, kbx * 256 + cols.start: kbx * 256 + cols.stop],
                                         sPalll[:, kbx * 256 + cols.start: kbx * 256 + cols.stop],
                                         (kbx == 0), (kbx == KB - 1),
                                         "t" if kbx == KB - 1 else None)

                    for pi, (kb, n2) in enumerate(pairs):
                        p = kb // 2
                        bank = pS0 if (p % 2) == 0 else pS1
                        if kb == 0:
                            P.wait("tensor", "v", st["qs_v"])
                            P.wait("scalar", "t", S.get(h - 1, {}).get("av_t", 0))
                        if p >= 2:
                            P.wait("tensor", "s", exp_s.get(p - 2, 0))
                        smm = 0
                        for k2 in range(n2):
                            kbx = kb + k2
                            cols = slice(0, 256) if kbx < full_end else slice(128, 256)
                            boff = (kbx % 2) * 256
                            ksl = slice(kvh * T + kbx * 128, kvh * T + (kbx + 1) * 128)
                            qsl2 = slice(par * 256 + cols.start, par * 256 + cols.stop)
                            smm = P.mm3(bank[:, boff + cols.start: boff + cols.stop], sKh[:, ksl], sKl[:, ksl],
                                        sQhh[:, qsl2], sQhl[:, qsl2], True, True, "t")
                        if pi >= 1:
                            emit_av(pi - 1)
                        P.wait("scalar", "t", smm)
                        P.wait("scalar", "v", lo_v.get(p - 2, 0))
                        sl = (p % 2) * 512
                        if n2 == 2:
                            psrc = bank[:, 0:512]
                            pfdst = sPf[:, sl:sl + 512]
                        else:
                            cols = slice(0, 256) if kb < full_end else slice(128, 256)
                            boff = (kb % 2) * 256
                            psrc = bank[:, boff + cols.start: boff + cols.stop]
                            pfdst = sPf[:, sl + boff + cols.start: sl + boff + cols.stop]
                        exp_s[p] = P.op("scalar", lambda e, o=pfdst, i=psrc:
                                        e.activation(o, i, AF.Exp, scale=SCALE), "s")
                        P.wait("vector", "s", exp_s[p])
                        for kbx in range(kb, kb + n2):
                            if kbx == 4 * qt + 1:
                                mask_v = P.op("vector", lambda e, o=sPf[:, sl + (kbx % 2) * 256: sl + (kbx % 2) * 256 + 128],
                                              m=sMask[:, 2 * qt:2 * qt + 1]: e.tensor_scalar_mul(o, o, m), "v")
                            if kbx == 4 * qt + 3:
                                mask_v = P.op("vector", lambda e, o=sPf[:, sl + (kbx % 2) * 256 + 128: sl + (kbx % 2) * 256 + 256],
                                              m=sMask[:, 2 * qt + 1:2 * qt + 2]: e.tensor_scalar_mul(o, o, m), "v")
                        for kbx in range(kb, kb + n2):
                            cols = slice(0, 256) if kbx < full_end else slice(128, 256)
                            if kbx == 0:
                                P.wait("vector", "t", S.get(h - 1, {}).get("d_t", 0))
                                acc_v = P.op("vector", lambda e, o=sAcc, i=sPf[:, sl:sl + 256]: e.tensor_copy(o[:], i), "v")
                            else:
                                acc_v = P.op("vector", lambda e, o=sAcc[:, cols],
                                             i=sPf[:, sl + (kbx % 2) * 256 + cols.start: sl + (kbx % 2) * 256 + cols.stop]:
                                             e.tensor_add(o, o, i), "v")
                        if mask_v:
                            P.wait("scalar", "v", mask_v)
                        P.wait("scalar", "s", exp_s[p])
                        if n2 == 2:
                            hdst = sPallh[:, kb * 256:(kb + 2) * 256]
                            ldst = sPalll[:, kb * 256:(kb + 2) * 256]
                            pfs = sPf[:, sl:sl + 512]
                        else:
                            cols = slice(0, 256) if kb < full_end else slice(128, 256)
                            boff = (kb % 2) * 256
                            hdst = sPallh[:, kb * 256 + cols.start: kb * 256 + cols.stop]
                            ldst = sPalll[:, kb * 256 + cols.start: kb * 256 + cols.stop]
                            pfs = sPf[:, sl + boff + cols.start: sl + boff + cols.stop]
                        hi_s[p] = P.op("scalar", lambda e, o=hdst, i=pfs: e.copy(o, i), "s")
                        P.wait("vector", "s", hi_s[p])
                        lo_v[p] = P.op("vector", lambda e, o=ldst, i=pfs, hh=hdst: e.tensor_sub(o, i, hh), "v")
                    emit_av(len(pairs) - 1)
                    st["av_t"] = av_t
                    st["acc_v"] = acc_v

                def stage_d(h, qt=qt, KB=KB):
                    st = S[h]
                    P.wait("tensor", "v", st["acc_v"])
                    P.wait("tensor", "v", S.get(h - 1, {}).get("rec_v", 0))
                    d_t = P.op("tensor", lambda e, o=pD[0:1, 0:256], l=sOC, r=sAcc:
                               e.matmul(o, l[:, :], r[:], start=True, stop=True), "t")
                    st["d_t"] = d_t
                    P.wait("vector", "t", d_t)
                    st["rec_v"] = P.op("vector", lambda e, o=sDr, i=pD[0:1, 0:256]:
                                       e.reciprocal(o[0:1, :], i), "v")

                def stage_fin(h, qt=qt, KB=KB):
                    st = S[h]
                    pYb = pY if h % 2 == 0 else pF
                    P.wait("tensor", "s", S.get(h - 1, {}).get("bev", 0))
                    P.wait("tensor", "v", st["rec_v"])
                    b_t = P.op("tensor", lambda e, o=pB[:, 0:256], l=sOR, r=sDr:
                               e.matmul(o, l[:, :], r[0:1, :], start=True, stop=True), "t")
                    P.wait("scalar", "t", b_t)
                    bev = P.op("scalar", lambda e, o=sB, i=pB[:, 0:256]: e.copy(o[:], i), "s")
                    st["bev"] = bev
                    P.wait("vector", "s", bev)
                    P.wait("vector", "t", st["av_t"])
                    P.wait("vector", "v", S.get(h - 1, {}).get("yh_v", 0))
                    ym = P.op("vector", lambda e, o=sYf, i=pYb[:, 0:256], b=sB: e.tensor_mul(o[:], i, b[:]), "v")
                    st["ym"] = ym
                    P.wait("vector", "v", ym)
                    yh_v, yh_lo = split_v(sYallh[:, h * 256:(h + 1) * 256], sYalll[:, h * 256:(h + 1) * 256], sYf[:])
                    st["yh_v"] = yh_lo

                for h in range(NH + 1):
                    if h < NH:
                        stage_q(h)
                    if h >= 1:
                        stage_s(h - 1)
                    if h >= 2:
                        stage_fin(h - 2)
                    if h < NH:
                        stage_rot(h)
                    if h >= 1:
                        stage_d(h - 1)
                stage_fin(NH - 1)

                P.wait("sync", "v", x_free_v)
                P.wait("sync", "s", x_free_s)
                P.wait("sync", "do2", P.val("do2"))
                xda = P.dma("sync", sX[:], xq[(2 * qt) * 128:(2 * qt + 1) * 128, :])
                xdb = P.dma("sync", sX2[:], xq[(2 * qt + 1) * 128:(2 * qt + 2) * 128, :])
                P.fence(engines=("tensor",))
                oadd_v = {0: 0, 1: 0}
                oev = {0: 0, 1: 0}
                tr_free = {0: 0, 1: 0}
                tr_free_t = {0: 0, 1: 0}
                for cch in range(16):
                    par = cch % 2
                    wdi = {}
                    for half in range(2):
                        jj = 2 * cch + half
                        P.wait("sync", "t", wo_t.get(jj - 2, 0))
                        P.wait("sync", "dw", wo_dw(jj))
                        P.dma("sync", sWoh[:, (jj % 2) * 1024:(jj % 2 + 1) * 1024],
                              woh[cch][:, half * 1024:(half + 1) * 1024])
                        wdi[half] = P.dma("sync", sWol[:, (jj % 2) * 1024:(jj % 2 + 1) * 1024],
                                          wol[cch][:, half * 1024:(half + 1) * 1024])
                    pO = pQ if par == 0 else pF
                    P.wait("tensor", "s", oev[par])
                    for h in range(NH):
                        jj = 2 * cch + h // 8
                        if h == 0:
                            P.wait("tensor", "di", wdi[0])
                        if h == 8:
                            P.wait("tensor", "di", wdi[1])
                        base2 = (jj % 2) * 1024 + (h % 8) * 128
                        P.mm3(pO[:, 0:256], sWoh[:, base2: base2 + 128],
                              sWol[:, base2: base2 + 128],
                              sYallh[:, h * 256:(h + 1) * 256], sYalll[:, h * 256:(h + 1) * 256],
                              (h == 0), False)
                    if cch == 0:
                        P.wait("tensor", "v", agg)
                    omm = P.mm3(pO[:, 0:256], sUh[:, cch * 128:(cch + 1) * 128], sUl[:, cch * 128:(cch + 1) * 128],
                                sAGh[:], sAGl[:], False, True, "t")
                    wo_t[2 * cch] = omm
                    wo_t[2 * cch + 1] = omm
                    P.wait("scalar", "t", omm)
                    P.wait("scalar", "t", max(tr_free_t.values()))
                    oev[par] = P.op("scalar", lambda e, o=sOT[:, par * 256:(par + 1) * 256], i=pO[:, 0:256]:
                                    e.copy(o, i), "s")
                    for t2 in range(2):
                        ptr = (pRot if t2 == 0 else pS0)[:, 0:128]
                        P.wait("tensor", "s", oev[par])
                        P.wait("tensor", "v", tr_free[t2])
                        tr_t = P.op("tensor", lambda e, o=ptr,
                                    i=sOT[:, par * 256 + t2 * 128: par * 256 + (t2 + 1) * 128], idm=sID:
                                    e.transpose(o, i, idm[:, :]), "t")
                        tr_free_t[t2] = tr_t
                        P.wait("vector", "t", tr_t)
                        if cch == 0 and t2 == 0:
                            P.wait("vector", "di", xdb)
                        sxt = sX if t2 == 0 else sX2
                        tr_free[t2] = P.op("vector", lambda e, o=sxt[:, cch * 128:(cch + 1) * 128], i=ptr,
                                           x2=sxt[:, cch * 128:(cch + 1) * 128]: e.tensor_add(o, i, x2), "v")
                        oadd_v[t2] = tr_free[t2]
                P.wait("sync", "v", oadd_v[0])
                P.dma("sync", out[(2 * qt) * 128:(2 * qt + 1) * 128, :], sX[:], "do2")
                P.wait("sync", "v", oadd_v[1])
                P.dma("sync", out[(2 * qt + 1) * 128:(2 * qt + 2) * 128, :], sX2[:], "do2")
                x_free_v = None
                x_free_s = 0

            P.wait("gpsimd", "do2", P.val("do2"))

        blk = ctx.enter_context(nc.Block())

        @blk.sync
        def _(e):
            for f in P.stream["sync"]:
                f(e)

        @blk.gpsimd
        def _(e):
            for f in P.stream["gpsimd"]:
                f(e)

        @blk.tensor
        def _(e):
            for f in P.stream["tensor"]:
                f(e)

        @blk.vector
        def _(e):
            for f in P.stream["vector"]:
                f(e)

        @blk.scalar
        def _(e):
            for f in P.stream["scalar"]:
                f(e)

    return nc


def _host_prep(L0_attn_x, Wq, Wk, Wv, Wo, router_w, down_w, up_w):
    f32 = np.float32
    x = np.ascontiguousarray(np.asarray(L0_attn_x, f32))
    Wq = np.asarray(Wq, f32)
    Wk = np.asarray(Wk, f32)
    Wv = np.asarray(Wv, f32)
    Wo = np.asarray(Wo, f32)
    router_w = np.asarray(router_w, f32)
    down_w = np.asarray(down_w, f32)
    up_w = np.asarray(up_w, f32)

    wq_t = np.ascontiguousarray(Wq.reshape(16, 128, 16, 128).transpose(2, 1, 0, 3).reshape(16, 128, C))
    wkv = np.ascontiguousarray(np.concatenate(
        [Wk.reshape(16, 128, 512), Wv.reshape(16, 128, 512)], axis=2))
    wo_t = np.ascontiguousarray(Wo.reshape(16, 128, 16, 128).transpose(2, 1, 0, 3).reshape(16, 128, C))
    dwt = np.ascontiguousarray(down_w.reshape(128, C).T.reshape(16, 128, 128))
    u128 = np.ascontiguousarray(up_w.transpose(0, 2, 1).reshape(128, C))
    rwt = np.ascontiguousarray(router_w.T.reshape(16, 128, 8))

    j = np.arange(64, dtype=np.float64)
    freqs = 1.0 / (10000.0 ** (2.0 * j / 128.0))
    tpos = np.arange(T, dtype=np.float64)
    emb = np.outer(tpos, freqs)
    cosT = np.cos(emb).T
    sinT = np.sin(emb).T
    cost = np.ascontiguousarray(np.concatenate([cosT, cosT], axis=0).astype(f32))
    sint = np.ascontiguousarray(np.concatenate([sinT, sinT], axis=0).astype(f32))

    R = np.zeros((128, 128), f32)
    for dd in range(64):
        R[dd, dd + 64] = -1.0
        R[dd + 64, dd] = 1.0
    rt = np.ascontiguousarray(R.T)

    rep = np.zeros((8, 128), f32)
    for e8 in range(8):
        rep[e8, e8 * 16:(e8 + 1) * 16] = 1.0
    ident = np.eye(128, dtype=f32)
    onesc = np.ones((128, 1), f32)
    onesr = np.ones((1, 128), f32)

    shared = dict(wq=wq_t, wkv=wkv, wo=wo_t, dwt=dwt, u128=u128, rwt=rwt,
                  cost=cost, sint=sint, rt=rt, rep=rep, ident=ident,
                  onesc=onesc, onesr=onesr)

    in_maps = []
    for core in range(8):
        seq, par = core // 2, core % 2
        own_qb = [2 * s + par for s in range(8)]
        xs = np.ascontiguousarray(x[seq])
        xqrows = np.ascontiguousarray(np.concatenate([xs[qb * 128:(qb + 1) * 128] for qb in own_qb], axis=0))
        own_cols = np.concatenate([np.arange(qb * 128, (qb + 1) * 128) for qb in own_qb])
        cosq = np.ascontiguousarray(cost[:, own_cols])
        sinq = np.ascontiguousarray(sint[:, own_cols])
        maskt = np.full((128, 8), 1.0 if par == 1 else 0.0, f32)
        m = dict(shared)
        m.update(x=xs, xq=xqrows, cosq=cosq, sinq=sinq, maskt=maskt)
        in_maps.append(m)
    return x, in_maps


def kernel(L0_attn_x, L0_attn_Wq, L0_attn_Wk, L0_attn_Wv, L0_attn_Wo,
           L0_ffn_router_w, L0_ffn_down_w, L0_ffn_up_w):
    x, in_maps = _host_prep(L0_attn_x, L0_attn_Wq, L0_attn_Wk, L0_attn_Wv, L0_attn_Wo,
                            L0_ffn_router_w, L0_ffn_down_w, L0_ffn_up_w)
    nc = build_nc()
    res = run_bass_kernel_spmd(nc, in_maps, core_ids=list(range(8)))
    B = x.shape[0]
    outp = np.empty((B, T, C), np.float32)
    for core in range(8):
        seq, par = core // 2, core % 2
        o = res.results[core]["out"]
        for s in range(8):
            qb = 2 * s + par
            outp[seq, qb * 128:(qb + 1) * 128, :] = o[s * 128:(s + 1) * 128, :]
    return outp


# version 07
# speedup vs baseline: 1.1811x; 1.1124x over previous
import contextlib
import math
import numpy as np
import concourse.bass as bass
import concourse.mybir as mybir
from concourse.bass_utils import run_bass_kernel_spmd

F32 = mybir.dt.float32
BF16 = mybir.dt.bfloat16
AF = mybir.ActivationFunctionType
ALU = mybir.AluOpType
AX = mybir.AxisListType

T = 2048
C = 2048
NKV = 4
NH = 16
SCALE = 1.0 / math.sqrt(128.0)
RN = 1.0 / float(C)
EPS = 1e-5

ENGS = ("sync", "gpsimd", "tensor", "vector", "scalar")
_ENG_SEM = {"tensor": "t", "vector": "v", "scalar": "s", "gpsimd": "g"}


class Prog:

    def __init__(self, nc, ctx):
        self.nc = nc
        self.ctx = ctx
        self.stream = {e: [] for e in ENGS}
        self.sems = {}
        self.cnt = {}

    def sem(self, name):
        if name not in self.sems:
            self.sems[name] = self.ctx.enter_context(self.nc.semaphore(name))
            self.cnt[name] = 0
        return self.sems[name]

    def wait(self, eng, name, val):
        if val is None or val <= 0:
            return
        s = self.sem(name)
        self.stream[eng].append(lambda e, s=s, v=val: e.wait_ge(s, v))

    def op(self, eng, fn, sem=None, n=1):
        if sem is None:
            self.stream[eng].append(fn)
            return None
        s = self.sem(sem)
        self.cnt[sem] += n
        v = self.cnt[sem]
        self.stream[eng].append(lambda e, fn=fn, s=s, n=n: fn(e).then_inc(s, n))
        return v

    def dma(self, eng, dst, src, sem=None):
        if sem is None:
            sem = "di" if eng == "sync" else "do"
        return self.op(eng, lambda e, d=dst, s=src: e.dma_start(out=d, in_=s), sem, 16)

    def val(self, name):
        return self.cnt.get(name, 0)

    def fence(self, engines=("tensor", "vector", "scalar", "sync", "gpsimd")):
        snap = {k: self.val(k) for k in ("t", "v", "s", "g")}
        for e in engines:
            for k, v in snap.items():
                if _ENG_SEM.get(e) == k:
                    continue
                self.wait(e, k, v)

    def mm3(self, bank, lh, ll, rh, rl, start, stop, sem=None):
        self.op("tensor", lambda e, o=bank, l=lh, r=rh, st=start: e.matmul(o, l, r, start=st, stop=False))
        self.op("tensor", lambda e, o=bank, l=lh, r=rl: e.matmul(o, l, r, start=False, stop=False))
        return self.op("tensor", lambda e, o=bank, l=ll, r=rh, sp=stop: e.matmul(o, l, r, start=False, stop=sp), sem)


def emit_ln(P, sX, sXN, ln, dma_sem, dma_val):
    (sNegSum, sSumSq, sNegM, sMsq, sVar, sStd, sRstd, sEps) = ln
    P.wait("vector", dma_sem, dma_val)
    rs = P.op("vector", lambda e, o=sNegSum, i=sX: e.reduce_sum(o[:, :], i[:], axis=AX.X, negate=True), "v")
    P.wait("scalar", dma_sem, dma_val)
    a = P.op("scalar", lambda e, o=sXN, i=sX, acc=sSumSq: e.activation(o[:], i[:], AF.Square, accum_out=acc[:, :]), "s")
    P.wait("vector", "s", a)
    P.wait("vector", "v", rs)
    b = P.op("vector", lambda e, o=sNegM, i=sNegSum: e.tensor_scalar_mul(o[:, :], i[:, :], RN), "v")
    P.wait("vector", "v", b)
    c = P.op("vector", lambda e, o=sMsq, i=sNegM: e.tensor_mul(o[:, :], i[:, :], i[:, :]), "v")
    P.wait("vector", "v", c)
    d = P.op("vector", lambda e, o=sVar, i=sSumSq, j=sMsq: e.scalar_tensor_tensor(
        o[:, :], i[:, :], RN, j[:, :], ALU.mult, ALU.subtract), "v")
    P.wait("scalar", "v", d)
    ee = P.op("scalar", lambda e, o=sStd, i=sVar, ep=sEps: e.activation(o[:, :], i[:, :], AF.Sqrt, bias=ep[:, 0:1]), "s")
    P.wait("vector", "s", ee)
    f = P.op("vector", lambda e, o=sRstd, i=sStd: e.reciprocal(o[:, :], i[:, :]), "v")
    P.wait("vector", "v", f)
    g = P.op("vector", lambda e, o=sXN, i=sX, m=sNegM, r=sRstd: e.tensor_scalar(
        o[:], i[:], m[:, 0:1], r[:, 0:1], ALU.add, ALU.mult), "v")
    return {"xn_v": g, "sq_s": a}


def build_nc():
    nc = bass.Bass("TRN2", target_bir_lowering=False, debug=False)

    x = nc.dram_tensor("x", [T, C], F32, kind="ExternalInput").ap()
    xq = nc.dram_tensor("xq", [1024, C], F32, kind="ExternalInput").ap()
    wq = nc.dram_tensor("wq", [NH, 128, C], F32, kind="ExternalInput").ap()
    wkv = nc.dram_tensor("wkv", [16, 128, 1024], F32, kind="ExternalInput").ap()
    wo = nc.dram_tensor("wo", [16, 128, C], F32, kind="ExternalInput").ap()
    dwt = nc.dram_tensor("dwt", [16, 128, 128], F32, kind="ExternalInput").ap()
    u128 = nc.dram_tensor("u128", [128, C], F32, kind="ExternalInput").ap()
    rwt = nc.dram_tensor("rwt", [16, 128, 8], F32, kind="ExternalInput").ap()
    cost = nc.dram_tensor("cost", [128, T], F32, kind="ExternalInput").ap()
    sint = nc.dram_tensor("sint", [128, T], F32, kind="ExternalInput").ap()
    cosq = nc.dram_tensor("cosq", [128, 1024], F32, kind="ExternalInput").ap()
    sinq = nc.dram_tensor("sinq", [128, 1024], F32, kind="ExternalInput").ap()
    rt = nc.dram_tensor("rt", [128, 128], F32, kind="ExternalInput").ap()
    rep = nc.dram_tensor("rep", [8, 128], F32, kind="ExternalInput").ap()
    ident = nc.dram_tensor("ident", [128, 128], F32, kind="ExternalInput").ap()
    onesc = nc.dram_tensor("onesc", [128, 1], F32, kind="ExternalInput").ap()
    onesr = nc.dram_tensor("onesr", [1, 128], F32, kind="ExternalInput").ap()
    maskt = nc.dram_tensor("maskt", [128, 8], F32, kind="ExternalInput").ap()
    out = nc.dram_tensor("out", [1024, C], F32, kind="ExternalOutput").ap()
    wqh = nc.dram_tensor("wqh", [NH, 128, C], BF16).ap()
    wql = nc.dram_tensor("wql", [NH, 128, C], BF16).ap()
    woh = nc.dram_tensor("woh", [16, 128, C], BF16).ap()
    wol = nc.dram_tensor("wol", [16, 128, C], BF16).ap()
    wkvh = nc.dram_tensor("wkvh", [16, 128, 1024], BF16).ap()
    wkvl = nc.dram_tensor("wkvl", [16, 128, 1024], BF16).ap()

    with contextlib.ExitStack() as ctx:
        P = Prog(nc, ctx)
        sbf = lambda name, shape, dt=F32: ctx.enter_context(nc.sbuf_tensor(name, shape, dt))

        sKh = sbf("sKh", [128, NKV * T], BF16)
        sKl = sbf("sKl", [128, NKV * T], BF16)
        sVh = sbf("sVh", [128, 16 * 512], BF16)
        sVl = sbf("sVl", [128, 16 * 512], BF16)
        sCosQ = sbf("sCosQ", [128, 1024])
        sSinQ = sbf("sSinQ", [128, 1024])
        sDWh = sbf("sDWh", [128, 16 * 128], BF16)
        sDWl = sbf("sDWl", [128, 16 * 128], BF16)
        sUh = sbf("sUh", [128, C], BF16)
        sUl = sbf("sUl", [128, C], BF16)
        sRW = sbf("sRW", [128, 16 * 8])
        sRT = sbf("sRT", [128, 128])
        sID = sbf("sID", [128, 128])
        sREP = sbf("sREP", [8, 128])
        sOC = sbf("sOC", [128, 1])
        sOR = sbf("sOR", [1, 128])
        sMask = sbf("sMask", [128, 8])
        sX = sbf("sX", [128, C])
        sX2 = sbf("sX2", [128, C])
        sXN = sbf("sXN", [128, C])
        sT1 = sbf("sT1", [128, 256])
        sT2 = sbf("sT2", [128, 256])
        sXf = sbf("sXf", [128, 4 * 128])
        sEps = sbf("sEps", [128, 1])
        ln = tuple(sbf(n, [128, 1]) for n in
                   ("sNegSum", "sSumSq", "sNegM", "sMsq", "sVar", "sStd", "sRstd")) + (sEps,)

        P.op("vector", lambda e, o=sEps: e.memset(o[:, :], EPS), "v")
        for d, s in ((sRT[:], rt), (sID[:], ident), (sREP[:], rep),
                     (sOC[:], onesc), (sOR[:], onesr), (sMask[:], maskt),
                     (sCosQ[:], cosq), (sSinQ[:], sinq)):
            P.dma("sync", d, s)
        for cch in range(16):
            P.dma("sync", sRW[:, cch * 8:(cch + 1) * 8], rwt[cch])

        def split(dst_h, dst_l, src):
            hv = P.op("vector", lambda e, o=dst_h, i=src: e.tensor_copy(o, i), "v")
            P.wait("gpsimd", "v", hv)
            gv = P.op("gpsimd", lambda e, o=dst_l, i=src, hh=dst_h: e.tensor_sub(o, i, hh), "g")
            return hv, gv

        def split_v(dst_h, dst_l, src):
            hv = P.op("vector", lambda e, o=dst_h, i=src: e.tensor_copy(o, i), "v")
            P.wait("vector", "v", hv)
            lv = P.op("vector", lambda e, o=dst_l, i=src, hh=dst_h: e.tensor_sub(o, i, hh), "v")
            return hv, lv

        for cch in range(16):
            P.dma("sync", sXN[:, cch * 128:(cch + 1) * 128], dwt[cch])
        dwd = P.val("di")
        P.wait("vector", "di", dwd)
        dw_h, dw_g = split(sDWh[:], sDWl[:], sXN[:])
        ud = P.dma("sync", sX[:], u128)
        P.wait("vector", "di", ud)
        u_h, u_g = split(sUh[:], sUl[:], sX[:])

        with contextlib.ExitStack() as p1:
            sb1 = lambda name, shape, dt=F32: p1.enter_context(nc.sbuf_tensor(name, shape, dt))
            sXnTh = sb1("sXnTh", [128, 16 * 512], BF16)
            sXnTl = sb1("sXnTl", [128, 16 * 512], BF16)
            sWkvh = sb1("sWkvh", [128, 2 * 1024], BF16)
            sWkvl = sb1("sWkvl", [128, 2 * 1024], BF16)
            sKst = sb1("sKst", [128, 4 * 512])
            sCos = sb1("sCos", [128, T])
            sSin = sb1("sSin", [128, T])
            sT3 = sb1("sT3", [128, 256])
            sWS = sb1("sWS", [128, 2 * 1024])
            sWH = sb1("sWH", [128, 2 * 1024], BF16)
            sWL = sb1("sWL", [128, 2 * 1024], BF16)
            pKb = p1.enter_context(nc.psum_tensor("pKb", [128, 2048], F32))
            pVb = p1.enter_context(nc.psum_tensor("pVb", [128, 2048], F32))

            P.dma("sync", sCos[:], cost)
            P.dma("sync", sSin[:], sint)
            cst_di = P.val("di")
            P.wait("tensor", "di", cst_di)
            P.wait("vector", "di", cst_di)
            P.wait("scalar", "v", max(dw_h, u_h))
            P.wait("scalar", "g", max(dw_g, u_g))
            P.wait("sync", "g", max(dw_g, u_g))

            wsplit_state = {"j": 0, "vg": {}, "dw": {}}
            wsplit_jobs = []
            for cch in range(16):
                wsplit_jobs.append((wkv[cch], 1024, wkvh[cch], wkvl[cch]))
            for hh in range(NH):
                for hf in range(2):
                    wsplit_jobs.append((wq[hh][:, hf * 1024:(hf + 1) * 1024], 1024,
                                        wqh[hh][:, hf * 1024:(hf + 1) * 1024],
                                        wql[hh][:, hf * 1024:(hf + 1) * 1024]))
            for cch in range(16):
                for hf in range(2):
                    wsplit_jobs.append((wo[cch][:, hf * 1024:(hf + 1) * 1024], 1024,
                                        woh[cch][:, hf * 1024:(hf + 1) * 1024],
                                        wol[cch][:, hf * 1024:(hf + 1) * 1024]))

            def emit_wsplit(njobs):
                st = wsplit_state
                for _ in range(njobs):
                    if not wsplit_jobs:
                        return
                    src, width, dsth, dstl = wsplit_jobs.pop(0)
                    j = st["j"]
                    st["j"] += 1
                    par = j % 2
                    P.wait("sync", "v", st["vg"].get(("v", j - 2), 0))
                    P.wait("sync", "g", st["vg"].get(("g", j - 2), 0))
                    dv = P.dma("sync", sWS[:, par * 1024:par * 1024 + width], src)
                    P.wait("vector", "di", dv)
                    P.wait("vector", "dw", st["dw"].get(("h", j - 2), 0))
                    hv = P.op("vector", lambda e, o=sWH[:, par * 1024:par * 1024 + width],
                              i=sWS[:, par * 1024:par * 1024 + width]: e.tensor_copy(o, i), "v")
                    P.wait("gpsimd", "v", hv)
                    P.wait("gpsimd", "dw", st["dw"].get(("l", j - 2), 0))
                    gv = P.op("gpsimd", lambda e, o=sWL[:, par * 1024:par * 1024 + width],
                              i=sWS[:, par * 1024:par * 1024 + width], hh2=sWH[:, par * 1024:par * 1024 + width]:
                              e.tensor_sub(o, i, hh2), "g")
                    st["vg"][("v", j)] = hv
                    st["vg"][("g", j)] = gv
                    P.wait("scalar", "v", hv)
                    st["dw"][("h", j)] = P.dma("scalar", dsth, sWH[:, par * 1024:par * 1024 + width], "dw")
                    P.wait("scalar", "g", gv)
                    st["dw"][("l", j)] = P.dma("scalar", dstl, sWL[:, par * 1024:par * 1024 + width], "dw")
                    st["dw"][j] = st["dw"][("l", j)]

            emit_wsplit(16)
            wkv_dw = {c: wsplit_state["dw"][c] for c in range(16)}

            def wq_dw(chunk):
                return wsplit_state["dw"].get(16 + chunk, None)

            def wo_dw(chunk):
                return wsplit_state["dw"].get(48 + chunk, None)

            def tr_slot(j):
                base = (j % 4) * 512 + 256
                return pKb[:, base:base + 128]

            tr_evac_s = [0] * 8
            prev_evac_s, prev_rope_v, prev_rope_g = 0, 0, 0
            x_free_v, x_free_s = 0, 0
            wkv_t = {}
            kc = 0
            split_done = []

            for g in range(4):
                for ti in range(4):
                    tb = g * 4 + ti
                    bn = sXN
                    P.wait("sync", "v", x_free_v)
                    P.wait("sync", "s", x_free_s)
                    xd = P.dma("sync", sX[:], x[tb * 128:(tb + 1) * 128, :])
                    lv = emit_ln(P, sX, sXN, ln, "di", xd)
                    x_free_v, x_free_s = lv["xn_v"], lv["sq_s"]
                    P.wait("tensor", "v", lv["xn_v"])
                    if g > 0 and ti == 0:
                        P.wait("tensor", "s", prev_evac_s)
                    ttr = 0
                    for cch in range(16):
                        j = (ti * 16 + cch) % 4
                        P.wait("tensor", "s", tr_evac_s[j])
                        slot = tr_slot(j)
                        ttr = P.op("tensor", lambda e, o=slot, i=bn[:, cch * 128:(cch + 1) * 128],
                                   idm=sID: e.transpose(o, i, idm[:, :]), "t")
                        P.wait("scalar", "t", ttr)
                        xf = sXf[:, (j % 4) * 128:(j % 4 + 1) * 128]
                        P.wait("scalar", "v", split_done[-4][0] if len(split_done) >= 4 else 0)
                        P.wait("scalar", "g", split_done[-4][1] if len(split_done) >= 4 else 0)
                        tr_evac_s[j] = P.op("scalar", lambda e, o=xf, i=slot: e.copy(o, i), "s")
                        P.wait("vector", "s", tr_evac_s[j])
                        dh = sXnTh[:, cch * 512 + ti * 128: cch * 512 + ti * 128 + 128]
                        dl = sXnTl[:, cch * 512 + ti * 128: cch * 512 + ti * 128 + 128]
                        split_done.append(split(dh, dl, xf))
                    P.wait("scalar", "t", ttr)
                    emit_wsplit(1)
                xn_split_v = split_done[-1][0]
                xn_split_g = split_done[-1][1]

                for cch in range(16):
                    P.wait("sync", "t", wkv_t.get(kc - 2, 0))
                    P.wait("sync", "dw", wkv_dw[cch])
                    P.dma("sync", sWkvh[:, (kc % 2) * 1024:(kc % 2 + 1) * 1024], wkvh[cch])
                    wv = P.dma("sync", sWkvl[:, (kc % 2) * 1024:(kc % 2 + 1) * 1024], wkvl[cch])
                    P.wait("tensor", "di", wv)
                    if cch == 0:
                        P.wait("tensor", "v", xn_split_v)
                        P.wait("tensor", "g", xn_split_g)
                        P.wait("tensor", "v", prev_rope_v)
                        P.wait("tensor", "g", prev_rope_g)
                    st0, sp0 = (cch == 0), (cch == 15)
                    base = (kc % 2) * 1024
                    lastmm = 0
                    for kvh in range(NKV):
                        lastmm = P.mm3(pKb[:, kvh * 512:(kvh + 1) * 512],
                                       sWkvh[:, base + kvh * 128:base + (kvh + 1) * 128],
                                       sWkvl[:, base + kvh * 128:base + (kvh + 1) * 128],
                                       sXnTh[:, cch * 512:(cch + 1) * 512],
                                       sXnTl[:, cch * 512:(cch + 1) * 512], st0, sp0, "t")
                    for ti in range(4):
                        lastmm = P.mm3(pVb[:, ti * 512:(ti + 1) * 512],
                                       sXnTh[:, cch * 512 + ti * 128:cch * 512 + (ti + 1) * 128],
                                       sXnTl[:, cch * 512 + ti * 128:cch * 512 + (ti + 1) * 128],
                                       sWkvh[:, base + 512:base + 1024],
                                       sWkvl[:, base + 512:base + 1024], st0, sp0, "t")
                    wkv_t[kc] = lastmm
                    kc += 1
                    emit_wsplit(1)
                kv_done = wkv_t[kc - 1]

                P.wait("scalar", "t", kv_done)
                P.wait("vector", "t", kv_done)
                ev_s, ev_v = 0, 0
                for ti in range(4):
                    dsth = sVh[:, (g * 4 + ti) * 512:(g * 4 + ti + 1) * 512]
                    dstl = sVl[:, (g * 4 + ti) * 512:(g * 4 + ti + 1) * 512]
                    ev_s = P.op("scalar", lambda e, o=dsth, i=pVb[:, ti * 512:(ti + 1) * 512]: e.copy(o, i), "s")
                    P.wait("vector", "s", ev_s)
                    ev_v = P.op("vector", lambda e, o=dstl, i=pVb[:, ti * 512:(ti + 1) * 512], hh=dsth:
                                e.tensor_sub(o, i, hh), "v")
                evK = [0] * NKV
                for kvh in range(NKV):
                    evK[kvh] = P.op("scalar", lambda e, o=sKst[:, kvh * 512:(kvh + 1) * 512],
                                    i=pKb[:, kvh * 512:(kvh + 1) * 512]: e.copy(o, i), "s")
                prev_evac_s = max(ev_s, max(evK))

                rv, rg = prev_rope_v, prev_rope_g
                for kvh in range(NKV):
                    P.wait("tensor", "s", evK[kvh])
                    if kvh == 0:
                        P.wait("tensor", "v", ev_v)
                    rot_t = P.op("tensor", lambda e, o=pVb[:, kvh * 512:(kvh + 1) * 512], l=sRT,
                                 r=sKst[:, kvh * 512:(kvh + 1) * 512]:
                                 e.matmul(o, l[:, :], r, start=True, stop=True), "t")
                    P.wait("vector", "t", rot_t)
                    for u in range(2):
                        csl = slice(g * 512 + u * 256, g * 512 + (u + 1) * 256)
                        ksl = slice(kvh * 512 + u * 256, kvh * 512 + (u + 1) * 256)
                        P.op("vector", lambda e, o=sT1, i=sKst[:, ksl], c2=sCos[:, csl]: e.tensor_mul(o[:], i, c2))
                        P.op("vector", lambda e, o=sT2, i=pVb[:, ksl], c2=sSin[:, csl]: e.tensor_mul(o[:], i, c2))
                        P.wait("vector", "g", rg)
                        t3v = P.op("vector", lambda e, o=sT3, a=sT1, b=sT2: e.tensor_add(o[:], a[:], b[:]), "v")
                        P.wait("vector", "v", t3v)
                        kdst = slice(kvh * T + g * 512 + u * 256, kvh * T + g * 512 + (u + 1) * 256)
                        rv, rg = split(sKh[:, kdst], sKl[:, kdst], sT3[:])
                prev_rope_v, prev_rope_g = rv, rg


            emit_wsplit(100)
            wsp_done = P.val("dw")

            ph = P.sem("ph")
            for eng in ENGS:
                P.op(eng, lambda e, s=ph: e.nop().then_inc(s, 1))
            for eng in ENGS:
                P.wait(eng, "ph", 5)

        with contextlib.ExitStack() as p2:
            sb2 = lambda name, shape, dt=F32: p2.enter_context(nc.sbuf_tensor(name, shape, dt))
            sXnTqh = sb2("sXnTqh", [128, 16 * 256], BF16)
            sXnTql = sb2("sXnTql", [128, 16 * 256], BF16)
            sYallh = sb2("sYallh", [128, 16 * 256], BF16)
            sYalll = sb2("sYalll", [128, 16 * 256], BF16)
            sPallh = sb2("sPallh", [128, 16 * 256], BF16)
            sPalll = sb2("sPalll", [128, 16 * 256], BF16)
            sWqh = sb2("sWqh", [128, 2 * 1024], BF16)
            sWql = sb2("sWql", [128, 2 * 1024], BF16)
            sWoh = sb2("sWoh", [128, 2 * 1024], BF16)
            sWol = sb2("sWol", [128, 2 * 1024], BF16)
            sQh = sb2("sQh", [128, 2 * 256])
            sQhh = sb2("sQhh", [128, 2 * 256], BF16)
            sQhl = sb2("sQhl", [128, 2 * 256], BF16)
            sPf = sb2("sPf", [128, 4 * 256])
            sYf = sb2("sYf", [128, 256])
            sAcc = sb2("sAcc", [128, 256])
            sB = sb2("sB", [128, 256])
            sOT = sb2("sOT", [128, 2 * 256])
            sDr = sb2("sDr", [1, 256])
            sW8 = sb2("sW8", [128, 16])
            sW2 = sb2("sW2", [128, 16])
            sGmask = sb2("sGmask", [128, 16])
            sGate = sb2("sGate", [128, 16])
            sG8 = sb2("sG8", [8, 256])
            sA = sb2("sA", [128, 256])
            sAGf = sb2("sAGf", [128, 256])
            sAGh = sb2("sAGh", [128, 256], BF16)
            sAGl = sb2("sAGl", [128, 256], BF16)
            sNM8 = sb2("sNM8", [128, 2])
            sDen8 = sb2("sDen8", [128, 2])
            sRec8 = sb2("sRec8", [128, 2])
            sM1 = sb2("sM1", [128, 2])
            sM2 = sb2("sM2", [128, 2])
            pb = [p2.enter_context(nc.psum_tensor("pb%d" % i, [128, 512], F32)) for i in range(8)]
            pQ, pRot, pS0, pS1, pY, pD, pB, pF = pb


            wq_t = {}
            wo_t = {}
            x_free_v, x_free_s = 0, 0

            for qt in range(4):
                KB = 4 * qt + 4
                P.fence()

                tr_evac = [0] * 4
                split_hist = []
                rt_mm = {}
                for t2 in range(2):
                    sidx = 2 * qt + t2
                    if x_free_v is not None:
                        P.wait("sync", "v", x_free_v)
                    P.wait("sync", "s", x_free_s)
                    P.wait("sync", "do2", P.val("do2"))
                    xd = P.dma("sync", sX[:], xq[sidx * 128:(sidx + 1) * 128, :])
                    lv = emit_ln(P, sX, sXN, ln, "di", xd)
                    x_free_v, x_free_s = lv["xn_v"], lv["sq_s"]
                    P.wait("tensor", "v", lv["xn_v"])
                    ttr = 0
                    for cch in range(16):
                        j = (t2 * 16 + cch) % 4
                        slot = pb[j][:, 256:384]
                        P.wait("tensor", "s", tr_evac[j])
                        ttr = P.op("tensor", lambda e, o=slot, i=sXN[:, cch * 128:(cch + 1) * 128],
                                   idm=sID: e.transpose(o, i, idm[:, :]), "t")
                        P.wait("scalar", "t", ttr)
                        xf = sXf[:, (cch % 4) * 128:(cch % 4 + 1) * 128]
                        P.wait("scalar", "v", split_hist[-4][0] if len(split_hist) >= 4 else 0)
                        P.wait("scalar", "g", split_hist[-4][1] if len(split_hist) >= 4 else 0)
                        P.wait("scalar", "t", rt_mm.get((t2, cch - 4), 0))
                        tr_evac[j] = P.op("scalar", lambda e, o=xf, i=slot: e.copy(o, i), "s")
                        P.wait("vector", "s", tr_evac[j])
                        dh = sXnTqh[:, cch * 256 + t2 * 128: cch * 256 + t2 * 128 + 128]
                        dl = sXnTql[:, cch * 256 + t2 * 128: cch * 256 + t2 * 128 + 128]
                        split_hist.append(split(dh, dl, xf))
                        pRt = pD[:, 256:264] if t2 == 0 else pD[:, 384:392]
                        P.wait("tensor", "s", tr_evac[j])
                        rt_mm[(t2, cch)] = P.op("tensor", lambda e, o=pRt, l=xf, r=sRW[:, cch * 8:(cch + 1) * 8],
                                                st=(cch == 0), sp=(cch == 15): e.matmul(o, l, r, start=st, stop=sp), "t")
                    P.wait("scalar", "t", ttr)
                xnq_v = split_hist[-1][0]
                xnq_g = split_hist[-1][1]

                P.wait("tensor", "v", xnq_v)
                P.wait("tensor", "g", xnq_g)
                hmm = 0
                for cch in range(16):
                    hmm = P.mm3(pF[:, 0:256], sDWh[:, cch * 128:(cch + 1) * 128], sDWl[:, cch * 128:(cch + 1) * 128],
                                sXnTqh[:, cch * 256:(cch + 1) * 256], sXnTql[:, cch * 256:(cch + 1) * 256],
                                (cch == 0), (cch == 15), "t")
                P.wait("scalar", "t", hmm)
                silu_s = P.op("scalar", lambda e, o=sA, i=pF[:, 0:256]: e.activation(o[:], i, AF.Silu), "s")

                S = {}
                GA = {}
                def emit_gating():
                    ag_v = 0
                    for t2 in range(2):
                        pRt = pD[:, 256:264] if t2 == 0 else pD[:, 384:392]
                        rmm = max(rt_mm[(0, 15)], rt_mm[(1, 15)])
                        P.wait("vector", "t", rmm)
                        nm = P.op("vector", lambda e, o=sNM8[:, t2:t2 + 1], i=pRt:
                                  e.reduce_max(o, i, axis=AX.X, negate=True), "v")
                        P.wait("scalar", "v", nm)
                        ex = P.op("scalar", lambda e, o=sW8[:, t2 * 8:(t2 + 1) * 8], i=pRt, b=sNM8[:, t2:t2 + 1],
                                  acc=sDen8[:, t2:t2 + 1]: e.activation(o, i, AF.Exp, bias=b, accum_out=acc), "s")
                        P.wait("vector", "s", ex)
                        rc = P.op("vector", lambda e, o=sRec8[:, t2:t2 + 1], i=sDen8[:, t2:t2 + 1]:
                                  e.reciprocal(o, i), "v")
                        P.wait("vector", "v", rc)
                        sm = P.op("vector", lambda e, o=sW8[:, t2 * 8:(t2 + 1) * 8], r=sRec8[:, t2:t2 + 1]:
                                  e.tensor_scalar_mul(o, o, r), "v")
                        P.wait("vector", "v", sm)
                        m1 = P.op("vector", lambda e, o=sM1[:, t2:t2 + 1], i=sW8[:, t2 * 8:(t2 + 1) * 8]:
                                  e.reduce_max(o, i, axis=AX.X), "v")
                        P.wait("vector", "v", m1)
                        eq = P.op("vector", lambda e, o=sGmask[:, t2 * 8:(t2 + 1) * 8], i=sW8[:, t2 * 8:(t2 + 1) * 8],
                                  m=sM1[:, t2:t2 + 1]: e.tensor_single_scalar(o, i, m, ALU.is_ge), "v")
                        P.wait("vector", "v", eq)
                        w2 = P.op("vector", lambda e, o=sW2[:, t2 * 8:(t2 + 1) * 8], i=sGmask[:, t2 * 8:(t2 + 1) * 8],
                                  w=sW8[:, t2 * 8:(t2 + 1) * 8]: e.scalar_tensor_tensor(o, i, -1e30, w, ALU.mult, ALU.add), "v")
                        P.wait("vector", "v", w2)
                        m2 = P.op("vector", lambda e, o=sM2[:, t2:t2 + 1], i=sW2[:, t2 * 8:(t2 + 1) * 8]:
                                  e.reduce_max(o, i, axis=AX.X), "v")
                        P.wait("vector", "v", m2)
                        gm = P.op("vector", lambda e, o=sGmask[:, t2 * 8:(t2 + 1) * 8], i=sW8[:, t2 * 8:(t2 + 1) * 8],
                                  m=sM2[:, t2:t2 + 1]: e.tensor_single_scalar(o, i, m, ALU.is_ge), "v")
                        P.wait("vector", "v", gm)
                        gv = P.op("vector", lambda e, o=sGate[:, t2 * 8:(t2 + 1) * 8], i=sW8[:, t2 * 8:(t2 + 1) * 8],
                                  m=sGmask[:, t2 * 8:(t2 + 1) * 8]: e.tensor_mul(o, i, m), "v")
                        P.wait("tensor", "v", gv)
                        gt_t = P.op("tensor", lambda e, o=pB[0:8, 384:512], i=sGate[:, t2 * 8:(t2 + 1) * 8], idm=sID:
                                    e.transpose(o, i, idm[:, :]), "t")
                        P.wait("scalar", "t", gt_t)
                        g8 = P.op("scalar", lambda e, o=sG8[0:8, t2 * 128:(t2 + 1) * 128], i=pB[0:8, 384:512]:
                                  e.copy(o, i), "s")
                        P.wait("tensor", "s", g8)
                        if t2 == 1:
                            P.wait("tensor", "v", ag_v)
                        grep_t = P.op("tensor", lambda e, o=pB[:, 256:384], l=sREP, r=sG8[0:8, t2 * 128:(t2 + 1) * 128]:
                                      e.matmul(o, l[:, :], r, start=True, stop=True), "t")
                        P.wait("vector", "t", grep_t)
                        P.wait("vector", "s", silu_s)
                        ag_v = P.op("vector", lambda e, o=sAGf[:, t2 * 128:(t2 + 1) * 128], i=sA[:, t2 * 128:(t2 + 1) * 128],
                                    g2=pB[:, 256:384]: e.tensor_mul(o, i, g2), "v")
                    P.wait("vector", "v", ag_v)
                    agh, agg = split_v(sAGh[:], sAGl[:], sAGf[:])
                    GA["agg"] = agg


                def stage_q(h, qt=qt):
                    par = h % 2
                    st = {"par": par}
                    wdi = {}
                    for half in range(2):
                        jj = 2 * h + half
                        P.wait("sync", "t", wq_t.get(jj - 2, 0))
                        P.wait("sync", "dw", wq_dw(jj))
                        P.dma("sync", sWqh[:, (jj % 2) * 1024:(jj % 2 + 1) * 1024],
                              wqh[h][:, half * 1024:(half + 1) * 1024])
                        wdi[half] = P.dma("sync", sWql[:, (jj % 2) * 1024:(jj % 2 + 1) * 1024],
                                          wql[h][:, half * 1024:(half + 1) * 1024])
                    if h == 0:
                        P.wait("tensor", "v", xnq_v)
                        P.wait("tensor", "g", xnq_g)
                    if h >= 1:
                        P.wait("tensor", "s", S[h - 1]["qev"])
                    qmm = 0
                    for cch in range(16):
                        jj = 2 * h + cch // 8
                        if cch == 0:
                            P.wait("tensor", "di", wdi[0])
                        if cch == 8:
                            P.wait("tensor", "di", wdi[1])
                        base2 = (jj % 2) * 1024 + (cch % 8) * 128
                        qmm = P.mm3(pQ[:, 0:256], sWqh[:, base2: base2 + 128],
                                    sWql[:, base2: base2 + 128],
                                    sXnTqh[:, cch * 256:(cch + 1) * 256], sXnTql[:, cch * 256:(cch + 1) * 256],
                                    (cch == 0), (cch == 15), "t" if cch == 15 else None)
                    wq_t[2 * h] = qmm
                    wq_t[2 * h + 1] = qmm
                    st["qmm"] = qmm
                    P.wait("scalar", "t", qmm)
                    st["qev"] = P.op("scalar", lambda e, o=sQh[:, par * 256:(par + 1) * 256], i=pQ[:, 0:256]:
                                     e.copy(o, i), "s")
                    S[h] = st

                def stage_rot(h, qt=qt):
                    st = S[h]
                    par = st["par"]
                    P.wait("tensor", "s", st["qev"])
                    rot_t = P.op("tensor", lambda e, o=pRot[:, 0:256], l=sRT, r=sQh[:, par * 256:(par + 1) * 256]:
                                 e.matmul(o, l[:, :], r, start=True, stop=True), "t")
                    P.wait("vector", "t", rot_t)
                    for t2 in range(2):
                        qsl = slice((2 * qt + t2) * 128, (2 * qt + t2 + 1) * 128)
                        dsl = slice(t2 * 128, (t2 + 1) * 128)
                        P.op("vector", lambda e, o=sT1[:, dsl],
                             i=sQh[:, par * 256 + t2 * 128: par * 256 + (t2 + 1) * 128],
                             c2=sCosQ[:, qsl]: e.tensor_mul(o, i, c2))
                        P.op("vector", lambda e, o=sT2[:, dsl], i=pRot[:, dsl], c2=sSinQ[:, qsl]:
                             e.tensor_mul(o, i, c2))
                    rope_v = P.op("vector", lambda e, o=sQh[:, par * 256:(par + 1) * 256], a=sT1, b=sT2:
                                  e.tensor_add(o, a[:], b[:]), "v")
                    P.wait("vector", "v", rope_v)
                    st["rope_v"] = rope_v
                    qs_v, qs_lo = split_v(sQhh[:, par * 256:(par + 1) * 256], sQhl[:, par * 256:(par + 1) * 256],
                                          sQh[:, par * 256:(par + 1) * 256])
                    st["qs_v"] = qs_lo

                def stage_s(h, qt=qt, KB=KB):
                    st = S[h]
                    par = st["par"]
                    kvh = h // 4
                    full_end = 4 * qt + 2
                    exp_s = {}
                    hi_s = {}
                    lo_v = {}
                    acc_v = 0
                    mask_v = 0
                    av_t = 0
                    pairs = []
                    kb = 0
                    while kb < KB:
                        paired = (kb % 2 == 0) and (kb + 1 < full_end)
                        pairs.append((kb, 2 if paired else 1))
                        kb += 2 if paired else 1

                    pYb = pY if h % 2 == 0 else pF

                    def emit_av(pi):
                        nonlocal av_t
                        kb0, n2 = pairs[pi]
                        p0 = kb0 // 2
                        P.wait("tensor", "s", hi_s[p0])
                        P.wait("tensor", "v", lo_v[p0])
                        if pi == 0:
                            P.wait("tensor", "v", S.get(h - 2, {}).get("ym", 0))
                            if h == 1:
                                P.wait("tensor", "s", silu_s)
                        for kbx in range(kb0, kb0 + n2):
                            cols = slice(0, 256) if kbx < full_end else slice(128, 256)
                            vsl = slice(kbx * 512 + kvh * 128, kbx * 512 + (kvh + 1) * 128)
                            av_t = P.mm3(pYb[:, cols], sVh[:, vsl], sVl[:, vsl],
                                         sPallh[:, kbx * 256 + cols.start: kbx * 256 + cols.stop],
                                         sPalll[:, kbx * 256 + cols.start: kbx * 256 + cols.stop],
                                         (kbx == 0), (kbx == KB - 1),
                                         "t" if kbx == KB - 1 else None)

                    for pi, (kb, n2) in enumerate(pairs):
                        p = kb // 2
                        bank = pS0 if (p % 2) == 0 else pS1
                        if kb == 0:
                            P.wait("tensor", "v", st["qs_v"])
                            P.wait("scalar", "t", S.get(h - 1, {}).get("av_t", 0))
                        if p >= 2:
                            P.wait("tensor", "s", exp_s.get(p - 2, 0))
                        smm = 0
                        for k2 in range(n2):
                            kbx = kb + k2
                            cols = slice(0, 256) if kbx < full_end else slice(128, 256)
                            boff = (kbx % 2) * 256
                            ksl = slice(kvh * T + kbx * 128, kvh * T + (kbx + 1) * 128)
                            qsl2 = slice(par * 256 + cols.start, par * 256 + cols.stop)
                            smm = P.mm3(bank[:, boff + cols.start: boff + cols.stop], sKh[:, ksl], sKl[:, ksl],
                                        sQhh[:, qsl2], sQhl[:, qsl2], True, True, "t")
                        if pi >= 1:
                            emit_av(pi - 1)
                        P.wait("scalar", "t", smm)
                        P.wait("scalar", "v", lo_v.get(p - 2, 0))
                        sl = (p % 2) * 512
                        if n2 == 2:
                            psrc = bank[:, 0:512]
                            pfdst = sPf[:, sl:sl + 512]
                        else:
                            cols = slice(0, 256) if kb < full_end else slice(128, 256)
                            boff = (kb % 2) * 256
                            psrc = bank[:, boff + cols.start: boff + cols.stop]
                            pfdst = sPf[:, sl + boff + cols.start: sl + boff + cols.stop]
                        exp_s[p] = P.op("scalar", lambda e, o=pfdst, i=psrc:
                                        e.activation(o, i, AF.Exp, scale=SCALE), "s")
                        P.wait("vector", "s", exp_s[p])
                        for kbx in range(kb, kb + n2):
                            if kbx == 4 * qt + 1:
                                mask_v = P.op("vector", lambda e, o=sPf[:, sl + (kbx % 2) * 256: sl + (kbx % 2) * 256 + 128],
                                              m=sMask[:, 2 * qt:2 * qt + 1]: e.tensor_scalar_mul(o, o, m), "v")
                            if kbx == 4 * qt + 3:
                                mask_v = P.op("vector", lambda e, o=sPf[:, sl + (kbx % 2) * 256 + 128: sl + (kbx % 2) * 256 + 256],
                                              m=sMask[:, 2 * qt + 1:2 * qt + 2]: e.tensor_scalar_mul(o, o, m), "v")
                        for kbx in range(kb, kb + n2):
                            cols = slice(0, 256) if kbx < full_end else slice(128, 256)
                            if kbx == 0:
                                P.wait("vector", "t", S.get(h - 1, {}).get("d_t", 0))
                                acc_v = P.op("vector", lambda e, o=sAcc, i=sPf[:, sl:sl + 256]: e.tensor_copy(o[:], i), "v")
                            else:
                                acc_v = P.op("vector", lambda e, o=sAcc[:, cols],
                                             i=sPf[:, sl + (kbx % 2) * 256 + cols.start: sl + (kbx % 2) * 256 + cols.stop]:
                                             e.tensor_add(o, o, i), "v")
                        if mask_v:
                            P.wait("scalar", "v", mask_v)
                        P.wait("scalar", "s", exp_s[p])
                        if n2 == 2:
                            hdst = sPallh[:, kb * 256:(kb + 2) * 256]
                            ldst = sPalll[:, kb * 256:(kb + 2) * 256]
                            pfs = sPf[:, sl:sl + 512]
                        else:
                            cols = slice(0, 256) if kb < full_end else slice(128, 256)
                            boff = (kb % 2) * 256
                            hdst = sPallh[:, kb * 256 + cols.start: kb * 256 + cols.stop]
                            ldst = sPalll[:, kb * 256 + cols.start: kb * 256 + cols.stop]
                            pfs = sPf[:, sl + boff + cols.start: sl + boff + cols.stop]
                        hi_s[p] = P.op("scalar", lambda e, o=hdst, i=pfs: e.copy(o, i), "s")
                        P.wait("vector", "s", hi_s[p])
                        lo_v[p] = P.op("vector", lambda e, o=ldst, i=pfs, hh=hdst: e.tensor_sub(o, i, hh), "v")
                    emit_av(len(pairs) - 1)
                    st["av_t"] = av_t
                    st["acc_v"] = acc_v

                def stage_d(h, qt=qt, KB=KB):
                    st = S[h]
                    P.wait("tensor", "v", st["acc_v"])
                    P.wait("tensor", "v", S.get(h - 1, {}).get("rec_v", 0))
                    d_t = P.op("tensor", lambda e, o=pD[0:1, 0:256], l=sOC, r=sAcc:
                               e.matmul(o, l[:, :], r[:], start=True, stop=True), "t")
                    st["d_t"] = d_t
                    P.wait("vector", "t", d_t)
                    st["rec_v"] = P.op("vector", lambda e, o=sDr, i=pD[0:1, 0:256]:
                                       e.reciprocal(o[0:1, :], i), "v")

                def stage_fin(h, qt=qt, KB=KB):
                    st = S[h]
                    pYb = pY if h % 2 == 0 else pF
                    P.wait("tensor", "s", S.get(h - 1, {}).get("bev", 0))
                    P.wait("tensor", "v", st["rec_v"])
                    b_t = P.op("tensor", lambda e, o=pB[:, 0:256], l=sOR, r=sDr:
                               e.matmul(o, l[:, :], r[0:1, :], start=True, stop=True), "t")
                    P.wait("scalar", "t", b_t)
                    bev = P.op("scalar", lambda e, o=sB, i=pB[:, 0:256]: e.copy(o[:], i), "s")
                    st["bev"] = bev
                    P.wait("vector", "s", bev)
                    P.wait("vector", "t", st["av_t"])
                    P.wait("vector", "v", S.get(h - 1, {}).get("yh_v", 0))
                    ym = P.op("vector", lambda e, o=sYf, i=pYb[:, 0:256], b=sB: e.tensor_mul(o[:], i, b[:]), "v")
                    st["ym"] = ym
                    P.wait("vector", "v", ym)
                    yh_v, yh_lo = split_v(sYallh[:, h * 256:(h + 1) * 256], sYalll[:, h * 256:(h + 1) * 256], sYf[:])
                    st["yh_v"] = yh_lo

                emit_gating()
                for h in range(NH + 1):
                    if h < NH:
                        stage_q(h)
                    if h >= 1:
                        stage_s(h - 1)
                    if h >= 2:
                        stage_fin(h - 2)
                    if h < NH:
                        stage_rot(h)
                    if h >= 1:
                        stage_d(h - 1)
                stage_fin(NH - 1)
                agg = GA["agg"]

                P.wait("sync", "v", x_free_v)
                P.wait("sync", "s", x_free_s)
                P.wait("sync", "do2", P.val("do2"))
                xda = P.dma("sync", sX[:], xq[(2 * qt) * 128:(2 * qt + 1) * 128, :])
                xdb = P.dma("sync", sX2[:], xq[(2 * qt + 1) * 128:(2 * qt + 2) * 128, :])
                P.fence(engines=("tensor",))
                oadd_v = {0: 0, 1: 0}
                oev = {0: 0, 1: 0}
                tr_free = {0: 0, 1: 0}
                tr_free_t = {0: 0, 1: 0}
                for cch in range(16):
                    par = cch % 2
                    wdi = {}
                    for half in range(2):
                        jj = 2 * cch + half
                        P.wait("sync", "t", wo_t.get(jj - 2, 0))
                        P.wait("sync", "dw", wo_dw(jj))
                        P.dma("sync", sWoh[:, (jj % 2) * 1024:(jj % 2 + 1) * 1024],
                              woh[cch][:, half * 1024:(half + 1) * 1024])
                        wdi[half] = P.dma("sync", sWol[:, (jj % 2) * 1024:(jj % 2 + 1) * 1024],
                                          wol[cch][:, half * 1024:(half + 1) * 1024])
                    pO = pQ if par == 0 else pF
                    P.wait("tensor", "s", oev[par])
                    for h in range(NH):
                        jj = 2 * cch + h // 8
                        if h == 0:
                            P.wait("tensor", "di", wdi[0])
                        if h == 8:
                            P.wait("tensor", "di", wdi[1])
                        base2 = (jj % 2) * 1024 + (h % 8) * 128
                        P.mm3(pO[:, 0:256], sWoh[:, base2: base2 + 128],
                              sWol[:, base2: base2 + 128],
                              sYallh[:, h * 256:(h + 1) * 256], sYalll[:, h * 256:(h + 1) * 256],
                              (h == 0), False)
                    if cch == 0:
                        P.wait("tensor", "v", agg)
                    omm = P.mm3(pO[:, 0:256], sUh[:, cch * 128:(cch + 1) * 128], sUl[:, cch * 128:(cch + 1) * 128],
                                sAGh[:], sAGl[:], False, True, "t")
                    wo_t[2 * cch] = omm
                    wo_t[2 * cch + 1] = omm
                    P.wait("scalar", "t", omm)
                    P.wait("scalar", "t", max(tr_free_t.values()))
                    oev[par] = P.op("scalar", lambda e, o=sOT[:, par * 256:(par + 1) * 256], i=pO[:, 0:256]:
                                    e.copy(o, i), "s")
                    for t2 in range(2):
                        ptr = (pRot if t2 == 0 else pS0)[:, 0:128]
                        P.wait("tensor", "s", oev[par])
                        P.wait("tensor", "v", tr_free[t2])
                        tr_t = P.op("tensor", lambda e, o=ptr,
                                    i=sOT[:, par * 256 + t2 * 128: par * 256 + (t2 + 1) * 128], idm=sID:
                                    e.transpose(o, i, idm[:, :]), "t")
                        tr_free_t[t2] = tr_t
                        P.wait("vector", "t", tr_t)
                        if cch == 0 and t2 == 0:
                            P.wait("vector", "di", xdb)
                        sxt = sX if t2 == 0 else sX2
                        tr_free[t2] = P.op("vector", lambda e, o=sxt[:, cch * 128:(cch + 1) * 128], i=ptr,
                                           x2=sxt[:, cch * 128:(cch + 1) * 128]: e.tensor_add(o, i, x2), "v")
                        oadd_v[t2] = tr_free[t2]
                P.wait("sync", "v", oadd_v[0])
                P.dma("sync", out[(2 * qt) * 128:(2 * qt + 1) * 128, :], sX[:], "do2")
                P.wait("sync", "v", oadd_v[1])
                P.dma("sync", out[(2 * qt + 1) * 128:(2 * qt + 2) * 128, :], sX2[:], "do2")
                x_free_v = None
                x_free_s = 0

            P.wait("gpsimd", "do2", P.val("do2"))

        blk = ctx.enter_context(nc.Block())

        @blk.sync
        def _(e):
            for f in P.stream["sync"]:
                f(e)

        @blk.gpsimd
        def _(e):
            for f in P.stream["gpsimd"]:
                f(e)

        @blk.tensor
        def _(e):
            for f in P.stream["tensor"]:
                f(e)

        @blk.vector
        def _(e):
            for f in P.stream["vector"]:
                f(e)

        @blk.scalar
        def _(e):
            for f in P.stream["scalar"]:
                f(e)

    return nc


def _host_prep(L0_attn_x, Wq, Wk, Wv, Wo, router_w, down_w, up_w):
    f32 = np.float32
    x = np.ascontiguousarray(np.asarray(L0_attn_x, f32))
    Wq = np.asarray(Wq, f32)
    Wk = np.asarray(Wk, f32)
    Wv = np.asarray(Wv, f32)
    Wo = np.asarray(Wo, f32)
    router_w = np.asarray(router_w, f32)
    down_w = np.asarray(down_w, f32)
    up_w = np.asarray(up_w, f32)

    wq_t = np.ascontiguousarray(Wq.reshape(16, 128, 16, 128).transpose(2, 1, 0, 3).reshape(16, 128, C))
    wkv = np.ascontiguousarray(np.concatenate(
        [Wk.reshape(16, 128, 512), Wv.reshape(16, 128, 512)], axis=2))
    wo_t = np.ascontiguousarray(Wo.reshape(16, 128, 16, 128).transpose(2, 1, 0, 3).reshape(16, 128, C))
    dwt = np.ascontiguousarray(down_w.reshape(128, C).T.reshape(16, 128, 128))
    u128 = np.ascontiguousarray(up_w.transpose(0, 2, 1).reshape(128, C))
    rwt = np.ascontiguousarray(router_w.T.reshape(16, 128, 8))

    j = np.arange(64, dtype=np.float64)
    freqs = 1.0 / (10000.0 ** (2.0 * j / 128.0))
    tpos = np.arange(T, dtype=np.float64)
    emb = np.outer(tpos, freqs)
    cosT = np.cos(emb).T
    sinT = np.sin(emb).T
    cost = np.ascontiguousarray(np.concatenate([cosT, cosT], axis=0).astype(f32))
    sint = np.ascontiguousarray(np.concatenate([sinT, sinT], axis=0).astype(f32))

    R = np.zeros((128, 128), f32)
    for dd in range(64):
        R[dd, dd + 64] = -1.0
        R[dd + 64, dd] = 1.0
    rt = np.ascontiguousarray(R.T)

    rep = np.zeros((8, 128), f32)
    for e8 in range(8):
        rep[e8, e8 * 16:(e8 + 1) * 16] = 1.0
    ident = np.eye(128, dtype=f32)
    onesc = np.ones((128, 1), f32)
    onesr = np.ones((1, 128), f32)

    shared = dict(wq=wq_t, wkv=wkv, wo=wo_t, dwt=dwt, u128=u128, rwt=rwt,
                  cost=cost, sint=sint, rt=rt, rep=rep, ident=ident,
                  onesc=onesc, onesr=onesr)

    in_maps = []
    for core in range(8):
        seq, par = core // 2, core % 2
        own_qb = [2 * s + par for s in range(8)]
        xs = np.ascontiguousarray(x[seq])
        xqrows = np.ascontiguousarray(np.concatenate([xs[qb * 128:(qb + 1) * 128] for qb in own_qb], axis=0))
        own_cols = np.concatenate([np.arange(qb * 128, (qb + 1) * 128) for qb in own_qb])
        cosq = np.ascontiguousarray(cost[:, own_cols])
        sinq = np.ascontiguousarray(sint[:, own_cols])
        maskt = np.full((128, 8), 1.0 if par == 1 else 0.0, f32)
        m = dict(shared)
        m.update(x=xs, xq=xqrows, cosq=cosq, sinq=sinq, maskt=maskt)
        in_maps.append(m)
    return x, in_maps


def kernel(L0_attn_x, L0_attn_Wq, L0_attn_Wk, L0_attn_Wv, L0_attn_Wo,
           L0_ffn_router_w, L0_ffn_down_w, L0_ffn_up_w):
    x, in_maps = _host_prep(L0_attn_x, L0_attn_Wq, L0_attn_Wk, L0_attn_Wv, L0_attn_Wo,
                            L0_ffn_router_w, L0_ffn_down_w, L0_ffn_up_w)
    nc = build_nc()
    res = run_bass_kernel_spmd(nc, in_maps, core_ids=list(range(8)))
    B = x.shape[0]
    outp = np.empty((B, T, C), np.float32)
    for core in range(8):
        seq, par = core // 2, core % 2
        o = res.results[core]["out"]
        for s in range(8):
            qb = 2 * s + par
            outp[seq, qb * 128:(qb + 1) * 128, :] = o[s * 128:(s + 1) * 128, :]
    return outp


# version 08
# speedup vs baseline: 1.1941x; 1.0111x over previous
import contextlib
import math
import numpy as np
import concourse.bass as bass
import concourse.mybir as mybir
from concourse.bass_utils import run_bass_kernel_spmd

F32 = mybir.dt.float32
BF16 = mybir.dt.bfloat16
AF = mybir.ActivationFunctionType
ALU = mybir.AluOpType
AX = mybir.AxisListType

T = 2048
C = 2048
NKV = 4
NH = 16
SCALE = 1.0 / math.sqrt(128.0)
RN = 1.0 / float(C)
EPS = 1e-5

ENGS = ("sync", "gpsimd", "tensor", "vector", "scalar")
_ENG_SEM = {"tensor": "t", "vector": "v", "scalar": "s", "gpsimd": "g"}


class Prog:

    def __init__(self, nc, ctx):
        self.nc = nc
        self.ctx = ctx
        self.stream = {e: [] for e in ENGS}
        self.sems = {}
        self.cnt = {}

    def sem(self, name):
        if name not in self.sems:
            self.sems[name] = self.ctx.enter_context(self.nc.semaphore(name))
            self.cnt[name] = 0
        return self.sems[name]

    def wait(self, eng, name, val):
        if val is None or val <= 0:
            return
        s = self.sem(name)
        self.stream[eng].append(lambda e, s=s, v=val: e.wait_ge(s, v))

    def op(self, eng, fn, sem=None, n=1):
        if sem is None:
            self.stream[eng].append(fn)
            return None
        s = self.sem(sem)
        self.cnt[sem] += n
        v = self.cnt[sem]
        self.stream[eng].append(lambda e, fn=fn, s=s, n=n: fn(e).then_inc(s, n))
        return v

    def dma(self, eng, dst, src, sem=None):
        if sem is None:
            sem = "di" if eng == "sync" else "do"
        return self.op(eng, lambda e, d=dst, s=src: e.dma_start(out=d, in_=s), sem, 16)

    def val(self, name):
        return self.cnt.get(name, 0)

    def fence(self, engines=("tensor", "vector", "scalar", "sync", "gpsimd")):
        snap = {k: self.val(k) for k in ("t", "v", "s", "g")}
        for e in engines:
            for k, v in snap.items():
                if _ENG_SEM.get(e) == k:
                    continue
                self.wait(e, k, v)

    def mm3(self, bank, lh, ll, rh, rl, start, stop, sem=None):
        self.op("tensor", lambda e, o=bank, l=lh, r=rh, st=start: e.matmul(o, l, r, start=st, stop=False))
        self.op("tensor", lambda e, o=bank, l=lh, r=rl: e.matmul(o, l, r, start=False, stop=False))
        return self.op("tensor", lambda e, o=bank, l=ll, r=rh, sp=stop: e.matmul(o, l, r, start=False, stop=sp), sem)


def emit_ln(P, sX, sXN, ln, dma_sem, dma_val):
    (sNegSum, sSumSq, sNegM, sMsq, sVar, sStd, sRstd, sEps) = ln
    P.wait("vector", dma_sem, dma_val)
    rs = P.op("vector", lambda e, o=sNegSum, i=sX: e.reduce_sum(o[:, :], i[:], axis=AX.X, negate=True), "v")
    P.wait("scalar", dma_sem, dma_val)
    a = P.op("scalar", lambda e, o=sXN, i=sX, acc=sSumSq: e.activation(o[:], i[:], AF.Square, accum_out=acc[:, :]), "s")
    P.wait("vector", "s", a)
    P.wait("vector", "v", rs)
    b = P.op("vector", lambda e, o=sNegM, i=sNegSum: e.tensor_scalar_mul(o[:, :], i[:, :], RN), "v")
    P.wait("vector", "v", b)
    c = P.op("vector", lambda e, o=sMsq, i=sNegM: e.tensor_mul(o[:, :], i[:, :], i[:, :]), "v")
    P.wait("vector", "v", c)
    d = P.op("vector", lambda e, o=sVar, i=sSumSq, j=sMsq: e.scalar_tensor_tensor(
        o[:, :], i[:, :], RN, j[:, :], ALU.mult, ALU.subtract), "v")
    P.wait("scalar", "v", d)
    ee = P.op("scalar", lambda e, o=sStd, i=sVar, ep=sEps: e.activation(o[:, :], i[:, :], AF.Sqrt, bias=ep[:, 0:1]), "s")
    P.wait("vector", "s", ee)
    f = P.op("vector", lambda e, o=sRstd, i=sStd: e.reciprocal(o[:, :], i[:, :]), "v")
    P.wait("vector", "v", f)
    g = P.op("vector", lambda e, o=sXN, i=sX, m=sNegM, r=sRstd: e.tensor_scalar(
        o[:], i[:], m[:, 0:1], r[:, 0:1], ALU.add, ALU.mult), "v")
    return {"xn_v": g, "sq_s": a}


def build_nc():
    nc = bass.Bass("TRN2", target_bir_lowering=False, debug=False)

    x = nc.dram_tensor("x", [T, C], F32, kind="ExternalInput").ap()
    xq = nc.dram_tensor("xq", [1024, C], F32, kind="ExternalInput").ap()
    wq = nc.dram_tensor("wq", [NH, 128, C], F32, kind="ExternalInput").ap()
    wkv = nc.dram_tensor("wkv", [16, 128, 1024], F32, kind="ExternalInput").ap()
    wo = nc.dram_tensor("wo", [16, 128, C], F32, kind="ExternalInput").ap()
    dwt = nc.dram_tensor("dwt", [16, 128, 128], F32, kind="ExternalInput").ap()
    u128 = nc.dram_tensor("u128", [128, C], F32, kind="ExternalInput").ap()
    rwt = nc.dram_tensor("rwt", [16, 128, 8], F32, kind="ExternalInput").ap()
    cost = nc.dram_tensor("cost", [128, T], F32, kind="ExternalInput").ap()
    sint = nc.dram_tensor("sint", [128, T], F32, kind="ExternalInput").ap()
    cosq = nc.dram_tensor("cosq", [128, 1024], F32, kind="ExternalInput").ap()
    sinq = nc.dram_tensor("sinq", [128, 1024], F32, kind="ExternalInput").ap()
    rt = nc.dram_tensor("rt", [128, 128], F32, kind="ExternalInput").ap()
    rep = nc.dram_tensor("rep", [8, 128], F32, kind="ExternalInput").ap()
    ident = nc.dram_tensor("ident", [128, 128], F32, kind="ExternalInput").ap()
    onesc = nc.dram_tensor("onesc", [128, 1], F32, kind="ExternalInput").ap()
    onesr = nc.dram_tensor("onesr", [1, 128], F32, kind="ExternalInput").ap()
    maskt = nc.dram_tensor("maskt", [128, 8], F32, kind="ExternalInput").ap()
    out = nc.dram_tensor("out", [1024, C], F32, kind="ExternalOutput").ap()
    wqh = nc.dram_tensor("wqh", [NH, 128, C], BF16).ap()
    wql = nc.dram_tensor("wql", [NH, 128, C], BF16).ap()
    woh = nc.dram_tensor("woh", [16, 128, C], BF16).ap()
    wol = nc.dram_tensor("wol", [16, 128, C], BF16).ap()
    wkvh = nc.dram_tensor("wkvh", [16, 128, 1024], BF16).ap()
    wkvl = nc.dram_tensor("wkvl", [16, 128, 1024], BF16).ap()

    with contextlib.ExitStack() as ctx:
        P = Prog(nc, ctx)
        sbf = lambda name, shape, dt=F32: ctx.enter_context(nc.sbuf_tensor(name, shape, dt))

        sKh = sbf("sKh", [128, NKV * T], BF16)
        sKl = sbf("sKl", [128, NKV * T], BF16)
        sVh = sbf("sVh", [128, 16 * 512], BF16)
        sVl = sbf("sVl", [128, 16 * 512], BF16)
        sCosQ = sbf("sCosQ", [128, 1024])
        sSinQ = sbf("sSinQ", [128, 1024])
        sDWh = sbf("sDWh", [128, 16 * 128], BF16)
        sDWl = sbf("sDWl", [128, 16 * 128], BF16)
        sUh = sbf("sUh", [128, C], BF16)
        sUl = sbf("sUl", [128, C], BF16)
        sRW = sbf("sRW", [128, 16 * 8])
        sRT = sbf("sRT", [128, 128])
        sID = sbf("sID", [128, 128])
        sREP = sbf("sREP", [8, 128])
        sOC = sbf("sOC", [128, 1])
        sOR = sbf("sOR", [1, 128])
        sMask = sbf("sMask", [128, 8])
        sX = sbf("sX", [128, C])
        sX2 = sbf("sX2", [128, C])
        sXN = sbf("sXN", [128, C])
        sT1 = sbf("sT1", [128, 256])
        sT2 = sbf("sT2", [128, 256])
        sXf = sbf("sXf", [128, 4 * 128])
        sEps = sbf("sEps", [128, 1])
        ln = tuple(sbf(n, [128, 1]) for n in
                   ("sNegSum", "sSumSq", "sNegM", "sMsq", "sVar", "sStd", "sRstd")) + (sEps,)

        P.op("vector", lambda e, o=sEps: e.memset(o[:, :], EPS), "v")
        for d, s in ((sRT[:], rt), (sID[:], ident), (sREP[:], rep),
                     (sOC[:], onesc), (sOR[:], onesr), (sMask[:], maskt),
                     (sCosQ[:], cosq), (sSinQ[:], sinq)):
            P.dma("sync", d, s)
        for cch in range(16):
            P.dma("sync", sRW[:, cch * 8:(cch + 1) * 8], rwt[cch])

        def split(dst_h, dst_l, src):
            hv = P.op("vector", lambda e, o=dst_h, i=src: e.tensor_copy(o, i), "v")
            P.wait("gpsimd", "v", hv)
            gv = P.op("gpsimd", lambda e, o=dst_l, i=src, hh=dst_h: e.tensor_sub(o, i, hh), "g")
            return hv, gv

        def split_v(dst_h, dst_l, src):
            hv = P.op("vector", lambda e, o=dst_h, i=src: e.tensor_copy(o, i), "v")
            P.wait("vector", "v", hv)
            lv = P.op("vector", lambda e, o=dst_l, i=src, hh=dst_h: e.tensor_sub(o, i, hh), "v")
            return hv, lv

        for cch in range(16):
            P.dma("sync", sXN[:, cch * 128:(cch + 1) * 128], dwt[cch])
        dwd = P.val("di")
        P.wait("vector", "di", dwd)
        dw_h, dw_g = split(sDWh[:], sDWl[:], sXN[:])
        ud = P.dma("sync", sX[:], u128)
        P.wait("vector", "di", ud)
        u_h, u_g = split(sUh[:], sUl[:], sX[:])

        with contextlib.ExitStack() as p1:
            sb1 = lambda name, shape, dt=F32: p1.enter_context(nc.sbuf_tensor(name, shape, dt))
            sXnTh = sb1("sXnTh", [128, 16 * 512], BF16)
            sXnTl = sb1("sXnTl", [128, 16 * 512], BF16)
            sWkvh = sb1("sWkvh", [128, 2 * 1024], BF16)
            sWkvl = sb1("sWkvl", [128, 2 * 1024], BF16)
            sKst = sb1("sKst", [128, 4 * 512])
            sCos = sb1("sCos", [128, T])
            sSin = sb1("sSin", [128, T])
            sT3 = sb1("sT3", [128, 256])
            sWS = sb1("sWS", [128, 2 * 1024])
            sWH = sb1("sWH", [128, 2 * 1024], BF16)
            sWL = sb1("sWL", [128, 2 * 1024], BF16)
            pKb = p1.enter_context(nc.psum_tensor("pKb", [128, 2048], F32))
            pVb = p1.enter_context(nc.psum_tensor("pVb", [128, 2048], F32))

            P.dma("sync", sCos[:], cost)
            P.dma("sync", sSin[:], sint)
            cst_di = P.val("di")
            P.wait("tensor", "di", cst_di)
            P.wait("vector", "di", cst_di)
            P.wait("scalar", "v", max(dw_h, u_h))
            P.wait("scalar", "g", max(dw_g, u_g))
            P.wait("sync", "g", max(dw_g, u_g))

            wsplit_state = {"j": 0, "vg": {}, "dw": {}}
            wsplit_jobs = []
            for cch in range(16):
                wsplit_jobs.append((wkv[cch], 1024, wkvh[cch], wkvl[cch]))
            for hh in range(NH):
                for hf in range(2):
                    wsplit_jobs.append((wq[hh][:, hf * 1024:(hf + 1) * 1024], 1024,
                                        wqh[hh][:, hf * 1024:(hf + 1) * 1024],
                                        wql[hh][:, hf * 1024:(hf + 1) * 1024]))
            for cch in range(16):
                for hf in range(2):
                    wsplit_jobs.append((wo[cch][:, hf * 1024:(hf + 1) * 1024], 1024,
                                        woh[cch][:, hf * 1024:(hf + 1) * 1024],
                                        wol[cch][:, hf * 1024:(hf + 1) * 1024]))

            def emit_wsplit(njobs):
                st = wsplit_state
                for _ in range(njobs):
                    if not wsplit_jobs:
                        return
                    src, width, dsth, dstl = wsplit_jobs.pop(0)
                    j = st["j"]
                    st["j"] += 1
                    par = j % 2
                    P.wait("sync", "v", st["vg"].get(("v", j - 2), 0))
                    P.wait("sync", "g", st["vg"].get(("g", j - 2), 0))
                    dv = P.dma("sync", sWS[:, par * 1024:par * 1024 + width], src)
                    P.wait("vector", "di", dv)
                    P.wait("vector", "dw", st["dw"].get(("h", j - 2), 0))
                    hv = P.op("vector", lambda e, o=sWH[:, par * 1024:par * 1024 + width],
                              i=sWS[:, par * 1024:par * 1024 + width]: e.tensor_copy(o, i), "v")
                    P.wait("gpsimd", "v", hv)
                    P.wait("gpsimd", "dw", st["dw"].get(("l", j - 2), 0))
                    gv = P.op("gpsimd", lambda e, o=sWL[:, par * 1024:par * 1024 + width],
                              i=sWS[:, par * 1024:par * 1024 + width], hh2=sWH[:, par * 1024:par * 1024 + width]:
                              e.tensor_sub(o, i, hh2), "g")
                    st["vg"][("v", j)] = hv
                    st["vg"][("g", j)] = gv
                    P.wait("scalar", "v", hv)
                    st["dw"][("h", j)] = P.dma("scalar", dsth, sWH[:, par * 1024:par * 1024 + width], "dw")
                    P.wait("scalar", "g", gv)
                    st["dw"][("l", j)] = P.dma("scalar", dstl, sWL[:, par * 1024:par * 1024 + width], "dw")
                    st["dw"][j] = st["dw"][("l", j)]

            emit_wsplit(16)
            wkv_dw = {c: wsplit_state["dw"][c] for c in range(16)}

            def wq_dw(chunk):
                return wsplit_state["dw"].get(16 + chunk, None)

            def wo_dw(chunk):
                return wsplit_state["dw"].get(48 + chunk, None)

            def tr_slot(j):
                base = (j % 4) * 512 + 256
                return pKb[:, base:base + 128]

            tr_evac_s = [0] * 8
            prev_evac_s, prev_rope_v, prev_rope_g = 0, 0, 0
            x_free_v, x_free_s = 0, 0
            wkv_t = {}
            kc = 0
            split_done = []

            for g in range(4):
                for ti in range(4):
                    tb = g * 4 + ti
                    bn = sXN
                    P.wait("sync", "v", x_free_v)
                    P.wait("sync", "s", x_free_s)
                    xd = P.dma("sync", sX[:], x[tb * 128:(tb + 1) * 128, :])
                    lv = emit_ln(P, sX, sXN, ln, "di", xd)
                    x_free_v, x_free_s = lv["xn_v"], lv["sq_s"]
                    P.wait("tensor", "v", lv["xn_v"])
                    if g > 0 and ti == 0:
                        P.wait("tensor", "s", prev_evac_s)
                    ttr = 0
                    for cch in range(16):
                        j = (ti * 16 + cch) % 4
                        P.wait("tensor", "s", tr_evac_s[j])
                        slot = tr_slot(j)
                        ttr = P.op("tensor", lambda e, o=slot, i=bn[:, cch * 128:(cch + 1) * 128],
                                   idm=sID: e.transpose(o, i, idm[:, :]), "t")
                        P.wait("scalar", "t", ttr)
                        xf = sXf[:, (j % 4) * 128:(j % 4 + 1) * 128]
                        P.wait("scalar", "v", split_done[-4][0] if len(split_done) >= 4 else 0)
                        P.wait("scalar", "g", split_done[-4][1] if len(split_done) >= 4 else 0)
                        tr_evac_s[j] = P.op("scalar", lambda e, o=xf, i=slot: e.copy(o, i), "s")
                        P.wait("vector", "s", tr_evac_s[j])
                        dh = sXnTh[:, cch * 512 + ti * 128: cch * 512 + ti * 128 + 128]
                        dl = sXnTl[:, cch * 512 + ti * 128: cch * 512 + ti * 128 + 128]
                        split_done.append(split(dh, dl, xf))
                    P.wait("scalar", "t", ttr)
                    emit_wsplit(1)
                xn_split_v = split_done[-1][0]
                xn_split_g = split_done[-1][1]

                for cch in range(16):
                    P.wait("sync", "t", wkv_t.get(kc - 2, 0))
                    P.wait("sync", "dw", wkv_dw[cch])
                    P.dma("sync", sWkvh[:, (kc % 2) * 1024:(kc % 2 + 1) * 1024], wkvh[cch])
                    wv = P.dma("sync", sWkvl[:, (kc % 2) * 1024:(kc % 2 + 1) * 1024], wkvl[cch])
                    P.wait("tensor", "di", wv)
                    if cch == 0:
                        P.wait("tensor", "v", xn_split_v)
                        P.wait("tensor", "g", xn_split_g)
                        P.wait("tensor", "v", prev_rope_v)
                        P.wait("tensor", "g", prev_rope_g)
                    st0, sp0 = (cch == 0), (cch == 15)
                    base = (kc % 2) * 1024
                    lastmm = 0
                    for kvh in range(NKV):
                        lastmm = P.mm3(pKb[:, kvh * 512:(kvh + 1) * 512],
                                       sWkvh[:, base + kvh * 128:base + (kvh + 1) * 128],
                                       sWkvl[:, base + kvh * 128:base + (kvh + 1) * 128],
                                       sXnTh[:, cch * 512:(cch + 1) * 512],
                                       sXnTl[:, cch * 512:(cch + 1) * 512], st0, sp0, "t")
                    for ti in range(4):
                        lastmm = P.mm3(pVb[:, ti * 512:(ti + 1) * 512],
                                       sXnTh[:, cch * 512 + ti * 128:cch * 512 + (ti + 1) * 128],
                                       sXnTl[:, cch * 512 + ti * 128:cch * 512 + (ti + 1) * 128],
                                       sWkvh[:, base + 512:base + 1024],
                                       sWkvl[:, base + 512:base + 1024], st0, sp0, "t")
                    wkv_t[kc] = lastmm
                    kc += 1
                    emit_wsplit(1)
                kv_done = wkv_t[kc - 1]

                P.wait("scalar", "t", kv_done)
                P.wait("vector", "t", kv_done)
                ev_s, ev_v = 0, 0
                for ti in range(4):
                    dsth = sVh[:, (g * 4 + ti) * 512:(g * 4 + ti + 1) * 512]
                    dstl = sVl[:, (g * 4 + ti) * 512:(g * 4 + ti + 1) * 512]
                    ev_s = P.op("scalar", lambda e, o=dsth, i=pVb[:, ti * 512:(ti + 1) * 512]: e.copy(o, i), "s")
                    P.wait("vector", "s", ev_s)
                    ev_v = P.op("vector", lambda e, o=dstl, i=pVb[:, ti * 512:(ti + 1) * 512], hh=dsth:
                                e.tensor_sub(o, i, hh), "v")
                evK = [0] * NKV
                for kvh in range(NKV):
                    evK[kvh] = P.op("scalar", lambda e, o=sKst[:, kvh * 512:(kvh + 1) * 512],
                                    i=pKb[:, kvh * 512:(kvh + 1) * 512]: e.copy(o, i), "s")
                prev_evac_s = max(ev_s, max(evK))

                rv, rg = prev_rope_v, prev_rope_g
                for kvh in range(NKV):
                    P.wait("tensor", "s", evK[kvh])
                    if kvh == 0:
                        P.wait("tensor", "v", ev_v)
                    rot_t = P.op("tensor", lambda e, o=pVb[:, kvh * 512:(kvh + 1) * 512], l=sRT,
                                 r=sKst[:, kvh * 512:(kvh + 1) * 512]:
                                 e.matmul(o, l[:, :], r, start=True, stop=True), "t")
                    P.wait("vector", "t", rot_t)
                    for u in range(2):
                        csl = slice(g * 512 + u * 256, g * 512 + (u + 1) * 256)
                        ksl = slice(kvh * 512 + u * 256, kvh * 512 + (u + 1) * 256)
                        P.op("vector", lambda e, o=sT1, i=sKst[:, ksl], c2=sCos[:, csl]: e.tensor_mul(o[:], i, c2))
                        P.op("vector", lambda e, o=sT2, i=pVb[:, ksl], c2=sSin[:, csl]: e.tensor_mul(o[:], i, c2))
                        P.wait("vector", "g", rg)
                        t3v = P.op("vector", lambda e, o=sT3, a=sT1, b=sT2: e.tensor_add(o[:], a[:], b[:]), "v")
                        P.wait("vector", "v", t3v)
                        kdst = slice(kvh * T + g * 512 + u * 256, kvh * T + g * 512 + (u + 1) * 256)
                        rv, rg = split(sKh[:, kdst], sKl[:, kdst], sT3[:])
                prev_rope_v, prev_rope_g = rv, rg


            emit_wsplit(100)
            wsp_done = P.val("dw")

            ph = P.sem("ph")
            for eng in ENGS:
                P.op(eng, lambda e, s=ph: e.nop().then_inc(s, 1))
            for eng in ENGS:
                P.wait(eng, "ph", 5)

        with contextlib.ExitStack() as p2:
            sb2 = lambda name, shape, dt=F32: p2.enter_context(nc.sbuf_tensor(name, shape, dt))
            sXnTqh = sb2("sXnTqh", [128, 16 * 256], BF16)
            sXnTql = sb2("sXnTql", [128, 16 * 256], BF16)
            sYallh = sb2("sYallh", [128, 16 * 256], BF16)
            sYalll = sb2("sYalll", [128, 16 * 256], BF16)
            sPallh = sb2("sPallh", [128, 16 * 256], BF16)
            sPalll = sb2("sPalll", [128, 16 * 256], BF16)
            sWqh = sb2("sWqh", [128, 2 * 1024], BF16)
            sWql = sb2("sWql", [128, 2 * 1024], BF16)
            sWoh = sb2("sWoh", [128, 2 * 1024], BF16)
            sWol = sb2("sWol", [128, 2 * 1024], BF16)
            sQh = sb2("sQh", [128, 2 * 256])
            sQhh = sb2("sQhh", [128, 2 * 256], BF16)
            sQhl = sb2("sQhl", [128, 2 * 256], BF16)
            sPf = sb2("sPf", [128, 4 * 256])
            sYf = sb2("sYf", [128, 256])
            sAcc = sb2("sAcc", [128, 256])
            sB = sb2("sB", [128, 256])
            sOT = sb2("sOT", [128, 2 * 256])
            sDr = sb2("sDr", [1, 256])
            sW8 = sb2("sW8", [128, 16])
            sW2 = sb2("sW2", [128, 16])
            sGmask = sb2("sGmask", [128, 16])
            sGate = sb2("sGate", [128, 16])
            sG8 = sb2("sG8", [8, 256])
            sA = sb2("sA", [128, 256])
            sAGf = sb2("sAGf", [128, 256])
            sAGh = sb2("sAGh", [128, 256], BF16)
            sAGl = sb2("sAGl", [128, 256], BF16)
            sNM8 = sb2("sNM8", [128, 2])
            sDen8 = sb2("sDen8", [128, 2])
            sRec8 = sb2("sRec8", [128, 2])
            sM1 = sb2("sM1", [128, 2])
            sM2 = sb2("sM2", [128, 2])
            pb = [p2.enter_context(nc.psum_tensor("pb%d" % i, [128, 512], F32)) for i in range(8)]
            pQ, pRot, pS0, pS1, pY, pD, pB, pF = pb


            wq_t = {}
            wo_t = {}
            x_free_v, x_free_s = 0, 0

            for qt in range(4):
                KB = 4 * qt + 4
                P.fence()

                tr_evac = [0] * 4
                split_hist = []
                rt_mm = {}
                for t2 in range(2):
                    sidx = 2 * qt + t2
                    if x_free_v is not None:
                        P.wait("sync", "v", x_free_v)
                    P.wait("sync", "s", x_free_s)
                    P.wait("sync", "do2", P.val("do2"))
                    xd = P.dma("sync", sX[:], xq[sidx * 128:(sidx + 1) * 128, :])
                    lv = emit_ln(P, sX, sXN, ln, "di", xd)
                    x_free_v, x_free_s = lv["xn_v"], lv["sq_s"]
                    P.wait("tensor", "v", lv["xn_v"])
                    ttr = 0
                    for cch in range(16):
                        j = (t2 * 16 + cch) % 4
                        slot = pb[j][:, 256:384]
                        P.wait("tensor", "s", tr_evac[j])
                        ttr = P.op("tensor", lambda e, o=slot, i=sXN[:, cch * 128:(cch + 1) * 128],
                                   idm=sID: e.transpose(o, i, idm[:, :]), "t")
                        P.wait("scalar", "t", ttr)
                        xf = sXf[:, (cch % 4) * 128:(cch % 4 + 1) * 128]
                        P.wait("scalar", "v", split_hist[-4][0] if len(split_hist) >= 4 else 0)
                        P.wait("scalar", "g", split_hist[-4][1] if len(split_hist) >= 4 else 0)
                        P.wait("scalar", "t", rt_mm.get((t2, cch - 4), 0))
                        tr_evac[j] = P.op("scalar", lambda e, o=xf, i=slot: e.copy(o, i), "s")
                        P.wait("vector", "s", tr_evac[j])
                        dh = sXnTqh[:, cch * 256 + t2 * 128: cch * 256 + t2 * 128 + 128]
                        dl = sXnTql[:, cch * 256 + t2 * 128: cch * 256 + t2 * 128 + 128]
                        split_hist.append(split(dh, dl, xf))
                        pRt = pD[:, 256:264] if t2 == 0 else pD[:, 384:392]
                        P.wait("tensor", "s", tr_evac[j])
                        rt_mm[(t2, cch)] = P.op("tensor", lambda e, o=pRt, l=xf, r=sRW[:, cch * 8:(cch + 1) * 8],
                                                st=(cch == 0), sp=(cch == 15): e.matmul(o, l, r, start=st, stop=sp), "t")
                    P.wait("scalar", "t", ttr)
                xnq_v = split_hist[-1][0]
                xnq_g = split_hist[-1][1]

                P.wait("tensor", "v", xnq_v)
                P.wait("tensor", "g", xnq_g)
                hmm = 0
                for cch in range(16):
                    hmm = P.mm3(pF[:, 0:256], sDWh[:, cch * 128:(cch + 1) * 128], sDWl[:, cch * 128:(cch + 1) * 128],
                                sXnTqh[:, cch * 256:(cch + 1) * 256], sXnTql[:, cch * 256:(cch + 1) * 256],
                                (cch == 0), (cch == 15), "t")
                P.wait("scalar", "t", hmm)
                silu_s = P.op("scalar", lambda e, o=sA, i=pF[:, 0:256]: e.activation(o[:], i, AF.Silu), "s")

                S = {}
                GA = {}
                def emit_gating():
                    ag_v = 0
                    for t2 in range(2):
                        pRt = pD[:, 256:264] if t2 == 0 else pD[:, 384:392]
                        rmm = max(rt_mm[(0, 15)], rt_mm[(1, 15)])
                        P.wait("vector", "t", rmm)
                        nm = P.op("vector", lambda e, o=sNM8[:, t2:t2 + 1], i=pRt:
                                  e.reduce_max(o, i, axis=AX.X, negate=True), "v")
                        P.wait("scalar", "v", nm)
                        ex = P.op("scalar", lambda e, o=sW8[:, t2 * 8:(t2 + 1) * 8], i=pRt, b=sNM8[:, t2:t2 + 1],
                                  acc=sDen8[:, t2:t2 + 1]: e.activation(o, i, AF.Exp, bias=b, accum_out=acc), "s")
                        P.wait("vector", "s", ex)
                        rc = P.op("vector", lambda e, o=sRec8[:, t2:t2 + 1], i=sDen8[:, t2:t2 + 1]:
                                  e.reciprocal(o, i), "v")
                        P.wait("vector", "v", rc)
                        sm = P.op("vector", lambda e, o=sW8[:, t2 * 8:(t2 + 1) * 8], r=sRec8[:, t2:t2 + 1]:
                                  e.tensor_scalar_mul(o, o, r), "v")
                        P.wait("vector", "v", sm)
                        m1 = P.op("vector", lambda e, o=sM1[:, t2:t2 + 1], i=sW8[:, t2 * 8:(t2 + 1) * 8]:
                                  e.reduce_max(o, i, axis=AX.X), "v")
                        P.wait("vector", "v", m1)
                        eq = P.op("vector", lambda e, o=sGmask[:, t2 * 8:(t2 + 1) * 8], i=sW8[:, t2 * 8:(t2 + 1) * 8],
                                  m=sM1[:, t2:t2 + 1]: e.tensor_single_scalar(o, i, m, ALU.is_ge), "v")
                        P.wait("vector", "v", eq)
                        w2 = P.op("vector", lambda e, o=sW2[:, t2 * 8:(t2 + 1) * 8], i=sGmask[:, t2 * 8:(t2 + 1) * 8],
                                  w=sW8[:, t2 * 8:(t2 + 1) * 8]: e.scalar_tensor_tensor(o, i, -1e30, w, ALU.mult, ALU.add), "v")
                        P.wait("vector", "v", w2)
                        m2 = P.op("vector", lambda e, o=sM2[:, t2:t2 + 1], i=sW2[:, t2 * 8:(t2 + 1) * 8]:
                                  e.reduce_max(o, i, axis=AX.X), "v")
                        P.wait("vector", "v", m2)
                        gm = P.op("vector", lambda e, o=sGmask[:, t2 * 8:(t2 + 1) * 8], i=sW8[:, t2 * 8:(t2 + 1) * 8],
                                  m=sM2[:, t2:t2 + 1]: e.tensor_single_scalar(o, i, m, ALU.is_ge), "v")
                        P.wait("vector", "v", gm)
                        gv = P.op("vector", lambda e, o=sGate[:, t2 * 8:(t2 + 1) * 8], i=sW8[:, t2 * 8:(t2 + 1) * 8],
                                  m=sGmask[:, t2 * 8:(t2 + 1) * 8]: e.tensor_mul(o, i, m), "v")
                        P.wait("tensor", "v", gv)
                        gt_t = P.op("tensor", lambda e, o=pB[0:8, 384:512], i=sGate[:, t2 * 8:(t2 + 1) * 8], idm=sID:
                                    e.transpose(o, i, idm[:, :]), "t")
                        P.wait("scalar", "t", gt_t)
                        g8 = P.op("scalar", lambda e, o=sG8[0:8, t2 * 128:(t2 + 1) * 128], i=pB[0:8, 384:512]:
                                  e.copy(o, i), "s")
                        P.wait("tensor", "s", g8)
                        if t2 == 1:
                            P.wait("tensor", "v", ag_v)
                        grep_t = P.op("tensor", lambda e, o=pB[:, 256:384], l=sREP, r=sG8[0:8, t2 * 128:(t2 + 1) * 128]:
                                      e.matmul(o, l[:, :], r, start=True, stop=True), "t")
                        P.wait("vector", "t", grep_t)
                        P.wait("vector", "s", silu_s)
                        ag_v = P.op("vector", lambda e, o=sAGf[:, t2 * 128:(t2 + 1) * 128], i=sA[:, t2 * 128:(t2 + 1) * 128],
                                    g2=pB[:, 256:384]: e.tensor_mul(o, i, g2), "v")
                    P.wait("vector", "v", ag_v)
                    agh, agg = split_v(sAGh[:], sAGl[:], sAGf[:])
                    GA["agg"] = agg


                def stage_q(h, qt=qt):
                    par = h % 2
                    st = {"par": par}
                    wdi = {}
                    for half in range(2):
                        jj = 2 * h + half
                        P.wait("sync", "t", wq_t.get(jj - 2, 0))
                        P.wait("sync", "dw", wq_dw(jj))
                        P.dma("sync", sWqh[:, (jj % 2) * 1024:(jj % 2 + 1) * 1024],
                              wqh[h][:, half * 1024:(half + 1) * 1024])
                        wdi[half] = P.dma("sync", sWql[:, (jj % 2) * 1024:(jj % 2 + 1) * 1024],
                                          wql[h][:, half * 1024:(half + 1) * 1024])
                    if h == 0:
                        P.wait("tensor", "v", xnq_v)
                        P.wait("tensor", "g", xnq_g)
                    if h >= 1:
                        P.wait("tensor", "s", S[h - 1]["qev"])
                    qmm = 0
                    for cch in range(16):
                        jj = 2 * h + cch // 8
                        if cch == 0:
                            P.wait("tensor", "di", wdi[0])
                        if cch == 8:
                            P.wait("tensor", "di", wdi[1])
                        base2 = (jj % 2) * 1024 + (cch % 8) * 128
                        qmm = P.mm3(pQ[:, 0:256], sWqh[:, base2: base2 + 128],
                                    sWql[:, base2: base2 + 128],
                                    sXnTqh[:, cch * 256:(cch + 1) * 256], sXnTql[:, cch * 256:(cch + 1) * 256],
                                    (cch == 0), (cch == 15), "t" if cch == 15 else None)
                    wq_t[2 * h] = qmm
                    wq_t[2 * h + 1] = qmm
                    st["qmm"] = qmm
                    P.wait("scalar", "t", qmm)
                    st["qev"] = P.op("scalar", lambda e, o=sQh[:, par * 256:(par + 1) * 256], i=pQ[:, 0:256]:
                                     e.copy(o, i), "s")
                    S[h] = st

                def stage_rot(h, qt=qt):
                    st = S[h]
                    par = st["par"]
                    P.wait("tensor", "s", st["qev"])
                    rot_t = P.op("tensor", lambda e, o=pRot[:, 0:256], l=sRT, r=sQh[:, par * 256:(par + 1) * 256]:
                                 e.matmul(o, l[:, :], r, start=True, stop=True), "t")
                    P.wait("vector", "t", rot_t)
                    for t2 in range(2):
                        qsl = slice((2 * qt + t2) * 128, (2 * qt + t2 + 1) * 128)
                        dsl = slice(t2 * 128, (t2 + 1) * 128)
                        P.op("vector", lambda e, o=sT1[:, dsl],
                             i=sQh[:, par * 256 + t2 * 128: par * 256 + (t2 + 1) * 128],
                             c2=sCosQ[:, qsl]: e.tensor_mul(o, i, c2))
                        P.op("vector", lambda e, o=sT2[:, dsl], i=pRot[:, dsl], c2=sSinQ[:, qsl]:
                             e.tensor_mul(o, i, c2))
                    rope_v = P.op("vector", lambda e, o=sQh[:, par * 256:(par + 1) * 256], a=sT1, b=sT2:
                                  e.tensor_add(o, a[:], b[:]), "v")
                    P.wait("vector", "v", rope_v)
                    st["rope_v"] = rope_v
                    qs_v, qs_lo = split_v(sQhh[:, par * 256:(par + 1) * 256], sQhl[:, par * 256:(par + 1) * 256],
                                          sQh[:, par * 256:(par + 1) * 256])
                    st["qs_v"] = qs_lo

                def stage_s(h, qt=qt, KB=KB):
                    st = S[h]
                    par = st["par"]
                    kvh = h // 4
                    full_end = 4 * qt + 2
                    exp_s = {}
                    hi_s = {}
                    lo_v = {}
                    acc_v = 0
                    mask_v = 0
                    av_t = 0
                    pairs = []
                    kb = 0
                    while kb < KB:
                        paired = (kb % 2 == 0) and (kb + 1 < full_end)
                        pairs.append((kb, 2 if paired else 1))
                        kb += 2 if paired else 1

                    pYb = pY if h % 2 == 0 else pF

                    def emit_av(pi):
                        nonlocal av_t
                        kb0, n2 = pairs[pi]
                        p0 = kb0 // 2
                        P.wait("tensor", "s", hi_s[p0])
                        P.wait("tensor", "v", lo_v[p0])
                        if pi == 0:
                            P.wait("tensor", "v", S.get(h - 2, {}).get("ym", 0))
                            if h == 1:
                                P.wait("tensor", "s", silu_s)
                        for kbx in range(kb0, kb0 + n2):
                            cols = slice(0, 256) if kbx < full_end else slice(128, 256)
                            vsl = slice(kbx * 512 + kvh * 128, kbx * 512 + (kvh + 1) * 128)
                            av_t = P.mm3(pYb[:, cols], sVh[:, vsl], sVl[:, vsl],
                                         sPallh[:, kbx * 256 + cols.start: kbx * 256 + cols.stop],
                                         sPalll[:, kbx * 256 + cols.start: kbx * 256 + cols.stop],
                                         (kbx == 0), (kbx == KB - 1),
                                         "t" if kbx == KB - 1 else None)

                    for pi, (kb, n2) in enumerate(pairs):
                        p = kb // 2
                        bank = pS0 if (p % 2) == 0 else pS1
                        if kb == 0:
                            P.wait("tensor", "v", st["qs_v"])
                            P.wait("scalar", "t", S.get(h - 1, {}).get("av_t", 0))
                        if p >= 2:
                            P.wait("tensor", "s", exp_s.get(p - 2, 0))
                        smm = 0
                        for k2 in range(n2):
                            kbx = kb + k2
                            cols = slice(0, 256) if kbx < full_end else slice(128, 256)
                            boff = (kbx % 2) * 256
                            ksl = slice(kvh * T + kbx * 128, kvh * T + (kbx + 1) * 128)
                            qsl2 = slice(par * 256 + cols.start, par * 256 + cols.stop)
                            smm = P.mm3(bank[:, boff + cols.start: boff + cols.stop], sKh[:, ksl], sKl[:, ksl],
                                        sQhh[:, qsl2], sQhl[:, qsl2], True, True, "t")
                        if pi >= 1:
                            emit_av(pi - 1)
                        P.wait("scalar", "t", smm)
                        P.wait("scalar", "v", lo_v.get(p - 2, 0))
                        sl = (p % 2) * 512
                        if n2 == 2:
                            psrc = bank[:, 0:512]
                            pfdst = sPf[:, sl:sl + 512]
                        else:
                            cols = slice(0, 256) if kb < full_end else slice(128, 256)
                            boff = (kb % 2) * 256
                            psrc = bank[:, boff + cols.start: boff + cols.stop]
                            pfdst = sPf[:, sl + boff + cols.start: sl + boff + cols.stop]
                        exp_s[p] = P.op("scalar", lambda e, o=pfdst, i=psrc:
                                        e.activation(o, i, AF.Exp, scale=SCALE), "s")
                        P.wait("vector", "s", exp_s[p])
                        for kbx in range(kb, kb + n2):
                            if kbx == 4 * qt + 1:
                                mask_v = P.op("vector", lambda e, o=sPf[:, sl + (kbx % 2) * 256: sl + (kbx % 2) * 256 + 128],
                                              m=sMask[:, 2 * qt:2 * qt + 1]: e.tensor_scalar_mul(o, o, m), "v")
                            if kbx == 4 * qt + 3:
                                mask_v = P.op("vector", lambda e, o=sPf[:, sl + (kbx % 2) * 256 + 128: sl + (kbx % 2) * 256 + 256],
                                              m=sMask[:, 2 * qt + 1:2 * qt + 2]: e.tensor_scalar_mul(o, o, m), "v")
                        for kbx in range(kb, kb + n2):
                            cols = slice(0, 256) if kbx < full_end else slice(128, 256)
                            if kbx == 0:
                                P.wait("vector", "t", S.get(h - 1, {}).get("d_t", 0))
                                acc_v = P.op("vector", lambda e, o=sAcc, i=sPf[:, sl:sl + 256]: e.tensor_copy(o[:], i), "v")
                            else:
                                acc_v = P.op("vector", lambda e, o=sAcc[:, cols],
                                             i=sPf[:, sl + (kbx % 2) * 256 + cols.start: sl + (kbx % 2) * 256 + cols.stop]:
                                             e.tensor_add(o, o, i), "v")
                        if mask_v:
                            P.wait("scalar", "v", mask_v)
                        P.wait("scalar", "s", exp_s[p])
                        if n2 == 2:
                            hdst = sPallh[:, kb * 256:(kb + 2) * 256]
                            ldst = sPalll[:, kb * 256:(kb + 2) * 256]
                            pfs = sPf[:, sl:sl + 512]
                        else:
                            cols = slice(0, 256) if kb < full_end else slice(128, 256)
                            boff = (kb % 2) * 256
                            hdst = sPallh[:, kb * 256 + cols.start: kb * 256 + cols.stop]
                            ldst = sPalll[:, kb * 256 + cols.start: kb * 256 + cols.stop]
                            pfs = sPf[:, sl + boff + cols.start: sl + boff + cols.stop]
                        hi_s[p] = P.op("scalar", lambda e, o=hdst, i=pfs: e.copy(o, i), "s")
                        P.wait("vector", "s", hi_s[p])
                        lo_v[p] = P.op("vector", lambda e, o=ldst, i=pfs, hh=hdst: e.tensor_sub(o, i, hh), "v")
                    emit_av(len(pairs) - 1)
                    st["av_t"] = av_t
                    st["acc_v"] = acc_v
                    st["p_lo_last"] = lo_v[(KB - 1) // 2]

                def stage_d(h, qt=qt, KB=KB):
                    st = S[h]
                    P.wait("tensor", "v", st["p_lo_last"])
                    P.wait("tensor", "v", S.get(h - 1, {}).get("rec_v", 0))
                    d_t = P.op("tensor", lambda e, o=pD[0:1, 0:256], l=sOC, r=sAcc:
                               e.matmul(o, l[:, :], r[:], start=True, stop=True), "t")
                    st["d_t"] = d_t
                    P.wait("vector", "t", d_t)
                    st["rec_v"] = P.op("vector", lambda e, o=sDr, i=pD[0:1, 0:256]:
                                       e.reciprocal(o[0:1, :], i), "v")

                def stage_fin(h, qt=qt, KB=KB):
                    st = S[h]
                    pYb = pY if h % 2 == 0 else pF
                    P.wait("tensor", "s", S.get(h - 1, {}).get("bev", 0))
                    P.wait("tensor", "v", st["rec_v"])
                    b_t = P.op("tensor", lambda e, o=pB[:, 0:256], l=sOR, r=sDr:
                               e.matmul(o, l[:, :], r[0:1, :], start=True, stop=True), "t")
                    P.wait("scalar", "t", b_t)
                    bev = P.op("scalar", lambda e, o=sB, i=pB[:, 0:256]: e.copy(o[:], i), "s")
                    st["bev"] = bev
                    P.wait("vector", "s", bev)
                    P.wait("vector", "t", st["av_t"])
                    P.wait("vector", "v", S.get(h - 1, {}).get("yh_v", 0))
                    ym = P.op("vector", lambda e, o=sYf, i=pYb[:, 0:256], b=sB: e.tensor_mul(o[:], i, b[:]), "v")
                    st["ym"] = ym
                    P.wait("vector", "v", ym)
                    yh_v, yh_lo = split_v(sYallh[:, h * 256:(h + 1) * 256], sYalll[:, h * 256:(h + 1) * 256], sYf[:])
                    st["yh_v"] = yh_lo

                emit_gating()
                for h in range(NH + 1):
                    if h < NH:
                        stage_q(h)
                    if h >= 1:
                        stage_s(h - 1)
                    if h >= 2:
                        stage_fin(h - 2)
                    if h < NH:
                        stage_rot(h)
                    if h >= 1:
                        stage_d(h - 1)
                stage_fin(NH - 1)
                agg = GA["agg"]

                P.wait("sync", "v", x_free_v)
                P.wait("sync", "s", x_free_s)
                P.wait("sync", "do2", P.val("do2"))
                xda = P.dma("sync", sX[:], xq[(2 * qt) * 128:(2 * qt + 1) * 128, :])
                xdb = P.dma("sync", sX2[:], xq[(2 * qt + 1) * 128:(2 * qt + 2) * 128, :])
                P.fence(engines=("tensor",))
                oadd_v = {0: 0, 1: 0}
                oev = {0: 0, 1: 0}
                tr_free = {0: 0, 1: 0}
                tr_free_t = {0: 0, 1: 0}
                for cch in range(16):
                    par = cch % 2
                    wdi = {}
                    for half in range(2):
                        jj = 2 * cch + half
                        P.wait("sync", "t", wo_t.get(jj - 2, 0))
                        P.wait("sync", "dw", wo_dw(jj))
                        P.dma("sync", sWoh[:, (jj % 2) * 1024:(jj % 2 + 1) * 1024],
                              woh[cch][:, half * 1024:(half + 1) * 1024])
                        wdi[half] = P.dma("sync", sWol[:, (jj % 2) * 1024:(jj % 2 + 1) * 1024],
                                          wol[cch][:, half * 1024:(half + 1) * 1024])
                    pO = pQ if par == 0 else pF
                    P.wait("tensor", "s", oev[par])
                    for h in range(NH):
                        jj = 2 * cch + h // 8
                        if h == 0:
                            P.wait("tensor", "di", wdi[0])
                        if h == 8:
                            P.wait("tensor", "di", wdi[1])
                        base2 = (jj % 2) * 1024 + (h % 8) * 128
                        P.mm3(pO[:, 0:256], sWoh[:, base2: base2 + 128],
                              sWol[:, base2: base2 + 128],
                              sYallh[:, h * 256:(h + 1) * 256], sYalll[:, h * 256:(h + 1) * 256],
                              (h == 0), False)
                    if cch == 0:
                        P.wait("tensor", "v", agg)
                    omm = P.mm3(pO[:, 0:256], sUh[:, cch * 128:(cch + 1) * 128], sUl[:, cch * 128:(cch + 1) * 128],
                                sAGh[:], sAGl[:], False, True, "t")
                    wo_t[2 * cch] = omm
                    wo_t[2 * cch + 1] = omm
                    P.wait("scalar", "t", omm)
                    P.wait("scalar", "t", max(tr_free_t.values()))
                    oev[par] = P.op("scalar", lambda e, o=sOT[:, par * 256:(par + 1) * 256], i=pO[:, 0:256]:
                                    e.copy(o, i), "s")
                    for t2 in range(2):
                        ptr = (pRot if t2 == 0 else pS0)[:, 0:128]
                        P.wait("tensor", "s", oev[par])
                        P.wait("tensor", "v", tr_free[t2])
                        tr_t = P.op("tensor", lambda e, o=ptr,
                                    i=sOT[:, par * 256 + t2 * 128: par * 256 + (t2 + 1) * 128], idm=sID:
                                    e.transpose(o, i, idm[:, :]), "t")
                        tr_free_t[t2] = tr_t
                        P.wait("vector", "t", tr_t)
                        if cch == 0 and t2 == 0:
                            P.wait("vector", "di", xdb)
                        sxt = sX if t2 == 0 else sX2
                        tr_free[t2] = P.op("vector", lambda e, o=sxt[:, cch * 128:(cch + 1) * 128], i=ptr,
                                           x2=sxt[:, cch * 128:(cch + 1) * 128]: e.tensor_add(o, i, x2), "v")
                        oadd_v[t2] = tr_free[t2]
                P.wait("sync", "v", oadd_v[0])
                P.dma("sync", out[(2 * qt) * 128:(2 * qt + 1) * 128, :], sX[:], "do2")
                P.wait("sync", "v", oadd_v[1])
                P.dma("sync", out[(2 * qt + 1) * 128:(2 * qt + 2) * 128, :], sX2[:], "do2")
                x_free_v = None
                x_free_s = 0

            P.wait("gpsimd", "do2", P.val("do2"))

        blk = ctx.enter_context(nc.Block())

        @blk.sync
        def _(e):
            for f in P.stream["sync"]:
                f(e)

        @blk.gpsimd
        def _(e):
            for f in P.stream["gpsimd"]:
                f(e)

        @blk.tensor
        def _(e):
            for f in P.stream["tensor"]:
                f(e)

        @blk.vector
        def _(e):
            for f in P.stream["vector"]:
                f(e)

        @blk.scalar
        def _(e):
            for f in P.stream["scalar"]:
                f(e)

    return nc


def _host_prep(L0_attn_x, Wq, Wk, Wv, Wo, router_w, down_w, up_w):
    f32 = np.float32
    x = np.ascontiguousarray(np.asarray(L0_attn_x, f32))
    Wq = np.asarray(Wq, f32)
    Wk = np.asarray(Wk, f32)
    Wv = np.asarray(Wv, f32)
    Wo = np.asarray(Wo, f32)
    router_w = np.asarray(router_w, f32)
    down_w = np.asarray(down_w, f32)
    up_w = np.asarray(up_w, f32)

    wq_t = np.ascontiguousarray(Wq.reshape(16, 128, 16, 128).transpose(2, 1, 0, 3).reshape(16, 128, C))
    wkv = np.ascontiguousarray(np.concatenate(
        [Wk.reshape(16, 128, 512), Wv.reshape(16, 128, 512)], axis=2))
    wo_t = np.ascontiguousarray(Wo.reshape(16, 128, 16, 128).transpose(2, 1, 0, 3).reshape(16, 128, C))
    dwt = np.ascontiguousarray(down_w.reshape(128, C).T.reshape(16, 128, 128))
    u128 = np.ascontiguousarray(up_w.transpose(0, 2, 1).reshape(128, C))
    rwt = np.ascontiguousarray(router_w.T.reshape(16, 128, 8))

    j = np.arange(64, dtype=np.float64)
    freqs = 1.0 / (10000.0 ** (2.0 * j / 128.0))
    tpos = np.arange(T, dtype=np.float64)
    emb = np.outer(tpos, freqs)
    cosT = np.cos(emb).T
    sinT = np.sin(emb).T
    cost = np.ascontiguousarray(np.concatenate([cosT, cosT], axis=0).astype(f32))
    sint = np.ascontiguousarray(np.concatenate([sinT, sinT], axis=0).astype(f32))

    R = np.zeros((128, 128), f32)
    for dd in range(64):
        R[dd, dd + 64] = -1.0
        R[dd + 64, dd] = 1.0
    rt = np.ascontiguousarray(R.T)

    rep = np.zeros((8, 128), f32)
    for e8 in range(8):
        rep[e8, e8 * 16:(e8 + 1) * 16] = 1.0
    ident = np.eye(128, dtype=f32)
    onesc = np.ones((128, 1), f32)
    onesr = np.ones((1, 128), f32)

    shared = dict(wq=wq_t, wkv=wkv, wo=wo_t, dwt=dwt, u128=u128, rwt=rwt,
                  cost=cost, sint=sint, rt=rt, rep=rep, ident=ident,
                  onesc=onesc, onesr=onesr)

    in_maps = []
    for core in range(8):
        seq, par = core // 2, core % 2
        own_qb = [2 * s + par for s in range(8)]
        xs = np.ascontiguousarray(x[seq])
        xqrows = np.ascontiguousarray(np.concatenate([xs[qb * 128:(qb + 1) * 128] for qb in own_qb], axis=0))
        own_cols = np.concatenate([np.arange(qb * 128, (qb + 1) * 128) for qb in own_qb])
        cosq = np.ascontiguousarray(cost[:, own_cols])
        sinq = np.ascontiguousarray(sint[:, own_cols])
        maskt = np.full((128, 8), 1.0 if par == 1 else 0.0, f32)
        m = dict(shared)
        m.update(x=xs, xq=xqrows, cosq=cosq, sinq=sinq, maskt=maskt)
        in_maps.append(m)
    return x, in_maps


def kernel(L0_attn_x, L0_attn_Wq, L0_attn_Wk, L0_attn_Wv, L0_attn_Wo,
           L0_ffn_router_w, L0_ffn_down_w, L0_ffn_up_w):
    x, in_maps = _host_prep(L0_attn_x, L0_attn_Wq, L0_attn_Wk, L0_attn_Wv, L0_attn_Wo,
                            L0_ffn_router_w, L0_ffn_down_w, L0_ffn_up_w)
    nc = build_nc()
    res = run_bass_kernel_spmd(nc, in_maps, core_ids=list(range(8)))
    B = x.shape[0]
    outp = np.empty((B, T, C), np.float32)
    for core in range(8):
        seq, par = core // 2, core % 2
        o = res.results[core]["out"]
        for s in range(8):
            qb = 2 * s + par
            outp[seq, qb * 128:(qb + 1) * 128, :] = o[s * 128:(s + 1) * 128, :]
    return outp
